# version 01
# baseline (speedup 1.0000x reference)
import numpy as np
import concourse.bass as bass
import concourse.mybir as mybir
from concourse.bass_utils import run_bass_kernel_spmd

F32 = mybir.dt.float32
F32R = mybir.dt.float32r
BF16 = mybir.dt.bfloat16
ALU = mybir.AluOpType
AF = mybir.ActivationFunctionType

B, T, D = 4, 4096, 1024
NCORES = 8
TSH = T // 2
EPS_LN = 1e-5
EPS_RMS = 1e-6
ETA_MIN = 1e-6


class Seq:

    def __init__(self):
        self.items = []
        self.total = 0

    def emit(self, eng_name, fn, inc=1):
        self.items.append((eng_name, fn, inc, self.total))
        self.total += inc

    def dma(self, eng_name, out, in_, **kw):
        self.emit(eng_name, lambda e: e.dma_start(out=out, in_=in_, **kw), inc=16)

    def build(self, nc, sem):
        by_eng = {}
        for eng_name, fn, inc, wait in self.items:
            by_eng.setdefault(eng_name, []).append((fn, inc, wait))
        total = self.total
        with nc.Block() as block:
            done_final = [False]
            for eng_name, items in by_eng.items():
                def mk(items, is_sync):
                    def emit(eng):
                        for fn, inc, wait in items:
                            if wait > 0:
                                eng.wait_ge(sem, wait)
                            ins = fn(eng)
                            ins.then_inc(sem, inc)
                        if is_sync:
                            eng.wait_ge(sem, total)
                    return emit
                is_sync = eng_name == "sync"
                done_final[0] = done_final[0] or is_sync
                getattr(block, eng_name)(mk(items, is_sync))
            if not done_final[0]:
                def fin(eng):
                    eng.wait_ge(sem, total)
                block.sync(fin)


def ceil_div(a, b):
    return (a + b - 1) // b


class Consts:

    def __init__(self, nc, sq, tile, name="consts"):
        self.tile = tile
        self.sq = sq
        self.vals = {}

    def cst(self, v, rows=128):
        v = float(v)
        if v not in self.vals:
            j = len(self.vals)
            self.vals[v] = j
            self.sq.emit("vector", lambda ve, j=j, v=v: ve.memset(self.tile[:, j:j + 1], v))
        j = self.vals[v]
        return self.tile[0:rows, j:j + 1]


def emit_ln_token_tile(sq, nc, x, xn, scr, cc, n_d=D, eps=EPS_LN, P=128):
    dump, col = scr["dump"], scr["col4"]
    sq.emit("vector", lambda v: v.tensor_reduce(col[0:P, 0:1], x, mybir.AxisListType.X, ALU.add))
    sq.emit("vector", lambda v: v.scalar_tensor_tensor(dump[0:P, :n_d], x, 1.0, x, ALU.mult, ALU.mult, accum_out=col[0:P, 1:2]))
    sq.emit("vector", lambda v: v.tensor_scalar(col[0:P, 2:4], col[0:P, 0:2], 1.0 / n_d, None, ALU.mult))
    sq.emit("vector", lambda v: v.scalar_tensor_tensor(col[0:P, 1:2], col[0:P, 2:3], col[0:P, 2:3], col[0:P, 3:4], ALU.mult, ALU.subtract))
    emit_rsqrt(sq, col[0:P, 1:2], col[0:P, 1:2], scr, cc, eps=eps, P=P, neg_in=True,
               t0=col[0:P, 0:1])
    sq.emit("vector", lambda v: v.tensor_scalar(xn, x, col[0:P, 2:3], col[0:P, 1:2], ALU.subtract, ALU.mult))


def emit_rsqrt(sq, out, var_in, scr, cc, eps=0.0, P=128, neg_in=False, t0=None):
    nr = scr["nr"]
    if t0 is None:
        t0 = nr[0:P, 2:3]
    sgn = -1.0 if neg_in else 1.0
    sq.emit("vector", lambda v: v.tensor_scalar(nr[0:P, 0:1], var_in, sgn, float(eps), ALU.mult, ALU.add))
    sq.emit("scalar", lambda s: s.activation(t0, nr[0:P, 0:1], AF.Ln))
    sq.emit("scalar", lambda s: s.activation(t0, t0, AF.Exp, scale=-0.5))
    sq.emit("vector", lambda v: v.scalar_tensor_tensor(nr[0:P, 1:2], t0, t0, nr[0:P, 0:1], ALU.mult, ALU.mult))
    sq.emit("vector", lambda v: v.tensor_scalar(nr[0:P, 1:2], nr[0:P, 1:2], -0.5, 1.5, ALU.mult, ALU.add))
    sq.emit("vector", lambda v: v.scalar_tensor_tensor(out, nr[0:P, 1:2], 1.0, t0, ALU.mult, ALU.mult))


def emit_transpose_128(sq, nc, src, dst, psum_tr, ident, dtype_note=None):
    sq.emit("tensor", lambda t: t.transpose(psum_tr[:, :], src, ident))
    sq.emit("vector", lambda v: v.tensor_copy(dst, psum_tr[:, :]))


def build_stage_a():
    nc = bass.Bass("TRN2", target_bir_lowering=False, debug=False)
    x = nc.dram_tensor("x", [TSH, D], F32, kind="ExternalInput").ap()
    nw = nc.dram_tensor("nw", [1, D], F32, kind="ExternalInput").ap()
    wkT = nc.dram_tensor("wkT", [D, 64], F32, kind="ExternalInput").ap()
    wqT = nc.dram_tensor("wqT", [D, 64], F32, kind="ExternalInput").ap()
    wvT = nc.dram_tensor("wvT", [D, D], F32, kind="ExternalInput").ap()
    eta = nc.dram_tensor("eta", [64, 1], F32, kind="ExternalInput").ap()
    ident = nc.dram_tensor("ident", [128, 128], F32, kind="ExternalInput").ap()
    xn_o = nc.dram_tensor("xn", [TSH, D], F32, kind="ExternalOutput").ap()
    ekfm_o = nc.dram_tensor("ekfm", [64, TSH], F32, kind="ExternalOutput").ap()
    knegfm_o = nc.dram_tensor("knegfm", [64, TSH], F32, kind="ExternalOutput").ap()
    qfm_o = nc.dram_tensor("qfm", [64, TSH], F32, kind="ExternalOutput").ap()
    vtok_o = nc.dram_tensor("vtok", [TSH, D], F32, kind="ExternalOutput").ap()

    NT = TSH // 128
    from contextlib import ExitStack
    with ExitStack() as ctx:
        def sb(name, shape, dt=F32):
            return ctx.enter_context(nc.sbuf_tensor(name, shape, dt))
        def psum(name, shape):
            return ctx.enter_context(nc.psum_tensor(name, shape, F32))
        xt = sb("xt", [128, D]); xnt = sb("xnt", [128, D]); xn2t = sb("xn2t", [128, D])
        dump = sb("dump", [128, D]); col4 = sb("col4", [128, 4])
        nwbc = sb("nwbc", [128, D]); nwrow = sb("nwrow", [1, D])
        onecol = sb("onecol", [128, 1]); identt = sb("identt", [128, 128])
        x2T = sb("x2T", [128, 8 * 128], F32R)
        wk = sb("wk", [128, 8 * 64], F32R); wq = sb("wq", [128, 8 * 64], F32R)
        wv = sb("wv", [128, 8 * D], F32R)
        etac = sb("etac", [64, 1]); kq = sb("kq", [64, 2 * 128]); ekk = sb("ekk", [64, 3 * 128])
        vstage = sb("vstage", [128, D])
        ps_tr = psum("ps_tr", [128, 128]); ps_kq = psum("ps_kq", [128, 512]); ps_v = psum("ps_v", [128, 512])
        cget = sb("consts", [128, 8]); nrt_ = sb("nr", [128, 3])
        S = ctx.enter_context(nc.semaphore("S"))
        sq = Seq()
        cc = Consts(nc, sq, cget)
        sq.dma("sync", identt[:, :], ident)
        sq.dma("sync", nwrow[0:1, :], nw)
        sq.dma("sync", etac[:, :], eta)
        for c in range(8):
            sq.dma("sync", wk[:, c * 64:(c + 1) * 64], wkT[c * 128:(c + 1) * 128, :].bitcast(F32R))
            sq.dma("sync", wq[:, c * 64:(c + 1) * 64], wqT[c * 128:(c + 1) * 128, :].bitcast(F32R))
            sq.dma("sync", wv[:, c * D:(c + 1) * D], wvT[c * 128:(c + 1) * 128, :].bitcast(F32R))
        sq.emit("scalar", lambda s: s.activation(etac[:, :], etac[:, :], AF.Abs))
        sq.emit("vector", lambda v: v.tensor_scalar(etac[:, :], etac[:, :], ETA_MIN, None, ALU.max))
        sq.emit("vector", lambda v: v.memset(onecol[:, :], 1.0))
        sq.emit("vector", lambda v: v.memset(dump[0:1, 0:128], 1.0))
        for j in range(D // 512):
            sq.emit("tensor", lambda t, j=j: t.matmul(ps_v[:, :], dump[0:1, 0:128], nwrow[0:1, j * 512:(j + 1) * 512], start=True, stop=True))
            sq.emit("vector", lambda v, j=j: v.tensor_copy(nwbc[:, j * 512:(j + 1) * 512], ps_v[:, :]))

        scr = {"dump": dump, "col4": col4, "nr": nrt_}
        for it in range(NT):
            tok0 = it * 128
            sq.dma("sync", xt[:, :], x[tok0:tok0 + 128, :])
            emit_ln_token_tile(sq, nc, xt[:, :], xnt[:, :], scr, cc)
            sq.emit("vector", lambda v: v.scalar_tensor_tensor(dump[:, :], xnt[:, :], 1.0, xnt[:, :], ALU.mult, ALU.mult, accum_out=col4[:, 0:1]))
            sq.emit("vector", lambda v: v.tensor_scalar(col4[:, 1:2], col4[:, 0:1], 1.0 / D, None, ALU.mult))
            emit_rsqrt(sq, col4[:, 3:4], col4[:, 1:2], scr, cc, eps=EPS_RMS)
            sq.emit("vector", lambda v: v.scalar_tensor_tensor(xn2t[:, :], xnt[:, :], col4[:, 3:4], nwbc[:, :], ALU.mult, ALU.mult))
            sq.dma("gpsimd", xn_o[tok0:tok0 + 128, :], xnt[:, :])
            for c in range(8):
                sq.emit("tensor", lambda t, c=c: t.transpose(ps_tr[:, :], xn2t[:, c * 128:(c + 1) * 128], identt[:, :]))
                sq.emit("vector", lambda v, c=c: v.tensor_copy(x2T[:, c * 128:(c + 1) * 128].bitcast(F32R), ps_tr[:, :]))
            for c in range(8):
                sq.emit("tensor", lambda t, c=c: t.matmul(ps_kq[0:64, 0:128], wk[:, c * 64:(c + 1) * 64], x2T[:, c * 128:(c + 1) * 128], start=(c == 0), stop=(c == 7)))
            for c in range(8):
                sq.emit("tensor", lambda t, c=c: t.matmul(ps_kq[0:64, 128:256], wq[:, c * 64:(c + 1) * 64], x2T[:, c * 128:(c + 1) * 128], start=(c == 0), stop=(c == 7)))
            sq.emit("vector", lambda v: v.tensor_scalar(kq[:, 0:256], ps_kq[0:64, 0:256], 0.0, None, ALU.min))
            sq.emit("scalar", lambda s: s.activation(kq[:, 0:256], kq[:, 0:256], AF.Exp))
            sq.emit("vector", lambda v: v.scalar_tensor_tensor(kq[:, 0:256], ps_kq[0:64, 0:256], 0.0, kq[:, 0:256], ALU.max, ALU.add))
            sq.emit("vector", lambda v: v.tensor_scalar(ekk[:, 0:128], kq[:, 0:128], etac[:, 0:1], None, ALU.mult))
            sq.emit("vector", lambda v: v.tensor_scalar(ekk[:, 128:256], kq[:, 0:128], -1.0, None, ALU.mult))
            sq.emit("vector", lambda v: v.tensor_copy(ekk[:, 256:384], kq[:, 128:256]))
            sq.dma("gpsimd", ekfm_o[:, tok0:tok0 + 128], ekk[:, 0:128])
            sq.dma("gpsimd", knegfm_o[:, tok0:tok0 + 128], ekk[:, 128:256])
            sq.dma("gpsimd", qfm_o[:, tok0:tok0 + 128], ekk[:, 256:384])
            for oj in range(2):
                for c in range(8):
                    sq.emit("tensor", lambda t, oj=oj, c=c: t.matmul(ps_v[:, :], x2T[:, c * 128:(c + 1) * 128], wv[:, c * D + oj * 512:c * D + (oj + 1) * 512], start=(c == 0), stop=(c == 7)))
                sq.emit("vector", lambda v, oj=oj: v.tensor_copy(vstage[:, oj * 512:(oj + 1) * 512], ps_v[:, :]))
            sq.dma("gpsimd", vtok_o[tok0:tok0 + 128, :], vstage[:, :])
        sq.build(nc, S)
    return nc


def emit_rsqrt_row(sq, out, a_in, rowscr, P=1):
    t0 = rowscr[0:P, 0:1]
    t1 = rowscr[0:P, 1:2]
    sq.emit("scalar", lambda s: s.activation(t0, a_in, AF.Ln))
    sq.emit("scalar", lambda s: s.activation(t0, t0, AF.Exp, scale=-0.5))
    sq.emit("vector", lambda v: v.scalar_tensor_tensor(t1, t0, t0, a_in, ALU.mult, ALU.mult))
    sq.emit("vector", lambda v: v.tensor_scalar(t1, t1, -0.5, 1.5, ALU.mult, ALU.add))
    sq.emit("vector", lambda v: v.scalar_tensor_tensor(out, t1, 1.0, t0, ALU.mult, ALU.mult))


def build_stage_scan(nsteps, t0_global):
    FD = 64 * D
    nc = bass.Bass("TRN2", target_bir_lowering=False, debug=False)
    kneg = nc.dram_tensor("kneg", [64, T], F32, kind="ExternalInput").ap()
    ekf = nc.dram_tensor("ekf", [64, T], F32, kind="ExternalInput").ap()
    vtok = nc.dram_tensor("vtok", [T, D], F32, kind="ExternalInput").ap()
    Min = nc.dram_tensor("Min", [64, D], F32, kind="ExternalInput").ap()
    ident = nc.dram_tensor("ident", [128, 128], F32, kind="ExternalInput").ap()
    ident64 = nc.dram_tensor("ident64", [64, 64], F32, kind="ExternalInput").ap()
    onesd = nc.dram_tensor("onesd", [128, 128], F32, kind="ExternalInput").ap()
    Mout_o = nc.dram_tensor("Mout", [64, D], F32, kind="ExternalOutput").ap()

    from contextlib import ExitStack
    with ExitStack() as ctx:
        def sb(name, shape, dt=F32):
            return ctx.enter_context(nc.sbuf_tensor(name, shape, dt))
        def psum(name, shape):
            return ctx.enter_context(nc.psum_tensor(name, shape, F32))
        ntile = ceil_div(nsteps, 128)
        vt = sb("vt", [128, ntile * D], F32R)
        kn = sb("kn", [64, nsteps], F32R)
        ek = sb("ek", [64, nsteps], F32R)
        Mr = sb("Mr", [64, D], F32R)
        T1 = sb("T1", [64, D])
        identt = sb("identt", [128, 128], F32R)
        id64 = sb("id64", [64, 64], F32R)
        ekrow = sb("ekrow", [1, 64], F32R)
        vsrow = sb("vsrow", [1, D], F32R)
        rowscr = sb("rowscr", [1, 8])
        srow = sb("srow", [1, 2], F32R)
        cols = sb("cols", [64, 2], F32R)
        onecol = sb("onecol", [64, 1], F32R)
        onerow = sb("onerow", [1, 64], F32R)
        dump = sb("dumpS", [1, D])
        dump64 = sb("dump64", [64, D])
        bcols = sb("bcols", [64, 2])
        cget = sb("consts", [128, 8])
        ps_r = psum("ps_r", [1, D])
        ps_v2 = psum("ps_v2", [1, D])
        ps_d = psum("ps_d", [64, D])
        ps_sm = psum("ps_sm", [64, 128])
        ps_e = ps_sm[0:1, 0:64]
        ps_s = ps_sm[0:1, 64:66]
        ps_b = ps_sm[0:64, 66:68]
        S = ctx.enter_context(nc.semaphore("S"))
        ctx.enter_context(nc.allow_low_precision(reason="f32r stats tiles; scan is contractive"))
        sq = Seq()
        cc = Consts(nc, sq, cget)
        sq.dma("sync", identt[:, :], ident.bitcast(F32R))
        sq.dma("sync", id64[:, :], ident64.bitcast(F32R))
        sq.dma("sync", kn[:, :], kneg[:, t0_global:t0_global + nsteps].bitcast(F32R), allow_slow_non_contiguous=True)
        sq.dma("sync", ek[:, :], ekf[:, t0_global:t0_global + nsteps].bitcast(F32R), allow_slow_non_contiguous=True)
        sq.dma("sync", Mr[:, :], Min.bitcast(F32R))
        for it in range(ntile):
            r0 = t0_global + it * 128
            sq.dma("sync", vt[:, it * D:(it + 1) * D],
                   vtok[r0:r0 + 128, :].bitcast(F32R))
        sq.dma("sync", onecol[:, :], onesd[0:64, 0:1].bitcast(F32R), allow_slow_non_contiguous=True)
        sq.dma("sync", onerow[:, :], onesd[0:1, 0:64].bitcast(F32R))

        for st in range(nsteps):
            tile, tin = st // 128, st % 128
            for j in range(2):
                vslj = vt[:, tile * D + j * 512: tile * D + (j + 1) * 512]
                icol = identt[:, tin:tin + 1]
                sq.emit("tensor", lambda t, j=j, vslj=vslj, icol=icol: t.matmul(ps_v2[0:1, j * 512:(j + 1) * 512], icol, vslj, start=True, stop=True))
            for j in range(2):
                vslj = vt[:, tile * D + j * 512: tile * D + (j + 1) * 512]
                icol = identt[:, tin:tin + 1]
                kcol = kn[:, st:st + 1]
                sq.emit("tensor", lambda t, j=j, vslj=vslj, icol=icol: t.matmul(ps_r[0:1, j * 512:(j + 1) * 512], icol, vslj, start=True, stop=False))
                sq.emit("tensor", lambda t, j=j, kcol=kcol: t.matmul(ps_r[0:1, j * 512:(j + 1) * 512], kcol, Mr[:, j * 512:(j + 1) * 512], start=False, stop=True))
            sq.emit("scalar", lambda s: s.activation(dump[0:1, :], ps_r[0:1, :], AF.Square, accum_out=rowscr[0:1, 2:3]))
            sq.emit("vector", lambda v: v.tensor_scalar(rowscr[0:1, 2:3], rowscr[0:1, 2:3], 1e-38, None, ALU.add))
            emit_rsqrt_row(sq, rowscr[0:1, 3:4], rowscr[0:1, 2:3], rowscr)
            sq.emit("vector", lambda v: v.scalar_tensor_tensor(rowscr[0:1, 3:4], rowscr[0:1, 3:4], 1.0, rowscr[0:1, 2:3], ALU.mult, ALU.mult))
            sq.emit("vector", lambda v: v.tensor_scalar(vsrow[0:1, :], ps_v2[0:1, :], rowscr[0:1, 3:4], None, ALU.mult))
            ecol = ek[:, st:st + 1]
            sq.emit("tensor", lambda t, ecol=ecol: t.matmul(ps_e, ecol, id64[:, :], start=True, stop=True))
            sq.emit("vector", lambda v: v.tensor_copy(ekrow[0:1, :], ps_e))
            for j in range(2):
                sq.emit("tensor", lambda t, j=j: t.matmul(ps_d[:, j * 512:(j + 1) * 512], ekrow[0:1, :], vsrow[0:1, j * 512:(j + 1) * 512], start=True, stop=True))
            sq.emit("vector", lambda v: v.tensor_tensor(T1[:, :], Mr[:, :], ps_d[:, :], ALU.add))
            sq.emit("vector", lambda v: v.tensor_reduce(cols[:, 0:1], T1[:, :], mybir.AxisListType.X, ALU.add))
            sq.emit("vector", lambda v: v.scalar_tensor_tensor(dump64[:, :], T1[:, :], 1.0, T1[:, :], ALU.mult, ALU.mult, accum_out=cols[:, 1:2]))
            sq.emit("tensor", lambda t: t.matmul(ps_s, onecol[:, :], cols[:, :], start=True, stop=True))
            sq.emit("vector", lambda v: v.tensor_scalar(rowscr[0:1, 4:6], ps_s, 1.0 / FD, None, ALU.mult))
            sq.emit("vector", lambda v: v.scalar_tensor_tensor(rowscr[0:1, 6:7], rowscr[0:1, 4:5], rowscr[0:1, 4:5], rowscr[0:1, 5:6], ALU.mult, ALU.subtract))
            sq.emit("vector", lambda v: v.tensor_scalar(rowscr[0:1, 6:7], rowscr[0:1, 6:7], -1.0, float(EPS_LN), ALU.mult, ALU.add))
            emit_rsqrt_row(sq, rowscr[0:1, 7:8], rowscr[0:1, 6:7], rowscr)
            sq.emit("vector", lambda v: v.tensor_copy(srow[0:1, 0:1], rowscr[0:1, 4:5]))
            sq.emit("vector", lambda v: v.tensor_copy(srow[0:1, 1:2], rowscr[0:1, 7:8]))
            sq.emit("tensor", lambda t: t.matmul(ps_b, onerow[0:1, :], srow[0:1, :], start=True, stop=True))
            sq.emit("vector", lambda v: v.tensor_copy(bcols[:, :], ps_b))
            sq.emit("vector", lambda v: v.tensor_scalar(Mr[:, :], T1[:, :], bcols[:, 0:1], bcols[:, 1:2], ALU.subtract, ALU.mult))
        sq.dma("gpsimd", Mout_o, Mr[:, :].bitcast(F32))
        sq.build(nc, S)
    return nc


def build_stage_c():
    nc = bass.Bass("TRN2", target_bir_lowering=False, debug=False)
    qfm = nc.dram_tensor("qfm", [64, TSH], F32, kind="ExternalInput").ap()
    Md = nc.dram_tensor("M", [64, D], F32, kind="ExternalInput").ap()
    woutT = nc.dram_tensor("woutT", [D, D], F32, kind="ExternalInput").ap()
    xn = nc.dram_tensor("xn", [TSH, D], F32, kind="ExternalInput").ap()
    x = nc.dram_tensor("x", [TSH, D], F32, kind="ExternalInput").ap()
    x1_o = nc.dram_tensor("x1", [TSH, D], F32, kind="ExternalOutput").ap()
    from contextlib import ExitStack
    with ExitStack() as ctx:
        def sb(name, shape, dt=F32):
            return ctx.enter_context(nc.sbuf_tensor(name, shape, dt))
        def psum(name, shape):
            return ctx.enter_context(nc.psum_tensor(name, shape, F32))
        qf = sb("qf", [64, TSH], F32R)
        Mt = sb("Mt", [64, D], F32R)
        wo = sb("wo", [128, 8 * D], F32R)
        mofm = sb("mofm", [128, 8 * TSH], F32R)
        xnt = sb("xnt", [128, D]); xtt = sb("xtt", [128, D])
        inner = sb("inner", [128, D]); t1 = sb("t1", [128, D])
        dump = sb("dumpC", [128, D]); col4 = sb("col4C", [128, 4]); nr = sb("nrC", [128, 3])
        cget = sb("constsC", [128, 8])
        ps = psum("psC", [128, 512]); ps2 = psum("psC2", [128, 512])
        S = ctx.enter_context(nc.semaphore("S"))
        sq = Seq()
        cc = Consts(nc, sq, cget)
        scr = {"dump": dump, "col4": col4, "nr": nr}
        sq.dma("sync", qf[:, :], qfm.bitcast(F32R))
        sq.dma("sync", Mt[:, :], Md.bitcast(F32R))
        for c in range(8):
            sq.dma("sync", wo[:, c * D:(c + 1) * D], woutT[c * 128:(c + 1) * 128, :].bitcast(F32R))
        for c in range(8):
            for tsl in range(TSH // 512):
                Msl = Mt[:, c * 128:(c + 1) * 128]
                qsl = qf[:, tsl * 512:(tsl + 1) * 512]
                sq.emit("tensor", lambda t, Msl=Msl, qsl=qsl: t.matmul(ps[:, :], Msl, qsl, start=True, stop=True))
                dst = mofm[:, c * TSH + tsl * 512: c * TSH + (tsl + 1) * 512]
                sq.emit("vector", lambda v, dst=dst: v.tensor_copy(dst, ps[:, :]))
        for it in range(TSH // 128):
            tok0 = it * 128
            sq.dma("sync", xnt[:, :], xn[tok0:tok0 + 128, :])
            sq.dma("sync", xtt[:, :], x[tok0:tok0 + 128, :])
            for oh in range(2):
                for c in range(8):
                    lh = mofm[:, c * TSH + tok0: c * TSH + tok0 + 128]
                    rh = wo[:, c * D + oh * 512: c * D + (oh + 1) * 512]
                    sq.emit("tensor", lambda t, lh=lh, rh=rh, c=c: t.matmul(ps2[:, :], lh, rh, start=(c == 0), stop=(c == 7)))
                sl = slice(oh * 512, (oh + 1) * 512)
                sq.emit("vector", lambda v, sl=sl: v.tensor_tensor(inner[:, sl], xnt[:, sl], ps2[:, :], ALU.add))
            emit_ln_token_tile(sq, nc, inner[:, :], t1[:, :], scr, cc)
            sq.emit("vector", lambda v: v.tensor_tensor(t1[:, :], t1[:, :], xtt[:, :], ALU.add))
            sq.dma("gpsimd", x1_o[tok0:tok0 + 128, :], t1[:, :])
        sq.build(nc, S)
    return nc


def build_stage_d():
    H = 4096
    nc = bass.Bass("TRN2", target_bir_lowering=False, debug=False)
    x1 = nc.dram_tensor("x1", [TSH, D], F32, kind="ExternalInput").ap()
    w1T = nc.dram_tensor("w1R", [32, 128, 8 * 128], F32, kind="ExternalInput").ap()
    w2T = nc.dram_tensor("w2R", [32, 128, 8 * 128], F32, kind="ExternalInput").ap()
    h1T = nc.dram_tensor("h1R", [32, 128, 32 * 128], F32, kind="ExternalInput").ap()
    h2T = nc.dram_tensor("h2R", [32, 128, 32 * 128], F32, kind="ExternalInput").ap()
    w3T = nc.dram_tensor("w3T", [H, D], F32, kind="ExternalInput").ap()
    s1d = nc.dram_tensor("s1", [H, 1], F32, kind="ExternalInput").ap()
    s2d = nc.dram_tensor("s2", [H, 1], F32, kind="ExternalInput").ap()
    sh1d = nc.dram_tensor("sh1", [H, 1], F32, kind="ExternalInput").ap()
    sh2d = nc.dram_tensor("sh2", [H, 1], F32, kind="ExternalInput").ap()
    s3d = nc.dram_tensor("s3row", [1, D], F32, kind="ExternalInput").ap()
    ident = nc.dram_tensor("ident", [128, 128], F32, kind="ExternalInput").ap()
    x2_o = nc.dram_tensor("x2", [TSH, D], F32, kind="ExternalOutput").ap()
    TS = 512
    from contextlib import ExitStack
    with ExitStack() as ctx:
        def sb(name, shape, dt=F32):
            return ctx.enter_context(nc.sbuf_tensor(name, shape, dt))
        def psum(name, shape):
            return ctx.enter_context(nc.psum_tensor(name, shape, F32))
        xt = sb("xt", [128, D]); xnt = sb("xnt", [128, D])
        dump = sb("dumpD", [128, D]); col4 = sb("col4D", [128, 4]); nr = sb("nrD", [128, 3])
        identt = sb("identt", [128, 128])
        xT = sb("xT", [128, 8 * TS], F32R)
        hbuf = sb("hbuf", [128, 32 * TS], F32R)
        gbuf = sb("gbuf", [128, 32 * TS], F32R)
        wblk = sb("wblk", [128, 32 * 128], F32R)
        wblk2 = sb("wblk2", [128, 8 * 128], F32R)
        w3b = sb("w3b", [128, D], F32R)
        scol = sb("scol", [128, 4])
        s3bc = sb("s3bc", [128, D])
        s3row = sb("s3rowS", [1, D])
        u1t = sb("u1t", [128, TS])
        fout = sb("fout", [128, D]); t1 = sb("t1D", [128, D])
        cget = sb("constsD", [128, 8])
        ps = psum("psD", [128, 512]); ps2 = psum("psD2", [128, 512]); ps_tr = psum("psD3", [128, 128])
        S = ctx.enter_context(nc.semaphore("S"))
        sq = Seq()
        cc = Consts(nc, sq, cget)
        scr = {"dump": dump, "col4": col4, "nr": nr}
        sq.dma("sync", identt[:, :], ident)
        sq.dma("sync", s3row[0:1, :], s3d)
        sq.emit("vector", lambda v: v.memset(dump[0:1, 0:128], 1.0))
        for j in range(2):
            sq.emit("tensor", lambda t, j=j: t.matmul(ps[:, :], dump[0:1, 0:128], s3row[0:1, j * 512:(j + 1) * 512], start=True, stop=True))
            sq.emit("vector", lambda v, j=j: v.tensor_copy(s3bc[:, j * 512:(j + 1) * 512], ps[:, :]))
        NSL = TSH // TS
        for isl in range(NSL):
            st0 = isl * TS
            for jt in range(TS // 128):
                tok0 = st0 + jt * 128
                sq.dma("sync", xt[:, :], x1[tok0:tok0 + 128, :])
                emit_ln_token_tile(sq, nc, xt[:, :], xnt[:, :], scr, cc)
                for c in range(8):
                    src_ = xnt[:, c * 128:(c + 1) * 128]
                    dst_ = xT[:, c * TS + jt * 128: c * TS + (jt + 1) * 128]
                    sq.emit("tensor", lambda t, src_=src_: t.transpose(ps_tr[:, :], src_, identt[:, :]))
                    sq.emit("vector", lambda v, dst_=dst_: v.tensor_copy(dst_, ps_tr[:, :]))
            for oc in range(32):
                sq.dma("sync", wblk2[:, 0:8 * 128], w1T[oc].bitcast(F32R))
                sq.dma("sync", scol[:, 0:1], s1d[oc * 128:(oc + 1) * 128, :])
                sq.dma("sync", scol[:, 1:2], s2d[oc * 128:(oc + 1) * 128, :])
                for c in range(8):
                    lh = wblk2[:, c * 128:(c + 1) * 128]
                    rh = xT[:, c * TS:(c + 1) * TS]
                    sq.emit("tensor", lambda t, lh=lh, rh=rh, c=c: t.matmul(ps[:, :], lh, rh, start=(c == 0), stop=(c == 7)))
                sq.emit("scalar", lambda s: s.activation(u1t[:, :], ps[:, :], AF.Silu, scale=scol[:, 0:1]))
                sq.dma("sync", wblk2[:, 0:8 * 128], w2T[oc].bitcast(F32R))
                for c in range(8):
                    lh = wblk2[:, c * 128:(c + 1) * 128]
                    rh = xT[:, c * TS:(c + 1) * TS]
                    sq.emit("tensor", lambda t, lh=lh, rh=rh, c=c: t.matmul(ps2[:, :], lh, rh, start=(c == 0), stop=(c == 7)))
                dst_ = hbuf[:, oc * TS:(oc + 1) * TS]
                sq.emit("vector", lambda v, dst_=dst_: v.scalar_tensor_tensor(dst_, ps2[:, :], scol[:, 1:2], u1t[:, :], ALU.mult, ALU.mult))
            for oc in range(32):
                sq.dma("sync", wblk[:, :], h1T[oc].bitcast(F32R))
                sq.dma("sync", scol[:, 2:3], sh1d[oc * 128:(oc + 1) * 128, :])
                for c in range(32):
                    lh = wblk[:, c * 128:(c + 1) * 128]
                    rh = hbuf[:, c * TS:(c + 1) * TS]
                    sq.emit("tensor", lambda t, lh=lh, rh=rh, c=c: t.matmul(ps[:, :], lh, rh, start=(c == 0), stop=(c == 31)))
                dst_ = gbuf[:, oc * TS:(oc + 1) * TS]
                sq.emit("scalar", lambda s, dst_=dst_: s.activation(dst_, ps[:, :], AF.Gelu, scale=scol[:, 2:3]))
            for oc in range(32):
                sq.dma("sync", wblk[:, :], h2T[oc].bitcast(F32R))
                sq.dma("sync", scol[:, 3:4], sh2d[oc * 128:(oc + 1) * 128, :])
                for c in range(32):
                    lh = wblk[:, c * 128:(c + 1) * 128]
                    rh = gbuf[:, c * TS:(c + 1) * TS]
                    sq.emit("tensor", lambda t, lh=lh, rh=rh, c=c: t.matmul(ps[:, :], lh, rh, start=(c == 0), stop=(c == 31)))
                dst_ = hbuf[:, oc * TS:(oc + 1) * TS]
                sq.emit("scalar", lambda s, dst_=dst_: s.activation(dst_, ps[:, :], AF.Gelu, scale=scol[:, 3:4]))
            for jt in range(TS // 128):
                tok0 = st0 + jt * 128
                for oh in range(2):
                    for c in range(32):
                        sq.dma("sync", w3b[:, oh * 512:(oh + 1) * 512],
                               w3T[c * 128:(c + 1) * 128, oh * 512:(oh + 1) * 512].bitcast(F32R))
                        lh = hbuf[:, c * TS + jt * 128: c * TS + (jt + 1) * 128]
                        sq.emit("tensor", lambda t, lh=lh, oh=oh, c=c: t.matmul(ps2[:, :], lh, w3b[:, oh * 512:(oh + 1) * 512], start=(c == 0), stop=(c == 31)))
                    sl = slice(oh * 512, (oh + 1) * 512)
                    sq.emit("vector", lambda v, sl=sl: v.tensor_tensor(fout[:, sl], ps2[:, :], s3bc[:, sl], ALU.mult))
                sq.dma("sync", xt[:, :], x1[tok0:tok0 + 128, :])
                emit_ln_token_tile(sq, nc, fout[:, :], t1[:, :], scr, cc)
                sq.emit("vector", lambda v: v.tensor_tensor(t1[:, :], t1[:, :], xt[:, :], ALU.add))
                sq.dma("gpsimd", x2_o[tok0:tok0 + 128, :], t1[:, :])
        sq.build(nc, S)
    return nc


def build_stage_e():
    nc = bass.Bass("TRN2", target_bir_lowering=False, debug=False)
    x2 = nc.dram_tensor("x2", [TSH, D], F32, kind="ExternalInput").ap()
    wqT = nc.dram_tensor("wqT", [D, D], F32, kind="ExternalInput").ap()
    wkT = nc.dram_tensor("wkT", [D, D], F32, kind="ExternalInput").ap()
    wvT = nc.dram_tensor("wvT", [D, D], F32, kind="ExternalInput").ap()
    ident = nc.dram_tensor("ident", [128, 128], F32, kind="ExternalInput").ap()
    qfm_o = nc.dram_tensor("qfm", [D, TSH], F32, kind="ExternalOutput").ap()
    kfm_o = nc.dram_tensor("kfm", [D, TSH], F32, kind="ExternalOutput").ap()
    vtok_o = nc.dram_tensor("vtok", [TSH, D], F32, kind="ExternalOutput").ap()
    from contextlib import ExitStack
    with ExitStack() as ctx:
        def sb(name, shape, dt=F32):
            return ctx.enter_context(nc.sbuf_tensor(name, shape, dt))
        def psum(name, shape):
            return ctx.enter_context(nc.psum_tensor(name, shape, F32))
        xt = sb("xt", [128, D]); xnt = sb("xnt", [128, D])
        dump = sb("dumpE", [128, D]); col4 = sb("col4E", [128, 4]); nr = sb("nrE", [128, 3])
        identt = sb("identt", [128, 128])
        xT = sb("xTE", [128, 8 * TSH // 4], F32R)
        wq = sb("wqE", [128, 8 * D], F32R); wk = sb("wkE", [128, 8 * D], F32R)
        wv = sb("wvE", [128, 8 * D], F32R)
        stage = sb("stageE", [128, D])
        cget = sb("constsE", [128, 8])
        ps = psum("psE", [128, 512]); ps_tr = psum("psE2", [128, 128])
        S = ctx.enter_context(nc.semaphore("S"))
        sq = Seq(); cc = Consts(nc, sq, cget)
        scr = {"dump": dump, "col4": col4, "nr": nr}
        sq.dma("sync", identt[:, :], ident)
        for c in range(8):
            sq.dma("sync", wq[:, c * D:(c + 1) * D], wqT[c * 128:(c + 1) * 128, :].bitcast(F32R))
            sq.dma("sync", wk[:, c * D:(c + 1) * D], wkT[c * 128:(c + 1) * 128, :].bitcast(F32R))
            sq.dma("sync", wv[:, c * D:(c + 1) * D], wvT[c * 128:(c + 1) * 128, :].bitcast(F32R))
        TS = 512
        for isl in range(TSH // TS):
            st0 = isl * TS
            for jt in range(TS // 128):
                tok0 = st0 + jt * 128
                sq.dma("sync", xt[:, :], x2[tok0:tok0 + 128, :])
                emit_ln_token_tile(sq, nc, xt[:, :], xnt[:, :], scr, cc)
                for c in range(8):
                    src_ = xnt[:, c * 128:(c + 1) * 128]
                    dst_ = xT[:, c * TS + jt * 128: c * TS + (jt + 1) * 128]
                    sq.emit("tensor", lambda t, src_=src_: t.transpose(ps_tr[:, :], src_, identt[:, :]))
                    sq.emit("vector", lambda v, dst_=dst_: v.tensor_copy(dst_, ps_tr[:, :]))
            for wmat, outd in ((wq, qfm_o), (wk, kfm_o)):
                for ot in range(8):
                    for c in range(8):
                        lh = wmat[:, c * D + ot * 128: c * D + (ot + 1) * 128]
                        rh = xT[:, c * TS:(c + 1) * TS]
                        sq.emit("tensor", lambda t, lh=lh, rh=rh, c=c: t.matmul(ps[:, :], lh, rh, start=(c == 0), stop=(c == 7)))
                    sq.emit("vector", lambda v: v.tensor_copy(stage[:, 0:512], ps[:, :]))
                    sq.dma("gpsimd", outd[ot * 128:(ot + 1) * 128, st0:st0 + TS], stage[:, 0:512])
            for jt in range(TS // 128):
                tok0 = st0 + jt * 128
                for oh in range(2):
                    for c in range(8):
                        lh = xT[:, c * TS + jt * 128: c * TS + (jt + 1) * 128]
                        rh = wv[:, c * D + oh * 512: c * D + (oh + 1) * 512]
                        sq.emit("tensor", lambda t, lh=lh, rh=rh, c=c: t.matmul(ps[:, :], lh, rh, start=(c == 0), stop=(c == 7)))
                    sl = slice(oh * 512, (oh + 1) * 512)
                    sq.emit("vector", lambda v, sl=sl: v.tensor_copy(stage[:, sl], ps[:, :]))
                sq.dma("gpsimd", vtok_o[tok0:tok0 + 128, :], stage[:, :])
        sq.build(nc, S)
    return nc


F_NBLK = [20, 24, 28, 32]
F_MOFF = [0, 20, 44, 72]
F_MTOT = 104


def build_stage_f():
    NH, HD = 16, 64
    nc = bass.Bass("TRN2", target_bir_lowering=False, debug=False)
    qfm = nc.dram_tensor("qfm", [D, TSH], F32, kind="ExternalInput").ap()
    kfm = nc.dram_tensor("kfm", [D, T], F32, kind="ExternalInput").ap()
    vtokR = nc.dram_tensor("vtokR", [128, T // 128, D], F32, kind="ExternalInput").ap()
    masks = nc.dram_tensor("masks", [F_MTOT, 128, 512], F32, kind="ExternalInput").ap()
    onesd = nc.dram_tensor("onesd", [128, 128], F32, kind="ExternalInput").ap()
    onesr = nc.dram_tensor("onesr", [1, T], F32, kind="ExternalInput").ap()
    attn_o = nc.dram_tensor("attn", [TSH, D], F32, kind="ExternalOutput").ap()
    from contextlib import ExitStack
    with ExitStack() as ctx:
        def sb(name, shape, dt=F32):
            return ctx.enter_context(nc.sbuf_tensor(name, shape, dt))
        def psum(name, shape):
            return ctx.enter_context(nc.psum_tensor(name, shape, F32))
        kh = sb("kh", [65, T], F32R)
        qh = sb("qh", [65, TSH], F32R)
        vh = sb("vh", [128, (T // 128) * (HD + 2)], F32R)
        sqk = sb("sqk", [64, T], F32R)
        krow = sb("krow", [1, T]); qrow = sb("qrow", [1, TSH])
        rows = sb("rowsF", [1, 8])
        mrow = sb("mrow", [1, TSH])
        pt = sb("pt", [128, 512], F32R)
        pte = sb("pte", [128, 512])
        mask_t = sb("mask_t", [128, 512])
        rec = sb("recF", [128, 1])
        att = sb("attF", [128, 64])
        onec = sb("onecF", [128, 1], F32R)
        cget = sb("constsF", [128, 8])
        ps_sc = psum("ps_sc", [128, 512])
        ps_nd = psum("ps_nd", [128, 4 * 128])
        ps_r = psum("ps_rF", [1, 512])
        S = ctx.enter_context(nc.semaphore("S"))
        ctx.enter_context(nc.allow_low_precision(reason="f32r attn tiles"))
        sq = Seq(); cc = Consts(nc, sq, cget)
        sq.dma("sync", onec[:, :], onesd[:, 0:1].bitcast(F32R), allow_slow_non_contiguous=True)
        for tkb in range(T // 128):
            sq.dma("sync", vh[:, tkb * 66 + 64: tkb * 66 + 66], onesd[:, 0:2].bitcast(F32R), allow_slow_non_contiguous=True)
        for h in range(NH):
            sq.dma("sync", kh[0:64, :], kfm[h * 64:(h + 1) * 64, :].bitcast(F32R))
            sq.dma("sync", kh[64:65, :], onesr.bitcast(F32R))
            sq.dma("sync", qh[0:64, :], qfm[h * 64:(h + 1) * 64, :].bitcast(F32R))
            sq.emit("vector", lambda v: v.tensor_scalar(qh[0:64, :], qh[0:64, :], 0.125, None, ALU.mult))
            for tkb in range(T // 128):
                sq.dma("sync", vh[:, tkb * 66: tkb * 66 + 64], vtokR[:, tkb, h * 64:(h + 1) * 64].bitcast(F32R))
            sq.emit("vector", lambda v: v.scalar_tensor_tensor(sqk[:, :], kh[0:64, :], 1.0, kh[0:64, :], ALU.mult, ALU.mult))
            for j in range(T // 512):
                sl = slice(j * 512, (j + 1) * 512)
                sq.emit("tensor", lambda t, sl=sl: t.matmul(ps_r[0:1, :], onec[0:64, :], sqk[:, sl], start=True, stop=True))
                sq.emit("vector", lambda v, sl=sl: v.tensor_copy(krow[0:1, sl], ps_r[0:1, :]))
            sq.emit("vector", lambda v: v.tensor_reduce(rows[0:1, 0:1], krow[0:1, :], mybir.AxisListType.X, ALU.max))
            sq.emit("vector", lambda v: v.scalar_tensor_tensor(sqk[:, 0:TSH], qh[0:64, :], 1.0, qh[0:64, :], ALU.mult, ALU.mult))
            for j in range(TSH // 512):
                sl = slice(j * 512, (j + 1) * 512)
                sq.emit("tensor", lambda t, sl=sl: t.matmul(ps_r[0:1, :], onec[0:64, :], sqk[0:64, sl], start=True, stop=True))
                sq.emit("vector", lambda v, sl=sl: v.tensor_copy(qrow[0:1, sl], ps_r[0:1, :]))
            sq.emit("vector", lambda v: v.tensor_scalar(mrow[0:1, :], qrow[0:1, :], rows[0:1, 0:1], None, ALU.mult))
            sq.emit("scalar", lambda s: s.activation(mrow[0:1, :], mrow[0:1, :], AF.Sqrt))
            sq.emit("vector", lambda v: v.tensor_scalar(qh[64:65, :], mrow[0:1, :], -1.0, None, ALU.mult))
            for jt in range(TSH // 512):
                nblk = F_NBLK[jt]
                for tkb in range(nblk):
                    lh = kh[:, tkb * 128:(tkb + 1) * 128]
                    rh = qh[:, jt * 512:(jt + 1) * 512]
                    sq.dma("sync", mask_t[:, :], masks[F_MOFF[jt] + tkb])
                    sq.emit("tensor", lambda t, lh=lh, rh=rh: t.matmul(ps_sc[:, :], lh, rh, start=True, stop=True))
                    sq.emit("scalar", lambda s: s.activation(pte[:, :], ps_sc[:, :], AF.Exp))
                    sq.emit("vector", lambda v: v.tensor_tensor(pt[:, :], pte[:, :], mask_t[:, :], ALU.mult))
                    first = tkb == 0
                    last = tkb == nblk - 1
                    for ss in range(4):
                        lh2 = pt[:, ss * 128:(ss + 1) * 128]
                        rh2 = vh[:, tkb * 66:(tkb + 1) * 66]
                        d0 = ps_nd[:, ss * 128: ss * 128 + 66]
                        st_ = first and ss == 0
                        sq.emit("tensor", lambda t, lh2=lh2, rh2=rh2, d0=d0, st_=st_, last=last: t.matmul(d0, lh2, rh2, start=st_, stop=last))
                for ss in range(4):
                    den = ps_nd[:, ss * 128 + 64: ss * 128 + 65]
                    num = ps_nd[:, ss * 128: ss * 128 + 64]
                    sq.emit("vector", lambda v, den=den: v.tensor_scalar(rec[:, 0:1], den, 1e-8, None, ALU.add))
                    sq.emit("vector", lambda v: v.reciprocal(rec[:, 0:1], rec[:, 0:1]))
                    sq.emit("vector", lambda v, num=num: v.tensor_scalar(att[:, :], num, rec[:, 0:1], None, ALU.mult))
                    tok0 = jt * 512 + ss * 128
                    sq.dma("gpsimd", attn_o[tok0:tok0 + 128, h * 64:(h + 1) * 64], att[:, :])
        sq.build(nc, S)
    return nc


def build_stage_g():
    nc = bass.Bass("TRN2", target_bir_lowering=False, debug=False)
    attn = nc.dram_tensor("attn", [TSH, D], F32, kind="ExternalInput").ap()
    woT = nc.dram_tensor("woT", [D, D], F32, kind="ExternalInput").ap()
    x2 = nc.dram_tensor("x2", [TSH, D], F32, kind="ExternalInput").ap()
    ident = nc.dram_tensor("ident", [128, 128], F32, kind="ExternalInput").ap()
    x3_o = nc.dram_tensor("x3", [TSH, D], F32, kind="ExternalOutput").ap()
    xi0_o = nc.dram_tensor("xi0", [TSH, D], F32, kind="ExternalOutput").ap()
    from contextlib import ExitStack
    with ExitStack() as ctx:
        def sb(name, shape, dt=F32):
            return ctx.enter_context(nc.sbuf_tensor(name, shape, dt))
        def psum(name, shape):
            return ctx.enter_context(nc.psum_tensor(name, shape, F32))
        at = sb("atG", [128, D]); aT = sb("aTG", [128, 8 * 128], F32R)
        wo = sb("woG", [128, 8 * D], F32R)
        identt = sb("identtG", [128, 128])
        xt = sb("xtG", [128, D]); outt = sb("outtG", [128, D]); t1 = sb("t1G", [128, D])
        dump = sb("dumpG", [128, D]); col4 = sb("col4G", [128, 4]); nr = sb("nrG", [128, 3])
        cget = sb("constsG", [128, 8])
        ps = psum("psG", [128, 512]); ps_tr = psum("psG2", [128, 128])
        S = ctx.enter_context(nc.semaphore("S"))
        sq = Seq(); cc = Consts(nc, sq, cget)
        scr = {"dump": dump, "col4": col4, "nr": nr}
        sq.dma("sync", identt[:, :], ident)
        for c in range(8):
            sq.dma("sync", wo[:, c * D:(c + 1) * D], woT[c * 128:(c + 1) * 128, :].bitcast(F32R))
        for it in range(TSH // 128):
            tok0 = it * 128
            sq.dma("sync", at[:, :], attn[tok0:tok0 + 128, :])
            sq.dma("sync", xt[:, :], x2[tok0:tok0 + 128, :])
            for c in range(8):
                src_ = at[:, c * 128:(c + 1) * 128]
                dst_ = aT[:, c * 128:(c + 1) * 128]
                sq.emit("tensor", lambda t, src_=src_: t.transpose(ps_tr[:, :], src_, identt[:, :]))
                sq.emit("vector", lambda v, dst_=dst_: v.tensor_copy(dst_, ps_tr[:, :]))
            for oh in range(2):
                for c in range(8):
                    lh = aT[:, c * 128:(c + 1) * 128]
                    rh = wo[:, c * D + oh * 512: c * D + (oh + 1) * 512]
                    sq.emit("tensor", lambda t, lh=lh, rh=rh, c=c: t.matmul(ps[:, :], lh, rh, start=(c == 0), stop=(c == 7)))
                sl = slice(oh * 512, (oh + 1) * 512)
                sq.emit("vector", lambda v, sl=sl: v.tensor_copy(outt[:, sl], ps[:, :]))
            emit_ln_token_tile(sq, nc, outt[:, :], t1[:, :], scr, cc)
            sq.emit("vector", lambda v: v.tensor_tensor(t1[:, :], t1[:, :], xt[:, :], ALU.add))
            sq.dma("gpsimd", x3_o[tok0:tok0 + 128, :], t1[:, :])
            emit_ln_token_tile(sq, nc, t1[:, :], outt[:, :], scr, cc)
            sq.dma("gpsimd", xi0_o[tok0:tok0 + 128, :], outt[:, :])
        sq.build(nc, S)
    return nc


TOH = TSH + 512


def build_stage_h1():
    nc = bass.Bass("TRN2", target_bir_lowering=False, debug=False)
    xi = nc.dram_tensor("xi", [TSH, D], F32, kind="ExternalInput").ap()
    xihalo = nc.dram_tensor("xihalo", [513, D], F32, kind="ExternalInput").ap()
    wqkvR = nc.dram_tensor("wqkvR", [12, 128, 8 * 128], F32, kind="ExternalInput").ap()
    qs2 = nc.dram_tensor("qs2", [128, 1], F32, kind="ExternalInput").ap()
    ks2 = nc.dram_tensor("ks2", [128, 1], F32, kind="ExternalInput").ap()
    r64T = nc.dram_tensor("r64T", [64, 64], F32, kind="ExternalInput").ap()
    tabs = nc.dram_tensor("tabs", [4, 128, TOH], F32, kind="ExternalInput").ap()
    ident = nc.dram_tensor("ident", [128, 128], F32, kind="ExternalInput").ap()
    onesd = nc.dram_tensor("onesd", [128, 128], F32, kind="ExternalInput").ap()
    qrot_o = nc.dram_tensor("qrot", [512, TOH], F32, kind="ExternalOutput").ap()
    krot_o = nc.dram_tensor("krot", [512, TOH], F32, kind="ExternalOutput").ap()
    vtok_o = nc.dram_tensor("vtok", [TOH, 512], F32, kind="ExternalOutput").ap()
    qkn2_o = nc.dram_tensor("qkn2", [16, TOH], F32, kind="ExternalOutput").ap()
    from contextlib import ExitStack
    with ExitStack() as ctx:
        def sb(name, shape, dt=F32):
            return ctx.enter_context(nc.sbuf_tensor(name, shape, dt))
        def psum(name, shape):
            return ctx.enter_context(nc.psum_tensor(name, shape, F32))
        ht = sb("htH", [128, D])
        hT = sb("hTH", [128, 8 * 128], F32R)
        wq = sb("wqH", [128, 12 * 1024], F32R)
        identt = sb("identtH", [128, 128])
        r64 = sb("r64H", [64, 64], F32R)
        qs = sb("qsH", [128, 1]); ks = sb("ksH", [128, 1])
        tab = sb("tabH", [128, 4 * 128], F32R)
        qkt = sb("qktH", [64, 16 * 128])
        sqt = sb("sqtH", [128, 128], F32R)
        nrow = sb("nrowH", [1, 16 * 128]); rrow = sb("rrowH", [1, 16 * 128], F32R)
        nsc = sb("nscH", [1, 16 * 128])
        qhat = sb("qhatH", [128, 128], F32R)
        t1t = sb("t1tH", [128, 128]); outt = sb("outtH", [128, 128])
        vst = sb("vstH", [128, 512])
        onec = sb("onecH", [128, 1], F32R)
        onerow = sb("onerowH", [1, 64], F32R)
        cget = sb("constsH", [128, 8])
        ps = psum("psH", [128, 128])
        ps_n = psum("psHn", [1, 128])
        ps_bc = psum("psHbc", [128, 128])
        ps_rot = psum("psHrot", [128, 128])
        ps_tr = psum("psHtr", [128, 128])
        S = ctx.enter_context(nc.semaphore("S"))
        ctx.enter_context(nc.allow_low_precision(reason="f32r staging"))
        sq = Seq(); cc = Consts(nc, sq, cget)
        sq.dma("sync", identt[:, :], ident)
        sq.dma("sync", r64[:, :], r64T.bitcast(F32R))
        sq.dma("sync", qs[:, :], qs2); sq.dma("sync", ks[:, :], ks2)
        sq.dma("sync", onec[:, :], onesd[:, 0:1].bitcast(F32R), allow_slow_non_contiguous=True)
        sq.dma("sync", onerow[0:1, :], onesd[0:1, 0:64].bitcast(F32R))
        for ob in range(12):
            sq.dma("sync", wq[:, ob * 1024:(ob + 1) * 1024], wqkvR[ob].bitcast(F32R))
        NTT = TOH // 128
        for it in range(NTT):
            u0 = it * 128
            if u0 + 128 <= 512:
                sq.dma("sync", ht[:, 0:512], xihalo[u0 + 1:u0 + 129, 0:512])
                sq.dma("sync", ht[:, 512:D], xihalo[u0:u0 + 128, 512:D])
            elif u0 == 512:
                sq.dma("sync", ht[:, 0:512], xi[0:128, 0:512])
                sq.dma("sync", ht[0:1, 512:D], xihalo[512:513, 512:D])
                sq.dma("sync", ht[1:128, 512:D], xi[0:127, 512:D])
            else:
                o0 = u0 - 512
                sq.dma("sync", ht[:, 0:512], xi[o0:o0 + 128, 0:512])
                sq.dma("sync", ht[:, 512:D], xi[o0 - 1:o0 + 127, 512:D])
            for c in range(8):
                src_ = ht[:, c * 128:(c + 1) * 128]
                dst_ = hT[:, c * 128:(c + 1) * 128]
                sq.emit("tensor", lambda t, src_=src_: t.transpose(ps_tr[:, :], src_, identt[:, :]))
                sq.emit("vector", lambda v, dst_=dst_: v.tensor_copy(dst_, ps_tr[:, :]))
            for vt_ in range(4):
                ob = 8 + vt_
                for c in range(8):
                    lh = hT[:, c * 128:(c + 1) * 128]
                    rh = wq[:, ob * 1024 + c * 128:ob * 1024 + (c + 1) * 128]
                    sq.emit("tensor", lambda t, lh=lh, rh=rh, c=c: t.matmul(ps[:, :], lh, rh, start=(c == 0), stop=(c == 7)))
                sl = slice(vt_ * 128, (vt_ + 1) * 128)
                sq.emit("vector", lambda v, sl=sl: v.tensor_copy(vst[:, sl], ps[:, :]))
            sq.dma("gpsimd", vtok_o[u0:u0 + 128, :], vst[:, :])
            for hh in range(16):
                ob2 = hh // 2
                c0 = (hh % 2) * 64
                for c in range(8):
                    lh = wq[:, ob2 * 1024 + c * 128 + c0: ob2 * 1024 + c * 128 + c0 + 64]
                    rh = hT[:, c * 128:(c + 1) * 128]
                    sq.emit("tensor", lambda t, lh=lh, rh=rh, c=c: t.matmul(ps[0:64, 0:128], lh, rh, start=(c == 0), stop=(c == 7)))
                dst_ = qkt[0:64, hh * 128:(hh + 1) * 128]
                sq.emit("vector", lambda v, dst_=dst_: v.tensor_copy(dst_, ps[0:64, 0:128]))
            for hh in range(16):
                src_ = qkt[0:64, hh * 128:(hh + 1) * 128]
                sq.emit("vector", lambda v, src_=src_: v.scalar_tensor_tensor(sqt[0:64, :], src_, 1.0, src_, ALU.mult, ALU.mult))
                sq.emit("tensor", lambda t: t.matmul(ps_n[0:1, :], onec[0:64, :], sqt[0:64, :], start=True, stop=True))
                nsl = nrow[0:1, hh * 128:(hh + 1) * 128]
                sq.emit("vector", lambda v, nsl=nsl: v.tensor_copy(nsl, ps_n[0:1, :]))
            sq.emit("vector", lambda v: v.tensor_scalar(nrow[0:1, :], nrow[0:1, :], 1e-24, None, ALU.max))
            sq.emit("scalar", lambda s: s.activation(rrow[0:1, :], nrow[0:1, :], AF.Ln))
            sq.emit("scalar", lambda s: s.activation(rrow[0:1, :], rrow[0:1, :], AF.Exp, scale=-0.5))
            sq.emit("vector", lambda v: v.tensor_tensor(nsc[0:1, :], rrow[0:1, :], rrow[0:1, :], ALU.mult))
            sq.emit("vector", lambda v: v.tensor_tensor(nsc[0:1, :], nsc[0:1, :], nrow[0:1, :], ALU.mult))
            sq.emit("vector", lambda v: v.tensor_scalar(nsc[0:1, :], nsc[0:1, :], -0.5, 1.5, ALU.mult, ALU.add))
            sq.emit("vector", lambda v: v.tensor_tensor(rrow[0:1, :], rrow[0:1, :], nsc[0:1, :], ALU.mult))
            for tb in range(4):
                sq.dma("sync", tab[:, tb * 128:(tb + 1) * 128], tabs[tb, :, u0:u0 + 128].bitcast(F32R))
            for hh in range(16):
                is_q = hh < 8
                scol = qs if is_q else ks
                Asl = tab[0:64, 0:128] if is_q else tab[0:64, 256:384]
                Bsl = tab[0:64, 128:256] if is_q else tab[0:64, 384:512]
                src_ = qkt[0:64, hh * 128:(hh + 1) * 128]
                rsl = rrow[0:1, hh * 128:(hh + 1) * 128]
                sq.emit("tensor", lambda t, rsl=rsl: t.matmul(ps_bc[0:64, :], onerow[0:1, :], rsl, start=True, stop=True))
                sq.emit("vector", lambda v, src_=src_, scol=scol: v.scalar_tensor_tensor(qhat[0:64, :], src_, scol[0:64, 0:1], ps_bc[0:64, 0:128], ALU.mult, ALU.mult))
                sq.emit("vector", lambda v: v.scalar_tensor_tensor(sqt[0:64, :], qhat[0:64, :], 1.0, qhat[0:64, :], ALU.mult, ALU.mult))
                sq.emit("tensor", lambda t: t.matmul(ps_n[0:1, :], onec[0:64, :], sqt[0:64, :], start=True, stop=True))
                sq.emit("vector", lambda v: v.tensor_copy(nsc[0:1, 0:128], ps_n[0:1, :]))
                sq.dma("gpsimd", qkn2_o[hh:hh + 1, u0:u0 + 128], nsc[0:1, 0:128])
                sq.emit("tensor", lambda t: t.matmul(ps_rot[0:64, :], r64[:, :], qhat[0:64, :], start=True, stop=True))
                sq.emit("vector", lambda v, Asl=Asl: v.tensor_tensor(t1t[0:64, :], qhat[0:64, :], Asl, ALU.mult))
                sq.emit("vector", lambda v, Bsl=Bsl: v.tensor_tensor(outt[0:64, :], ps_rot[0:64, 0:128], Bsl, ALU.mult))
                sq.emit("vector", lambda v: v.tensor_tensor(outt[0:64, :], outt[0:64, :], t1t[0:64, :], ALU.add))
                if is_q:
                    sq.dma("gpsimd", qrot_o[hh * 64:(hh + 1) * 64, u0:u0 + 128], outt[0:64, :])
                else:
                    sq.dma("gpsimd", krot_o[(hh - 8) * 64:(hh - 7) * 64, u0:u0 + 128], outt[0:64, :])
        sq.build(nc, S)
    return nc


    return nc


H_NBLK = 8


def build_stage_h2():
    nc = bass.Bass("TRN2", target_bir_lowering=False, debug=False)
    qrot = nc.dram_tensor("qrot", [512, TOH], F32, kind="ExternalInput").ap()
    krot = nc.dram_tensor("krot", [512, TOH], F32, kind="ExternalInput").ap()
    vtokR = nc.dram_tensor("vtokR", [128, TOH // 128, 512], F32, kind="ExternalInput").ap()
    masksH = nc.dram_tensor("masksH", [2, H_NBLK, 128, 512], BF16, kind="ExternalInput").ap()
    wod = nc.dram_tensor("wo", [512, D], F32, kind="ExternalInput").ap()
    qkn2 = nc.dram_tensor("qkn2", [16, TOH], F32, kind="ExternalInput").ap()
    xi = nc.dram_tensor("xi", [TSH, D], F32, kind="ExternalInput").ap()
    onesd = nc.dram_tensor("onesd", [128, 128], F32, kind="ExternalInput").ap()
    onesr = nc.dram_tensor("onesr", [1, TOH], F32, kind="ExternalInput").ap()
    xiout_o = nc.dram_tensor("xiout", [TSH, D], F32, kind="ExternalOutput").ap()
    from contextlib import ExitStack
    with ExitStack() as ctx:
        def sb(name, shape, dt=F32):
            return ctx.enter_context(nc.sbuf_tensor(name, shape, dt))
        def psum(name, shape):
            return ctx.enter_context(nc.psum_tensor(name, shape, F32))
        kh = sb("khH2", [65, TOH], F32R)
        qh = sb("qhH2", [65, TSH], F32R)
        vtk = sb("vtkH2", [128, (TOH // 128) * 512], BF16)
        mask_t = sb("maskH2", [128, 2 * H_NBLK * 512], BF16)
        sqk = sb("sqkH2", [64, TOH], F32R)
        krow = sb("krowH2", [1, TOH]); qrow = sb("qrowH2", [1, TSH])
        rows = sb("rowsH2", [1, 8]); mrow = sb("mrowH2", [1, TSH])
        pte = sb("pteH2", [128, 512]); pt = sb("ptH2", [128, 512], BF16)
        ofm = sb("ofmH2", [64, 8 * TSH], F32R)
        oev = sb("oevH2", [64, 512])
        rrow = sb("rrowH2", [1, 512], F32R)
        wo = sb("woH2", [64, 8 * D], F32R)
        xt = sb("xtH2", [128, D]); outt = sb("outtH2", [128, D])
        onec = sb("onecH2", [128, 1], F32R)
        onecb = sb("onecbH2", [128, 1], BF16)
        onerow = sb("onerowH2", [1, 64], F32R)
        cget = sb("constsH2", [128, 8])
        ps_sc = psum("ps_scH2", [128, 512])
        ps_o = psum("ps_oH2", [64, 512])
        ps_d = psum("ps_dH2", [1, 512])
        ps_bc = psum("ps_bcH2", [64, 512])
        ps_w = psum("ps_wH2", [128, 512])
        S = ctx.enter_context(nc.semaphore("S"))
        ctx.enter_context(nc.allow_low_precision(reason="bf16 P tiles"))
        sq = Seq(); cc = Consts(nc, sq, cget)
        sq.dma("sync", onec[:, :], onesd[:, 0:1].bitcast(F32R), allow_slow_non_contiguous=True)
        sq.dma("gpsimd", onecb[:, :], onesd[:, 0:1], allow_slow_non_contiguous=True)
        sq.dma("sync", onerow[0:1, :], onesd[0:1, 0:64].bitcast(F32R))
        for s_ in range(2):
            for b_ in range(H_NBLK):
                dst_ = mask_t[:, (s_ * H_NBLK + b_) * 512:(s_ * H_NBLK + b_ + 1) * 512]
                sq.dma("sync", dst_, masksH[s_, b_])
        for c in range(8):
            sq.dma("sync", wo[:, c * D:(c + 1) * D], wod[c * 64:(c + 1) * 64, :].bitcast(F32R))
        for tt_ in range(TOH // 128):
            sq.dma("gpsimd", vtk[:, tt_ * 512:(tt_ + 1) * 512], vtokR[:, tt_, :])
        for h in range(8):
            p_, hb = h // 2, h % 2
            r0 = hb * 64
            sq.dma("sync", kh[0:64, :], krot[p_ * 128 + r0:p_ * 128 + r0 + 64, :].bitcast(F32R))
            sq.dma("sync", kh[64:65, :], onesr.bitcast(F32R))
            sq.dma("sync", qh[0:64, :], qrot[p_ * 128 + r0:p_ * 128 + r0 + 64, 512:].bitcast(F32R))
            sq.dma("sync", krow[0:1, :], qkn2[8 + h:9 + h, :])
            sq.dma("sync", qrow[0:1, :], qkn2[h:h + 1, 512:])
            for w in range(4):
                ksl = krow[0:1, w * 512:w * 512 + 1024]
                rsl_ = rows[0:1, w:w + 1]
                sq.emit("vector", lambda v, ksl=ksl, rsl_=rsl_: v.tensor_reduce(rsl_, ksl, mybir.AxisListType.X, ALU.max))
                qsl_ = qrow[0:1, w * 512:(w + 1) * 512]
                msl_ = mrow[0:1, w * 512:(w + 1) * 512]
                sq.emit("vector", lambda v, qsl_=qsl_, msl_=msl_, rsl_=rsl_: v.tensor_scalar(msl_, qsl_, rsl_, 64.0, ALU.mult, ALU.mult))
            sq.emit("scalar", lambda s: s.activation(mrow[0:1, :], mrow[0:1, :], AF.Sqrt))
            sq.emit("vector", lambda v: v.tensor_scalar(qh[64:65, :], mrow[0:1, :], -1.0, None, ALU.mult))
            for w in range(4):
                mset = 0 if w == 0 else 1
                for b_ in range(H_NBLK):
                    tk0 = w * 512 + b_ * 128
                    lh = kh[:, tk0:tk0 + 128]
                    rh = qh[:, w * 512:(w + 1) * 512]
                    msl = mask_t[:, (mset * H_NBLK + b_) * 512:(mset * H_NBLK + b_ + 1) * 512]
                    sq.emit("tensor", lambda t, lh=lh, rh=rh: t.matmul(ps_sc[:, :], lh, rh, start=True, stop=True))
                    sq.emit("scalar", lambda s: s.activation(pte[:, :], ps_sc[:, :], AF.Exp))
                    sq.emit("vector", lambda v, msl=msl: v.tensor_tensor(pt[:, :], pte[:, :], msl, ALU.mult))
                    first, last = b_ == 0, b_ == H_NBLK - 1
                    vsl = vtk[:, (w * 4 + b_) * 512 + h * 64:(w * 4 + b_) * 512 + (h + 1) * 64]
                    sq.emit("tensor", lambda t, vsl=vsl, first=first, last=last: t.matmul(ps_o[:, :], vsl, pt[:, :], start=first, stop=last))
                    sq.emit("tensor", lambda t, first=first, last=last: t.matmul(ps_d[0:1, :], onecb[:, :], pt[:, :], start=first, stop=last))
                sq.emit("vector", lambda v: v.tensor_copy(oev[:, :], ps_o[:, :]))
                sq.emit("vector", lambda v: v.reciprocal(rrow[0:1, :], ps_d[0:1, :]))
                sq.emit("tensor", lambda t: t.matmul(ps_bc[0:64, :], onerow[0:1, :], rrow[0:1, :], start=True, stop=True))
                dst_ = ofm[0:64, h * TSH + w * 512:h * TSH + (w + 1) * 512]
                sq.emit("vector", lambda v, dst_=dst_: v.tensor_tensor(dst_, oev[:, :], ps_bc[0:64, :], ALU.mult))
        for it in range(TSH // 128):
            tok0 = it * 128
            sq.dma("sync", xt[:, :], xi[tok0:tok0 + 128, :])
            for oh in range(2):
                for c in range(8):
                    lh = ofm[0:64, c * TSH + tok0:c * TSH + tok0 + 128]
                    rh = wo[0:64, c * D + oh * 512:c * D + (oh + 1) * 512]
                    sq.emit("tensor", lambda t, lh=lh, rh=rh, c=c: t.matmul(ps_w[:, :], lh, rh, start=(c == 0), stop=(c == 7)))
                sl = slice(oh * 512, (oh + 1) * 512)
                sq.emit("vector", lambda v, sl=sl: v.tensor_tensor(outt[:, sl], xt[:, sl], ps_w[:, :], ALU.add))
            sq.dma("gpsimd", xiout_o[tok0:tok0 + 128, :], outt[:, :])
        sq.build(nc, S)
    return nc


HID2 = 5460
HID = 2730


def build_stage_i(final):
    nc = bass.Bass("TRN2", target_bir_lowering=False, debug=False)
    xi = nc.dram_tensor("xi", [TSH, D], F32, kind="ExternalInput").ap()
    xiprev = nc.dram_tensor("xiprev", [1, D], F32, kind="ExternalInput").ap()
    w1d = nc.dram_tensor("w1", [D, HID2], F32, kind="ExternalInput").ap()
    b1d = nc.dram_tensor("b1row", [1, HID2], F32, kind="ExternalInput").ap()
    w2d = nc.dram_tensor("w2", [HID, D], F32, kind="ExternalInput").ap()
    b2d = nc.dram_tensor("b2row", [1, D], F32, kind="ExternalInput").ap()
    lngd = nc.dram_tensor("lng", [1, D], F32, kind="ExternalInput").ap()
    lnbd = nc.dram_tensor("lnb", [1, D], F32, kind="ExternalInput").ap()
    ident = nc.dram_tensor("ident", [128, 128], F32, kind="ExternalInput").ap()
    if final:
        x3d = nc.dram_tensor("x3", [TSH, D], F32, kind="ExternalInput").ap()
    xiout_o = nc.dram_tensor("xiout", [TSH, D], F32, kind="ExternalOutput").ap()
    TS = 512
    NW1 = ceil_div(HID2, 512)
    NC2 = ceil_div(HID, 128)
    from contextlib import ExitStack
    with ExitStack() as ctx:
        def sb(name, shape, dt=F32):
            return ctx.enter_context(nc.sbuf_tensor(name, shape, dt))
        def psum(name, shape):
            return ctx.enter_context(nc.psum_tensor(name, shape, F32))
        ht = sb("htI", [128, D]); xnt = sb("xntI", [128, D])
        dump = sb("dumpI", [128, D]); col4 = sb("col4I", [128, 4]); nr = sb("nrI", [128, 3])
        identt = sb("identtI", [128, 128])
        gbc = sb("gbcI", [128, D]); bbc = sb("bbcI", [128, D])
        rowd = sb("rowdI", [1, D], F32R)
        mrow1 = sb("mrow1I", [1, 128])
        xT = sb("xTI", [128, 8 * TS], F32R)
        ggT = sb("ggTI", [128, 4 * NC2 * 128], F32R)
        w1b = sb("w1bI", [128, 8 * 512], F32R)
        b1b = sb("b1bI", [1, 512], F32R)
        w2b = sb("w2bI", [128, 512], F32R)
        gel = sb("gelI", [128, 4 * 512])
        onerow = sb("onerowI", [1, 128], F32R)
        osl = sb("oslI", [128, 4 * D])
        cget = sb("constsI", [128, 8])
        ps = psum("psI", [128, 512]); ps_tr = psum("psI2", [128, 128])
        ps4 = psum("ps4I", [128, 4 * 512])
        S = ctx.enter_context(nc.semaphore("S"))
        ctx.enter_context(nc.allow_low_precision(reason="f32r staging"))
        sq = Seq(); cc = Consts(nc, sq, cget)
        scr = {"dump": dump, "col4": col4, "nr": nr}
        sq.dma("sync", identt[:, :], ident)
        sq.emit("vector", lambda v: v.memset(mrow1[0:1, :], 1.0))
        sq.emit("vector", lambda v: v.tensor_copy(onerow[0:1, :], mrow1[0:1, :]))
        sq.dma("sync", rowd[0:1, :], lngd.bitcast(F32R))
        for j in range(2):
            sq.emit("tensor", lambda t, j=j: t.matmul(ps[:, :], onerow[0:1, :].bitcast(F32R), rowd[0:1, j * 512:(j + 1) * 512], start=True, stop=True))
            sq.emit("vector", lambda v, j=j: v.tensor_copy(gbc[:, j * 512:(j + 1) * 512], ps[:, :]))
        sq.dma("sync", rowd[0:1, :], lnbd.bitcast(F32R))
        for j in range(2):
            sq.emit("tensor", lambda t, j=j: t.matmul(ps[:, :], onerow[0:1, :].bitcast(F32R), rowd[0:1, j * 512:(j + 1) * 512], start=True, stop=True))
            sq.emit("vector", lambda v, j=j: v.tensor_copy(bbc[:, j * 512:(j + 1) * 512], ps[:, :]))
        for isl in range(TSH // TS):
            st0 = isl * TS
            for jt in range(TS // 128):
                tok0 = st0 + jt * 128
                sq.dma("sync", ht[:, 0:512], xi[tok0:tok0 + 128, 0:512])
                if tok0 == 0:
                    sq.dma("sync", ht[0:1, 512:D], xiprev[0:1, 512:D])
                    sq.dma("sync", ht[1:128, 512:D], xi[0:127, 512:D])
                else:
                    sq.dma("sync", ht[:, 512:D], xi[tok0 - 1:tok0 + 127, 512:D])
                emit_ln_token_tile(sq, nc, ht[:, :], xnt[:, :], scr, cc)
                sq.emit("vector", lambda v: v.tensor_tensor(xnt[:, :], xnt[:, :], gbc[:, :], ALU.mult))
                sq.emit("vector", lambda v: v.tensor_tensor(xnt[:, :], xnt[:, :], bbc[:, :], ALU.add))
                for c in range(8):
                    src_ = xnt[:, c * 128:(c + 1) * 128]
                    dst_ = xT[:, c * TS + jt * 128: c * TS + (jt + 1) * 128]
                    sq.emit("tensor", lambda t, src_=src_: t.transpose(ps_tr[:, :], src_, identt[:, :]))
                    sq.emit("vector", lambda v, dst_=dst_: v.tensor_copy(dst_, ps_tr[:, :]))
            NOSP = ceil_div(HID, 512)
            for osp in range(NOSP):
                o0 = osp * 512
                ow = min(512, HID - o0)
                for c in range(8):
                    sq.dma("sync", w1b[:, c * 512:c * 512 + ow], w1d[c * 128:(c + 1) * 128, HID + o0:HID + o0 + ow].bitcast(F32R))
                sq.dma("sync", b1b[0:1, 0:ow], b1d[0:1, HID + o0:HID + o0 + ow].bitcast(F32R))
                for jt in range(TS // 128):
                    for c in range(8):
                        lh = xT[:, c * TS + jt * 128: c * TS + (jt + 1) * 128]
                        rh = w1b[:, c * 512:c * 512 + ow]
                        sq.emit("tensor", lambda t, lh=lh, rh=rh, c=c, ow=ow: t.matmul(ps[:, 0:ow], lh, rh, start=(c == 0), stop=False))
                    sq.emit("tensor", lambda t, ow=ow: t.matmul(ps[:, 0:ow], onerow[0:1, :], b1b[0:1, 0:ow], start=False, stop=True))
                    dst_ = gel[:, jt * 512: jt * 512 + ow]
                    sq.emit("scalar", lambda s, dst_=dst_, ow=ow: s.activation(dst_, ps[:, 0:ow], AF.Gelu))
                for c in range(8):
                    sq.dma("sync", w1b[:, c * 512:c * 512 + ow], w1d[c * 128:(c + 1) * 128, o0:o0 + ow].bitcast(F32R))
                sq.dma("sync", b1b[0:1, 0:ow], b1d[0:1, o0:o0 + ow].bitcast(F32R))
                for jt in range(TS // 128):
                    for c in range(8):
                        lh = xT[:, c * TS + jt * 128: c * TS + (jt + 1) * 128]
                        rh = w1b[:, c * 512:c * 512 + ow]
                        sq.emit("tensor", lambda t, lh=lh, rh=rh, c=c, ow=ow: t.matmul(ps[:, 0:ow], lh, rh, start=(c == 0), stop=False))
                    sq.emit("tensor", lambda t, ow=ow: t.matmul(ps[:, 0:ow], onerow[0:1, :], b1b[0:1, 0:ow], start=False, stop=True))
                    gsl = gel[:, jt * 512: jt * 512 + ow]
                    sq.emit("vector", lambda v, gsl=gsl, ow=ow: v.tensor_tensor(gsl, gsl, ps[:, 0:ow], ALU.mult))
                    for cc2 in range(ceil_div(ow, 128)):
                        i0g = o0 + cc2 * 128
                        iw = min(128, HID - i0g)
                        src_ = gel[:, jt * 512 + cc2 * 128: jt * 512 + cc2 * 128 + iw]
                        cch = i0g // 128
                        dst_ = ggT[0:iw, (jt * NC2 + cch) * 128:(jt * NC2 + cch) * 128 + 128]
                        sq.emit("tensor", lambda t, src_=src_, iw=iw: t.transpose(ps_tr[0:iw, :], src_, identt[:, :]))
                        sq.emit("vector", lambda v, dst_=dst_, iw=iw: v.tensor_copy(dst_, ps_tr[0:iw, :]))
            for oh in range(2):
                o0 = oh * 512
                for c in range(NC2):
                    i0 = c * 128
                    iw = min(128, HID - i0)
                    sq.dma("sync", w2b[0:iw, :], w2d[i0:i0 + iw, o0:o0 + 512].bitcast(F32R))
                    for jt in range(TS // 128):
                        lh = ggT[0:iw, (jt * NC2 + c) * 128:(jt * NC2 + c) * 128 + 128]
                        dst_ = ps4[:, jt * 512:(jt + 1) * 512]
                        first = c == 0
                        sq.emit("tensor", lambda t, lh=lh, iw=iw, first=first, dst_=dst_: t.matmul(dst_, lh, w2b[0:iw, :], start=first, stop=False))
                sq.dma("sync", b1b[0:1, 0:512], b2d[0:1, o0:o0 + 512].bitcast(F32R))
                for jt in range(TS // 128):
                    dst_ = ps4[:, jt * 512:(jt + 1) * 512]
                    sq.emit("tensor", lambda t, dst_=dst_: t.matmul(dst_, onerow[0:1, :], b1b[0:1, 0:512], start=False, stop=True))
                    osl_ = osl[:, jt * D + o0: jt * D + o0 + 512]
                    sq.emit("vector", lambda v, dst_=dst_, osl_=osl_: v.tensor_copy(osl_, dst_))
            for jt in range(TS // 128):
                tok0 = st0 + jt * 128
                sq.dma("sync", ht[:, :], xi[tok0:tok0 + 128, :])
                src_ = osl[:, jt * D:(jt + 1) * D]
                sq.emit("vector", lambda v, src_=src_: v.tensor_tensor(xnt[:, :], ht[:, :], src_, ALU.add))
                if final:
                    sq.dma("sync", ht[:, :], x3d[tok0:tok0 + 128, :])
                    src2_ = osl[:, jt * D:(jt + 1) * D]
                    sq.emit("vector", lambda v, src2_=src2_: v.tensor_copy(src2_, xnt[:, :]))
                    emit_ln_token_tile(sq, nc, src2_, xnt[:, :], scr, cc)
                    sq.emit("vector", lambda v: v.tensor_tensor(xnt[:, :], xnt[:, :], ht[:, :], ALU.add))
                sq.dma("gpsimd", xiout_o[tok0:tok0 + 128, :], xnt[:, :])
        sq.build(nc, S)
    return nc


_IDENT = np.eye(128, dtype=np.float32)
_IDENT64 = np.eye(64, dtype=np.float32)
_ONESD = np.ones((128, 128), dtype=np.float32)


def host_rotary_tables():
    n = T
    DH, WIN = 64, 512
    half = np.arange(0, DH, 2, dtype=np.float64)
    inv_freq = 1.0 / 10000.0 ** (half / DH)
    t = np.arange(n, dtype=np.float64)
    freqs = t[:, None] * inv_freq[None, :]
    freqs = np.concatenate([freqs, freqs], axis=-1)
    base = (half + 0.4 * DH) / (1.4 * DH)
    power = (t - n // 2) / float(WIN // 2)
    scale = base[None, :] ** power[:, None]
    scale = np.concatenate([scale, scale], axis=-1)
    cos, sin = np.cos(freqs), np.sin(freqs)
    Aq = (cos * scale * 8.0).T
    Bq = (sin * scale * 8.0).T
    Ak = (cos / scale).T
    Bk = (sin / scale).T
    out = np.zeros((4, 128, n + 512), dtype=np.float32)
    for i, tab in enumerate((Aq, Bq, Ak, Bk)):
        dup = np.concatenate([tab, tab], axis=0)
        out[i, :, 512:] = dup
        out[i, :, :512] = dup[:, 0:1]
    return out


def host_r64T():
    R = np.zeros((64, 64), np.float32)
    for d in range(32):
        R[d, d + 32] = -1.0
        R[d + 32, d] = 1.0
    return np.ascontiguousarray(R.T)


def host_masks_f(half):
    m = np.zeros((F_MTOT, 128, 512), np.float32)
    i = np.arange(128)[:, None]
    j = np.arange(512)[None, :]
    for jt in range(4):
        tqbase = half * TSH + jt * 512
        for tkb in range(F_NBLK[jt]):
            m[F_MOFF[jt] + tkb] = ((tqbase + j) >= (tkb * 128 + i)).astype(np.float32)
    return m


def host_masks_h(half):
    import ml_dtypes
    m = np.zeros((2, H_NBLK, 128, 512), np.float32)
    i = np.arange(128)[:, None]
    j = np.arange(512)[None, :]
    for b_ in range(H_NBLK):
        diff = j - (b_ * 128 + i - 512)
        base = ((diff >= 0) & (diff <= 512)).astype(np.float32)
        m[1, b_] = base
        if half == 0 and b_ < 4:
            m[0, b_] = 0.0
        else:
            m[0, b_] = base
    return m.astype(ml_dtypes.bfloat16)


def run_stage_a(inputs):
    x = inputs["L0_attn_x"]
    nw = np.ascontiguousarray(inputs["L0_attn_norm_weight"].reshape(1, D))
    wkT = np.ascontiguousarray(inputs["L0_attn_Wk"].T)
    wqT = np.ascontiguousarray(inputs["L0_attn_Wq"].T)
    wvT = np.ascontiguousarray(inputs["L0_attn_Wv"].T)
    eta = np.ascontiguousarray(inputs["L0_attn_eta"].reshape(64, 1))
    nc = build_stage_a()
    in_maps = []
    for c in range(NCORES):
        b, h = c // 2, c % 2
        xs = np.ascontiguousarray(x[b, h * TSH:(h + 1) * TSH, :])
        in_maps.append({"x": xs, "nw": nw, "wkT": wkT, "wqT": wqT, "wvT": wvT,
                        "eta": eta, "ident": _IDENT})
    res = run_bass_kernel_spmd(nc, in_maps, core_ids=list(range(NCORES)))
    return res.results


def _blkR(wt, nc_):
    i_, o_ = wt.shape
    return np.ascontiguousarray(
        wt.reshape(nc_, 128, o_ // 128, 128).transpose(2, 1, 0, 3).reshape(o_ // 128, 128, nc_ * 128))


def _cc(a):
    return np.ascontiguousarray(a, dtype=np.float32)


def kernel(**inputs):
    ncpu = np
    cores = list(range(NCORES))
    x = inputs["L0_attn_x"]

    nwr = _cc(inputs["L0_attn_norm_weight"].reshape(1, D))
    wkT = _cc(inputs["L0_attn_Wk"].T); wqT = _cc(inputs["L0_attn_Wq"].T)
    wvT = _cc(inputs["L0_attn_Wv"].T)
    eta = _cc(inputs["L0_attn_eta"].reshape(64, 1))
    ncA = build_stage_a()
    imA = []
    for c in cores:
        b, h = c // 2, c % 2
        imA.append({"x": _cc(x[b, h * TSH:(h + 1) * TSH]), "nw": nwr, "wkT": wkT,
                    "wqT": wqT, "wvT": wvT, "eta": eta, "ident": _IDENT})
    rA = run_bass_kernel_spmd(ncA, imA, core_ids=cores).results

    kneg = {}; ekf = {}; vt = {}
    for b in range(B):
        kneg[b] = _cc(np.concatenate([rA[2 * b]["knegfm"], rA[2 * b + 1]["knegfm"]], axis=1))
        ekf[b] = _cc(np.concatenate([rA[2 * b]["ekfm"], rA[2 * b + 1]["ekfm"]], axis=1))
        vt[b] = _cc(np.concatenate([rA[2 * b]["vtok"], rA[2 * b + 1]["vtok"]], axis=0))
    Mst = {b: np.zeros((64, D), np.float32) for b in range(B)}
    SSTEPS = 1024
    for j in range(T // SSTEPS):
        ncS = build_stage_scan(SSTEPS, j * SSTEPS)
        imS = []
        for c in cores:
            b = c % B
            imS.append({"kneg": kneg[b], "ekf": ekf[b], "vtok": vt[b], "Min": Mst[b],
                        "ident": _IDENT, "ident64": _IDENT64, "onesd": _ONESD})
        rS = run_bass_kernel_spmd(ncS, imS, core_ids=cores).results
        for b in range(B):
            Mst[b] = rS[b]["Mout"]

    woutT = _cc(inputs["L0_attn_Wout"].T)
    ncC = build_stage_c()
    imC = []
    for c in cores:
        b, h = c // 2, c % 2
        imC.append({"qfm": rA[c]["qfm"], "M": Mst[b], "woutT": woutT,
                    "xn": rA[c]["xn"], "x": imA[c]["x"]})
    rC = run_bass_kernel_spmd(ncC, imC, core_ids=cores).results

    imD = []
    w1R = _blkR(inputs["L0_ffn_w1_w"].T.astype(np.float32), 8)
    w2R = _blkR(inputs["L0_ffn_w2_w"].T.astype(np.float32), 8)
    h1R = _blkR(inputs["L0_ffn_h1_w"].T.astype(np.float32), 32)
    h2R = _blkR(inputs["L0_ffn_h2_w"].T.astype(np.float32), 32)
    w3T = _cc(inputs["L0_ffn_w3_w"].T)
    ncD = build_stage_d()
    for c in cores:
        imD.append({"x1": rC[c]["x1"], "w1R": w1R, "w2R": w2R, "h1R": h1R, "h2R": h2R,
                    "w3T": w3T, "s1": _cc(inputs["L0_ffn_w1_s"].reshape(-1, 1)),
                    "s2": _cc(inputs["L0_ffn_w2_s"].reshape(-1, 1)),
                    "sh1": _cc(inputs["L0_ffn_h1_s"].reshape(-1, 1)),
                    "sh2": _cc(inputs["L0_ffn_h2_s"].reshape(-1, 1)),
                    "s3row": _cc(inputs["L0_ffn_w3_s"].reshape(1, -1)), "ident": _IDENT})
    rD = run_bass_kernel_spmd(ncD, imD, core_ids=cores).results

    ncE = build_stage_e()
    imE = []
    for c in cores:
        imE.append({"x2": rD[c]["x2"], "wqT": _cc(inputs["L1_attn_wq"].T),
                    "wkT": _cc(inputs["L1_attn_wk"].T), "wvT": _cc(inputs["L1_attn_wv"].T),
                    "ident": _IDENT})
    rE = run_bass_kernel_spmd(ncE, imE, core_ids=cores).results

    ONESR_T = np.ones((1, T), np.float32)
    kfmb = {}; vtokRb = {}
    for b in range(B):
        kfmb[b] = _cc(np.concatenate([rE[2 * b]["kfm"], rE[2 * b + 1]["kfm"]], axis=1))
        vv = np.concatenate([rE[2 * b]["vtok"], rE[2 * b + 1]["vtok"]], axis=0)
        vtokRb[b] = _cc(vv.reshape(T // 128, 128, D).transpose(1, 0, 2))
    ncF = build_stage_f()
    imF = []
    masksF = [host_masks_f(0), host_masks_f(1)]
    for c in cores:
        b, h = c // 2, c % 2
        imF.append({"qfm": rE[c]["qfm"], "kfm": kfmb[b], "vtokR": vtokRb[b],
                    "masks": masksF[h], "onesd": _ONESD, "onesr": ONESR_T})
    rF = run_bass_kernel_spmd(ncF, imF, core_ids=cores).results

    ncG = build_stage_g()
    imG = []
    for c in cores:
        imG.append({"attn": rF[c]["attn"], "woT": _cc(inputs["L1_attn_wo"].T),
                    "x2": rD[c]["x2"], "ident": _IDENT})
    rG = run_bass_kernel_spmd(ncG, imG, core_ids=cores).results

    TABS = host_rotary_tables()
    R64T = host_r64T()
    ONESR_TOH = np.ones((1, TOH), np.float32)
    xi = {c: rG[c]["xi0"] for c in cores}
    x3 = {c: rG[c]["x3"] for c in cores}
    for l in range(2):
        wqkvR = _blkR(_cc(inputs["L1_ffn_wqkv"][l]), 8)
        qs2 = _cc(np.tile(inputs["L1_ffn_q_scale"][l], 2).reshape(128, 1))
        ks2 = _cc(np.tile(inputs["L1_ffn_k_scale"][l], 2).reshape(128, 1))
        wol = _cc(inputs["L1_ffn_wo"][l])
        lngl = _cc(inputs["L1_ffn_ln_g"][l].reshape(1, -1))
        lnbl = _cc(inputs["L1_ffn_ln_b"][l].reshape(1, -1))
        w1l = _cc(inputs["L1_ffn_w1"][l]); b1l = _cc(inputs["L1_ffn_b1"][l].reshape(1, -1))
        w2l = _cc(inputs["L1_ffn_w2"][l]); b2l = _cc(inputs["L1_ffn_b2"][l].reshape(1, -1))
        ncH1 = build_stage_h1()
        im1 = []
        for c in cores:
            b, h = c // 2, c % 2
            if h == 0:
                halo = np.zeros((513, D), np.float32)
            else:
                halo = _cc(xi[c - 1][TSH - 513:TSH])
            tabsl = _cc(TABS[:, :, h * TSH:h * TSH + TOH])
            im1.append({"xi": xi[c], "xihalo": halo, "wqkvR": wqkvR, "qs2": qs2,
                        "ks2": ks2, "r64T": R64T, "tabs": tabsl, "ident": _IDENT,
                        "onesd": _ONESD})
        r1 = run_bass_kernel_spmd(ncH1, im1, core_ids=cores).results
        ncH2 = build_stage_h2()
        im2 = []
        for c in cores:
            b, h = c // 2, c % 2
            vtokR = _cc(r1[c]["vtok"].reshape(TOH // 128, 128, 512).transpose(1, 0, 2))
            im2.append({"qrot": r1[c]["qrot"], "krot": r1[c]["krot"], "vtokR": vtokR,
                        "qkn2": r1[c]["qkn2"], "masksH": host_masks_h(h), "wo": wol,
                        "xi": xi[c], "onesd": _ONESD, "onesr": ONESR_TOH})
        r2 = run_bass_kernel_spmd(ncH2, im2, core_ids=cores).results
        ncI = build_stage_i(l == 1)
        im3 = []
        for c in cores:
            b, h = c // 2, c % 2
            if h == 0:
                prev = np.zeros((1, D), np.float32)
            else:
                prev = _cc(r2[c - 1]["xiout"][-1:])
            m = {"xi": r2[c]["xiout"], "xiprev": prev, "w1": w1l, "b1row": b1l,
                 "w2": w2l, "b2row": b2l, "lng": lngl, "lnb": lnbl, "ident": _IDENT}
            if l == 1:
                m["x3"] = x3[c]
            im3.append(m)
        r3 = run_bass_kernel_spmd(ncI, im3, core_ids=cores).results
        xi = {c: r3[c]["xiout"] for c in cores}

    out = np.empty((B, T, D), np.float32)
    for c in cores:
        b, h = c // 2, c % 2
        out[b, h * TSH:(h + 1) * TSH] = xi[c]
    return out


# version 02 (final)
# speedup vs baseline: 1.6161x; 1.6161x over previous
import numpy as np
import concourse.bass as bass
import concourse.mybir as mybir
from concourse.bass_utils import run_bass_kernel_spmd

F32 = mybir.dt.float32
F32R = mybir.dt.float32r
BF16 = mybir.dt.bfloat16
ALU = mybir.AluOpType
AF = mybir.ActivationFunctionType

B, T, D = 4, 4096, 1024
NCORES = 8
TSH = T // 2
EPS_LN = 1e-5
EPS_RMS = 1e-6
ETA_MIN = 1e-6


class Seq:

    def __init__(self):
        self.items = []
        self.total = 0

    def emit(self, eng_name, fn, inc=1):
        self.items.append((eng_name, fn, inc, self.total))
        self.total += inc

    def dma(self, eng_name, out, in_, **kw):
        self.emit(eng_name, lambda e: e.dma_start(out=out, in_=in_, **kw), inc=16)

    BATCHABLE = ("tensor", "sync", "gpsimd")

    def build(self, nc, sem):
        items2 = []
        prev_eng = None
        for eng_name, fn, inc, wait in self.items:
            if prev_eng == eng_name and eng_name in self.BATCHABLE:
                items2.append((eng_name, fn, inc, None))
            else:
                items2.append((eng_name, fn, inc, wait))
            prev_eng = eng_name
        by_eng = {}
        for it in items2:
            by_eng.setdefault(it[0], []).append(it)
        total = self.total
        with nc.Block() as block:
            done_final = [False]
            for eng_name, eitems in by_eng.items():
                def mk(eitems, is_sync):
                    def emit(eng):
                        for _, fn, inc, wait in eitems:
                            if wait is not None and wait > 0:
                                eng.wait_ge(sem, wait)
                            fn(eng).then_inc(sem, inc)
                        if is_sync:
                            eng.wait_ge(sem, total)
                    return emit
                is_sync = eng_name == "sync"
                done_final[0] = done_final[0] or is_sync
                getattr(block, eng_name)(mk(eitems, is_sync))
            if not done_final[0]:
                def fin(eng):
                    eng.wait_ge(sem, total)
                block.sync(fin)


def ceil_div(a, b):
    return (a + b - 1) // b


class Consts:

    def __init__(self, nc, sq, tile, name="consts"):
        self.tile = tile
        self.sq = sq
        self.vals = {}

    def cst(self, v, rows=128):
        v = float(v)
        if v not in self.vals:
            j = len(self.vals)
            self.vals[v] = j
            self.sq.emit("vector", lambda ve, j=j, v=v: ve.memset(self.tile[:, j:j + 1], v))
        j = self.vals[v]
        return self.tile[0:rows, j:j + 1]


def emit_ln_token_tile(sq, nc, x, xn, scr, cc, n_d=D, eps=EPS_LN, P=128):
    dump, col = scr["dump"], scr["col4"]
    sq.emit("vector", lambda v: v.tensor_reduce(col[0:P, 0:1], x, mybir.AxisListType.X, ALU.add))
    sq.emit("vector", lambda v: v.scalar_tensor_tensor(dump[0:P, :n_d], x, 1.0, x, ALU.mult, ALU.mult, accum_out=col[0:P, 1:2]))
    sq.emit("vector", lambda v: v.tensor_scalar(col[0:P, 2:4], col[0:P, 0:2], 1.0 / n_d, None, ALU.mult))
    sq.emit("vector", lambda v: v.scalar_tensor_tensor(col[0:P, 1:2], col[0:P, 2:3], col[0:P, 2:3], col[0:P, 3:4], ALU.mult, ALU.subtract))
    emit_rsqrt(sq, col[0:P, 1:2], col[0:P, 1:2], scr, cc, eps=eps, P=P, neg_in=True,
               t0=col[0:P, 0:1])
    sq.emit("vector", lambda v: v.tensor_scalar(xn, x, col[0:P, 2:3], col[0:P, 1:2], ALU.subtract, ALU.mult))


def emit_rsqrt(sq, out, var_in, scr, cc, eps=0.0, P=128, neg_in=False, t0=None):
    nr = scr["nr"]
    if t0 is None:
        t0 = nr[0:P, 2:3]
    sgn = -1.0 if neg_in else 1.0
    sq.emit("vector", lambda v: v.tensor_scalar(nr[0:P, 0:1], var_in, sgn, float(eps), ALU.mult, ALU.add))
    sq.emit("scalar", lambda s: s.activation(t0, nr[0:P, 0:1], AF.Ln))
    sq.emit("scalar", lambda s: s.activation(t0, t0, AF.Exp, scale=-0.5))
    sq.emit("vector", lambda v: v.scalar_tensor_tensor(nr[0:P, 1:2], t0, t0, nr[0:P, 0:1], ALU.mult, ALU.mult))
    sq.emit("vector", lambda v: v.tensor_scalar(nr[0:P, 1:2], nr[0:P, 1:2], -0.5, 1.5, ALU.mult, ALU.add))
    sq.emit("vector", lambda v: v.scalar_tensor_tensor(out, nr[0:P, 1:2], 1.0, t0, ALU.mult, ALU.mult))


def emit_transpose_128(sq, nc, src, dst, psum_tr, ident, dtype_note=None):
    sq.emit("tensor", lambda t: t.transpose(psum_tr[:, :], src, ident))
    sq.emit("vector", lambda v: v.tensor_copy(dst, psum_tr[:, :]))


class ESeq:

    ENGS = ("tensor", "vector", "scalar", "gpsimd", "sync")
    SELF_SYNC = True

    def __init__(self):
        self.stream = {e: [] for e in self.ENGS}
        self.cnt = {e: 0 for e in self.ENGS}
        self.hwm = {e: {f: 0 for f in self.ENGS} for e in self.ENGS}

    def emit(self, eng, fn, inc=1, waits=None):
        w = {}
        if waits:
            for oe, cv in waits:
                if oe == eng or cv <= 0:
                    continue
                if cv > self.hwm[eng][oe]:
                    w[oe] = max(w.get(oe, 0), cv)
        for oe, cv in w.items():
            self.hwm[eng][oe] = cv
        self.stream[eng].append((fn, inc, w))
        self.cnt[eng] += inc
        return self.cnt[eng]

    def dma(self, eng, out, in_, waits=None, **kw):
        return self.emit(eng, lambda e: e.dma_start(out=out, in_=in_, **kw),
                         inc=16, waits=waits)

    def marks(self):
        return [(e, self.cnt[e]) for e in self.ENGS]

    def build(self, nc, sems):
        with nc.Block() as block:
            for eng_name in self.ENGS:
                items = self.stream[eng_name]
                if not items and eng_name != "sync":
                    continue

                def mk(items, eng_name):
                    def emit(eng):
                        run = 0
                        for fn, inc, w in items:
                            if self.SELF_SYNC and run and eng_name not in ("tensor", "sync"):
                                eng.wait_ge(sems[eng_name], run)
                            for oe, cv in w.items():
                                eng.wait_ge(sems[oe], cv)
                            fn(eng).then_inc(sems[eng_name], inc)
                            run += inc
                        if eng_name == "sync":
                            for oe in self.ENGS:
                                if self.cnt[oe]:
                                    eng.wait_ge(sems[oe], self.cnt[oe])
                    return emit
                getattr(block, eng_name)(mk(items, eng_name))


def build_stage_a():
    nc = bass.Bass("TRN2", target_bir_lowering=False, debug=False)
    x = nc.dram_tensor("x", [TSH, D], F32, kind="ExternalInput").ap()
    nw = nc.dram_tensor("nw", [1, D], F32, kind="ExternalInput").ap()
    wkT = nc.dram_tensor("wkT", [D, 64], F32, kind="ExternalInput").ap()
    wqT = nc.dram_tensor("wqT", [D, 64], F32, kind="ExternalInput").ap()
    wvT = nc.dram_tensor("wvT", [D, D], F32, kind="ExternalInput").ap()
    eta = nc.dram_tensor("eta", [64, 1], F32, kind="ExternalInput").ap()
    ident = nc.dram_tensor("ident", [128, 128], F32, kind="ExternalInput").ap()
    xn_o = nc.dram_tensor("xn", [TSH, D], F32, kind="ExternalOutput").ap()
    ekfm_o = nc.dram_tensor("ekfm", [64, TSH], F32, kind="ExternalOutput").ap()
    knegfm_o = nc.dram_tensor("knegfm", [64, TSH], F32, kind="ExternalOutput").ap()
    qfm_o = nc.dram_tensor("qfm", [64, TSH], F32, kind="ExternalOutput").ap()
    vtok_o = nc.dram_tensor("vtok", [TSH, D], F32, kind="ExternalOutput").ap()
    vfm_o = nc.dram_tensor("vfm", [D, TSH], F32, kind="ExternalOutput").ap()

    NT = TSH // 128
    from contextlib import ExitStack
    with ExitStack() as ctx:
        def sb(name, shape, dt=F32):
            return ctx.enter_context(nc.sbuf_tensor(name, shape, dt))
        def psum(name, shape):
            return ctx.enter_context(nc.psum_tensor(name, shape, F32))
        xt = sb("xt", [128, D]); xnt = sb("xnt", [128, D]); xn2t = sb("xn2t", [128, D])
        dump = sb("dump", [128, D]); col4 = sb("col4", [128, 4])
        nwbc = sb("nwbc", [128, D]); nwrow = sb("nwrow", [1, D])
        onecol = sb("onecol", [128, 1]); identt = sb("identt", [128, 128])
        x2T = sb("x2T", [128, 8 * 128], F32R)
        wk = sb("wk", [128, 8 * 64], F32R); wq = sb("wq", [128, 8 * 64], F32R)
        wv = sb("wv", [128, 8 * D], F32R)
        etac = sb("etac", [64, 1]); kq = sb("kq", [64, 2 * 128]); ekk = sb("ekk", [64, 3 * 128])
        vstage = sb("vstage", [128, D]); vfst = sb("vfst", [128, D])
        ps_tr = psum("ps_tr", [128, 128]); ps_kq = psum("ps_kq", [128, 512]); ps_v = psum("ps_v", [128, 512])
        cget = sb("consts", [128, 8]); nrt_ = sb("nr", [128, 3])
        S = ctx.enter_context(nc.semaphore("S"))
        sq = Seq()
        cc = Consts(nc, sq, cget)
        sq.dma("sync", identt[:, :], ident)
        sq.dma("sync", nwrow[0:1, :], nw)
        sq.dma("sync", etac[:, :], eta)
        for c in range(8):
            sq.dma("sync", wk[:, c * 64:(c + 1) * 64], wkT[c * 128:(c + 1) * 128, :].bitcast(F32R))
            sq.dma("sync", wq[:, c * 64:(c + 1) * 64], wqT[c * 128:(c + 1) * 128, :].bitcast(F32R))
            sq.dma("sync", wv[:, c * D:(c + 1) * D], wvT[c * 128:(c + 1) * 128, :].bitcast(F32R))
        sq.emit("scalar", lambda s: s.activation(etac[:, :], etac[:, :], AF.Abs))
        sq.emit("vector", lambda v: v.tensor_scalar(etac[:, :], etac[:, :], ETA_MIN, None, ALU.max))
        sq.emit("vector", lambda v: v.memset(onecol[:, :], 1.0))
        sq.emit("vector", lambda v: v.memset(dump[0:1, 0:128], 1.0))
        for j in range(D // 512):
            sq.emit("tensor", lambda t, j=j: t.matmul(ps_v[:, :], dump[0:1, 0:128], nwrow[0:1, j * 512:(j + 1) * 512], start=True, stop=True))
            sq.emit("vector", lambda v, j=j: v.tensor_copy(nwbc[:, j * 512:(j + 1) * 512], ps_v[:, :]))

        scr = {"dump": dump, "col4": col4, "nr": nrt_}
        for it in range(NT):
            tok0 = it * 128
            sq.dma("sync", xt[:, :], x[tok0:tok0 + 128, :])
            emit_ln_token_tile(sq, nc, xt[:, :], xnt[:, :], scr, cc)
            sq.emit("vector", lambda v: v.scalar_tensor_tensor(dump[:, :], xnt[:, :], 1.0, xnt[:, :], ALU.mult, ALU.mult, accum_out=col4[:, 0:1]))
            sq.emit("vector", lambda v: v.tensor_scalar(col4[:, 1:2], col4[:, 0:1], 1.0 / D, None, ALU.mult))
            emit_rsqrt(sq, col4[:, 3:4], col4[:, 1:2], scr, cc, eps=EPS_RMS)
            sq.emit("vector", lambda v: v.scalar_tensor_tensor(xn2t[:, :], xnt[:, :], col4[:, 3:4], nwbc[:, :], ALU.mult, ALU.mult))
            sq.dma("gpsimd", xn_o[tok0:tok0 + 128, :], xnt[:, :])
            for c in range(8):
                sq.emit("tensor", lambda t, c=c: t.transpose(ps_tr[:, :], xn2t[:, c * 128:(c + 1) * 128], identt[:, :]))
                sq.emit("vector", lambda v, c=c: v.tensor_copy(x2T[:, c * 128:(c + 1) * 128].bitcast(F32R), ps_tr[:, :]))
            for c in range(8):
                sq.emit("tensor", lambda t, c=c: t.matmul(ps_kq[0:64, 0:128], wk[:, c * 64:(c + 1) * 64], x2T[:, c * 128:(c + 1) * 128], start=(c == 0), stop=(c == 7)))
            for c in range(8):
                sq.emit("tensor", lambda t, c=c: t.matmul(ps_kq[0:64, 128:256], wq[:, c * 64:(c + 1) * 64], x2T[:, c * 128:(c + 1) * 128], start=(c == 0), stop=(c == 7)))
            sq.emit("vector", lambda v: v.tensor_scalar(kq[:, 0:256], ps_kq[0:64, 0:256], 0.0, None, ALU.min))
            sq.emit("scalar", lambda s: s.activation(kq[:, 0:256], kq[:, 0:256], AF.Exp))
            sq.emit("vector", lambda v: v.scalar_tensor_tensor(kq[:, 0:256], ps_kq[0:64, 0:256], 0.0, kq[:, 0:256], ALU.max, ALU.add))
            sq.emit("vector", lambda v: v.tensor_scalar(ekk[:, 0:128], kq[:, 0:128], etac[:, 0:1], None, ALU.mult))
            sq.emit("vector", lambda v: v.tensor_scalar(ekk[:, 128:256], kq[:, 0:128], -1.0, None, ALU.mult))
            sq.emit("vector", lambda v: v.tensor_copy(ekk[:, 256:384], kq[:, 128:256]))
            sq.dma("gpsimd", ekfm_o[:, tok0:tok0 + 128], ekk[:, 0:128])
            sq.dma("gpsimd", knegfm_o[:, tok0:tok0 + 128], ekk[:, 128:256])
            sq.dma("gpsimd", qfm_o[:, tok0:tok0 + 128], ekk[:, 256:384])
            for oj in range(2):
                for c in range(8):
                    sq.emit("tensor", lambda t, oj=oj, c=c: t.matmul(ps_v[:, :], x2T[:, c * 128:(c + 1) * 128], wv[:, c * D + oj * 512:c * D + (oj + 1) * 512], start=(c == 0), stop=(c == 7)))
                sq.emit("vector", lambda v, oj=oj: v.tensor_copy(vstage[:, oj * 512:(oj + 1) * 512], ps_v[:, :]))
            sq.dma("gpsimd", vtok_o[tok0:tok0 + 128, :], vstage[:, :])
            for c in range(8):
                sq.emit("tensor", lambda t, c=c: t.transpose(ps_tr[:, :], vstage[:, c * 128:(c + 1) * 128], identt[:, :]))
                sq.emit("vector", lambda v, c=c: v.tensor_copy(vfst[:, c * 128:(c + 1) * 128], ps_tr[:, :]))
            for c in range(8):
                sq.dma("gpsimd", vfm_o[c * 128:(c + 1) * 128, tok0:tok0 + 128], vfst[:, c * 128:(c + 1) * 128])
        sq.build(nc, S)
    return nc


def emit_rsqrt_row(sq, out, a_in, rowscr, P=1):
    t0 = rowscr[0:P, 0:1]
    t1 = rowscr[0:P, 1:2]
    sq.emit("scalar", lambda s: s.activation(t0, a_in, AF.Ln))
    sq.emit("scalar", lambda s: s.activation(t0, t0, AF.Exp, scale=-0.5))
    sq.emit("vector", lambda v: v.scalar_tensor_tensor(t1, t0, t0, a_in, ALU.mult, ALU.mult))
    sq.emit("vector", lambda v: v.tensor_scalar(t1, t1, -0.5, 1.5, ALU.mult, ALU.add))
    sq.emit("vector", lambda v: v.scalar_tensor_tensor(out, t1, 1.0, t0, ALU.mult, ALU.mult))


def build_stage_scan(nsteps, t0_global):
    FD = 64 * D
    nc = bass.Bass("TRN2", target_bir_lowering=False, debug=False)
    kneg = nc.dram_tensor("kneg", [64, T], F32, kind="ExternalInput").ap()
    ekf = nc.dram_tensor("ekf", [64, T], F32, kind="ExternalInput").ap()
    vtok = nc.dram_tensor("vtok", [T, D], F32, kind="ExternalInput").ap()
    Min = nc.dram_tensor("Min", [64, D], F32, kind="ExternalInput").ap()
    ident = nc.dram_tensor("ident", [128, 128], F32, kind="ExternalInput").ap()
    ident64 = nc.dram_tensor("ident64", [64, 64], F32, kind="ExternalInput").ap()
    onesd = nc.dram_tensor("onesd", [128, 128], F32, kind="ExternalInput").ap()
    Mout_o = nc.dram_tensor("Mout", [64, D], F32, kind="ExternalOutput").ap()

    from contextlib import ExitStack
    with ExitStack() as ctx:
        def sb(name, shape, dt=F32):
            return ctx.enter_context(nc.sbuf_tensor(name, shape, dt))
        def psum(name, shape):
            return ctx.enter_context(nc.psum_tensor(name, shape, F32))
        ntile = ceil_div(nsteps, 128)
        vt = sb("vt", [128, ntile * D], F32R)
        kn = sb("kn", [64, nsteps], F32R)
        ek = sb("ek", [64, nsteps], F32R)
        Mr = sb("Mr", [64, D], F32R)
        T1 = sb("T1", [64, D])
        identt = sb("identt", [128, 128], F32R)
        id64 = sb("id64", [64, 64], F32R)
        ekrow = sb("ekrow", [1, 64], F32R)
        vsrow = sb("vsrow", [1, D], F32R)
        rowscr = sb("rowscr", [1, 8])
        srow = sb("srow", [1, 2], F32R)
        cols = sb("cols", [64, 2], F32R)
        onecol = sb("onecol", [64, 1], F32R)
        onerow = sb("onerow", [1, 64], F32R)
        dump = sb("dumpS", [1, D])
        dump64 = sb("dump64", [64, D])
        bcols = sb("bcols", [64, 2])
        cget = sb("consts", [128, 8])
        ps_r = psum("ps_r", [1, D])
        ps_v2 = psum("ps_v2", [1, D])
        ps_d = psum("ps_d", [64, D])
        ps_sm = psum("ps_sm", [64, 128])
        ps_e = ps_sm[0:1, 0:64]
        ps_s = ps_sm[0:1, 64:66]
        ps_b = ps_sm[0:64, 66:68]
        S = ctx.enter_context(nc.semaphore("S"))
        ctx.enter_context(nc.allow_low_precision(reason="f32r stats tiles; scan is contractive"))
        sq = Seq()
        cc = Consts(nc, sq, cget)
        sq.dma("sync", identt[:, :], ident.bitcast(F32R))
        sq.dma("sync", id64[:, :], ident64.bitcast(F32R))
        sq.dma("sync", kn[:, :], kneg[:, t0_global:t0_global + nsteps].bitcast(F32R), allow_slow_non_contiguous=True)
        sq.dma("sync", ek[:, :], ekf[:, t0_global:t0_global + nsteps].bitcast(F32R), allow_slow_non_contiguous=True)
        sq.dma("sync", Mr[:, :], Min.bitcast(F32R))
        for it in range(ntile):
            r0 = t0_global + it * 128
            sq.dma("sync", vt[:, it * D:(it + 1) * D],
                   vtok[r0:r0 + 128, :].bitcast(F32R))
        sq.dma("sync", onecol[:, :], onesd[0:64, 0:1].bitcast(F32R), allow_slow_non_contiguous=True)
        sq.dma("sync", onerow[:, :], onesd[0:1, 0:64].bitcast(F32R))

        for st in range(nsteps):
            tile, tin = st // 128, st % 128
            for j in range(2):
                vslj = vt[:, tile * D + j * 512: tile * D + (j + 1) * 512]
                icol = identt[:, tin:tin + 1]
                sq.emit("tensor", lambda t, j=j, vslj=vslj, icol=icol: t.matmul(ps_v2[0:1, j * 512:(j + 1) * 512], icol, vslj, start=True, stop=True))
            for j in range(2):
                vslj = vt[:, tile * D + j * 512: tile * D + (j + 1) * 512]
                icol = identt[:, tin:tin + 1]
                kcol = kn[:, st:st + 1]
                sq.emit("tensor", lambda t, j=j, vslj=vslj, icol=icol: t.matmul(ps_r[0:1, j * 512:(j + 1) * 512], icol, vslj, start=True, stop=False))
                sq.emit("tensor", lambda t, j=j, kcol=kcol: t.matmul(ps_r[0:1, j * 512:(j + 1) * 512], kcol, Mr[:, j * 512:(j + 1) * 512], start=False, stop=True))
            sq.emit("scalar", lambda s: s.activation(dump[0:1, :], ps_r[0:1, :], AF.Square, accum_out=rowscr[0:1, 2:3]))
            sq.emit("vector", lambda v: v.tensor_scalar(rowscr[0:1, 2:3], rowscr[0:1, 2:3], 1e-38, None, ALU.add))
            emit_rsqrt_row(sq, rowscr[0:1, 3:4], rowscr[0:1, 2:3], rowscr)
            sq.emit("vector", lambda v: v.scalar_tensor_tensor(rowscr[0:1, 3:4], rowscr[0:1, 3:4], 1.0, rowscr[0:1, 2:3], ALU.mult, ALU.mult))
            sq.emit("vector", lambda v: v.tensor_scalar(vsrow[0:1, :], ps_v2[0:1, :], rowscr[0:1, 3:4], None, ALU.mult))
            ecol = ek[:, st:st + 1]
            sq.emit("tensor", lambda t, ecol=ecol: t.matmul(ps_e, ecol, id64[:, :], start=True, stop=True))
            sq.emit("vector", lambda v: v.tensor_copy(ekrow[0:1, :], ps_e))
            for j in range(2):
                sq.emit("tensor", lambda t, j=j: t.matmul(ps_d[:, j * 512:(j + 1) * 512], ekrow[0:1, :], vsrow[0:1, j * 512:(j + 1) * 512], start=True, stop=True))
            sq.emit("vector", lambda v: v.tensor_tensor(T1[:, :], Mr[:, :], ps_d[:, :], ALU.add))
            sq.emit("vector", lambda v: v.tensor_reduce(cols[:, 0:1], T1[:, :], mybir.AxisListType.X, ALU.add))
            sq.emit("vector", lambda v: v.scalar_tensor_tensor(dump64[:, :], T1[:, :], 1.0, T1[:, :], ALU.mult, ALU.mult, accum_out=cols[:, 1:2]))
            sq.emit("tensor", lambda t: t.matmul(ps_s, onecol[:, :], cols[:, :], start=True, stop=True))
            sq.emit("vector", lambda v: v.tensor_scalar(rowscr[0:1, 4:6], ps_s, 1.0 / FD, None, ALU.mult))
            sq.emit("vector", lambda v: v.scalar_tensor_tensor(rowscr[0:1, 6:7], rowscr[0:1, 4:5], rowscr[0:1, 4:5], rowscr[0:1, 5:6], ALU.mult, ALU.subtract))
            sq.emit("vector", lambda v: v.tensor_scalar(rowscr[0:1, 6:7], rowscr[0:1, 6:7], -1.0, float(EPS_LN), ALU.mult, ALU.add))
            emit_rsqrt_row(sq, rowscr[0:1, 7:8], rowscr[0:1, 6:7], rowscr)
            sq.emit("vector", lambda v: v.tensor_copy(srow[0:1, 0:1], rowscr[0:1, 4:5]))
            sq.emit("vector", lambda v: v.tensor_copy(srow[0:1, 1:2], rowscr[0:1, 7:8]))
            sq.emit("tensor", lambda t: t.matmul(ps_b, onerow[0:1, :], srow[0:1, :], start=True, stop=True))
            sq.emit("vector", lambda v: v.tensor_copy(bcols[:, :], ps_b))
            sq.emit("vector", lambda v: v.tensor_scalar(Mr[:, :], T1[:, :], bcols[:, 0:1], bcols[:, 1:2], ALU.subtract, ALU.mult))
        sq.dma("gpsimd", Mout_o, Mr[:, :].bitcast(F32))
        sq.build(nc, S)
    return nc


def build_stage_scan2(S):
    FD = 64 * D
    FOLD = 4
    nc = bass.Bass("TRN2", target_bir_lowering=False, debug=False)
    kekd = nc.dram_tensor("kek", [128, 2 * S], F32, kind="ExternalInput").ap()
    vfmd = nc.dram_tensor("vfm", [128, 8, S], F32, kind="ExternalInput").ap()
    vekd = nc.dram_tensor("vek", [S, 1088], F32, kind="ExternalInput").ap()
    Uind = nc.dram_tensor("Uin", [64, D], F32, kind="ExternalInput").ap()
    SCind = nc.dram_tensor("SCin", [128, 8], F32, kind="ExternalInput").ap()
    onesd = nc.dram_tensor("onesd", [128, 128], F32, kind="ExternalInput").ap()
    Uout_o = nc.dram_tensor("Uout", [64, D], F32, kind="ExternalOutput").ap()
    SCout_o = nc.dram_tensor("SCout", [128, 8], F32, kind="ExternalOutput").ap()
    Mout_o = nc.dram_tensor("Mout", [64, D], F32, kind="ExternalOutput").ap()
    dbg_o = nc.dram_tensor("dbg", [128, 32], F32, kind="ExternalOutput").ap()
    from contextlib import ExitStack
    with ExitStack() as ctx:
        def sb(name, shape, dt=F32):
            return ctx.enter_context(nc.sbuf_tensor(name, shape, dt))
        def psum(name, shape):
            return ctx.enter_context(nc.psum_tensor(name, shape, F32))
        kek = sb("kekS", [128, 2 * S], F32R)
        vfm = sb("vfmS", [128, 8 * S], F32R)
        dump = sb("dumpS2", [128, 8 * S], F32R)
        Us = sb("UsS", [64, D], F32R)
        SC = sb("SCS", [128, 16], F32)
        c1c = sb("c1c", [128, S], F32)
        ddc = sb("ddc", [128, S], F32)
        skc = sb("skc", [128, S], F32)
        w8 = sb("w8S", [128, 8], F32)
        d8 = sb("d8S", [128, 8], F32)
        Pc = sb("PcS", [128, 2], F32R)
        onesf = sb("onesfS", [128, 128], F32R)
        onec = sb("onecS", [128, 1], F32R)
        oner = sb("onerS", [1, 128], F32R)
        vek3 = sb("vek3S", [1, 3 * 1088], F32R)
        eksr = sb("eksrS", [1, 64], F32R)
        r5 = sb("r5S", [1, 2 * S], F32R)
        r6 = sb("r6S", [1, 2 * S], F32R)
        r7 = sb("r7S", [1, S], F32R)
        r8 = sb("r8S", [1, S], F32R)
        c1r = sb("c1rS", [1, S], F32R)
        ddr = sb("ddrS", [1, S], F32R)
        skr = sb("skrS", [1, S], F32R)
        Mt = sb("MtS", [64, D], F32)
        CC = sb("CCS", [128, 8], F32)
        ps_kU = psum("ps_kU", [128, 16])
        ps_qb = psum("ps_qb", [128, 2])
        ps_del = psum("ps_del", [64, D])
        ps_bc = psum("ps_bc", [128, 512])
        ps_pro = psum("ps_pro", [1, 512])
        sems = {e: ctx.enter_context(nc.semaphore("sem_" + e)) for e in ESeq.ENGS}
        ctx.enter_context(nc.allow_low_precision(reason="f32r staging tiles"))
        q = ESeq()
        TT, VV, AA, LL, QQ = "tensor", "vector", "scalar", "gpsimd", "sync"
        ALU_, AF_ = ALU, AF

        q_kek = q.dma(QQ, kek[:, :], kekd.bitcast(F32R))
        for j in range(8):
            q_vfm = q.dma(QQ, vfm[:, j * S:(j + 1) * S], vfmd[:, j, :].bitcast(F32R))
        q_st = q.dma(QQ, Us[:, :], Uind.bitcast(F32R))
        q_st = q.dma(QQ, SC[:, 0:8], SCind)
        v_on = q.emit(VV, lambda v: v.memset(onesf[:, :].bitcast(F32), 1.0))
        v_on = q.emit(VV, lambda v: v.memset(onec[:, :].bitcast(F32), 1.0))
        v_on = q.emit(VV, lambda v: v.memset(oner[0:1, :].bitcast(F32), 1.0))
        q_on = 0
        vek_ready = {}
        for i in range(min(2, S)):
            vek_ready[i] = q.dma(QQ, vek3[0:1, i * 1088:(i + 1) * 1088],
                                 vekd[i:i + 1, :].bitcast(F32R))
        a_sqk = q.emit(AA, lambda s: s.activation(dump[0:64, 0:2 * S], kek[0:64, :], AF_.Square),
                       waits=[(QQ, q_kek)])
        v_cp = 0
        NC4 = 2 * S // 512
        for c4 in range(NC4):
            sl = slice(c4 * 512, (c4 + 1) * 512)
            t_m = q.emit(TT, lambda t, sl=sl: t.matmul(ps_pro[0:1, :], onec[0:64, :], kek[0:64, sl], start=True, stop=True),
                         waits=[(QQ, q_kek), (VV, v_on), (VV, v_cp)])
            v_cp = q.emit(VV, lambda v, sl=sl: v.tensor_copy(r5[0:1, sl], ps_pro[0:1, :]),
                          waits=[(TT, t_m)])
        for c4 in range(NC4):
            sl = slice(c4 * 512, (c4 + 1) * 512)
            t_m = q.emit(TT, lambda t, sl=sl: t.matmul(ps_pro[0:1, :], onec[0:64, :], dump[0:64, sl], start=True, stop=True),
                         waits=[(AA, a_sqk), (VV, v_cp)])
            v_cp = q.emit(VV, lambda v, sl=sl: v.tensor_copy(r6[0:1, sl], ps_pro[0:1, :]),
                          waits=[(TT, t_m)])
        for c2 in range(S // 512):
            sl = slice(c2 * 512, (c2 + 1) * 512)
            for j in range(8):
                t_m = q.emit(TT, lambda t, j=j, sl=sl: t.matmul(ps_pro[0:1, :], onec[:, :], vfm[:, j * S + sl.start: j * S + sl.stop], start=(j == 0), stop=(j == 7)),
                             waits=[(QQ, q_vfm), (VV, v_cp)])
            v_cp = q.emit(VV, lambda v, sl=sl: v.tensor_copy(r7[0:1, sl], ps_pro[0:1, :]),
                          waits=[(TT, t_m)])
        a_sqv = q.emit(AA, lambda s: s.activation(dump[:, :], vfm[:, :], AF_.Square),
                       waits=[(QQ, q_vfm), (VV, v_cp)])
        for c2 in range(S // 512):
            sl = slice(c2 * 512, (c2 + 1) * 512)
            for j in range(8):
                t_m = q.emit(TT, lambda t, j=j, sl=sl: t.matmul(ps_pro[0:1, :], onec[:, :], dump[:, j * S + sl.start: j * S + sl.stop], start=(j == 0), stop=(j == 7)),
                             waits=[(AA, a_sqv), (VV, v_cp)])
            v_cp = q.emit(VV, lambda v, sl=sl: v.tensor_copy(r8[0:1, sl], ps_pro[0:1, :]),
                          waits=[(TT, t_m)])
        v_cmb = q.emit(VV, lambda v: v.scalar_tensor_tensor(c1r[0:1, :], r5[0:1, 1:2 * S:2], 1.0 / FD, r7[0:1, :], ALU_.mult, ALU_.mult))
        v_cmb = q.emit(VV, lambda v: v.scalar_tensor_tensor(ddr[0:1, :], r6[0:1, 1:2 * S:2], 1.0 / FD, r8[0:1, :], ALU_.mult, ALU_.mult))
        v_cmb = q.emit(VV, lambda v: v.tensor_scalar(skr[0:1, :], r5[0:1, 0:2 * S:2], -1.0, None, ALU_.mult))
        v_cmb = q.emit(VV, lambda v: v.tensor_tensor(r8[0:1, :], c1r[0:1, :], c1r[0:1, :], ALU_.mult))
        v_cmb = q.emit(VV, lambda v: v.tensor_tensor(ddr[0:1, :], ddr[0:1, :], r8[0:1, :], ALU_.subtract))
        v_bc = v_cmb
        for row, dst in ((c1r, c1c), (ddr, ddc), (skr, skc)):
            for c2 in range(S // 512):
                sl = slice(c2 * 512, (c2 + 1) * 512)
                t_m = q.emit(TT, lambda t, row=row, sl=sl: t.matmul(ps_bc[:, :], oner[0:1, :], row[0:1, sl], start=True, stop=True),
                             waits=[(VV, v_bc)])
                v_bc = q.emit(VV, lambda v, dst=dst, sl=sl: v.tensor_copy(dst[:, sl], ps_bc[:, :]),
                              waits=[(TT, t_m)])
        for ci, cv in ((0, 2.0 / FD), (1, 2.0), (2, -0.5), (3, 1.5), (4, float(EPS_LN)), (5, -1.0)):
            v_bc = q.emit(VV, lambda v, ci=ci, cv=cv: v.memset(CC[:, ci:ci + 1], cv))
        dbg = sb("dbgS", [128, 32], F32)
        for di, srcc in ((0, c1c), (2, ddc), (4, skc)):
            v_bc = q.emit(VV, lambda v, di=di, srcc=srcc: v.tensor_copy(dbg[:, di:di + 2], srcc[:, 0:2]))
        v_bc = q.emit(VV, lambda v: v.tensor_copy(dbg[:, 6:7], onec[:, 0:1].bitcast(F32)))
        v_bc = q.emit(VV, lambda v: v.tensor_copy(dbg[0:1, 8:16], r5[0:1, 0:8].bitcast(F32)))

        l_pro = q.emit(LL, lambda g: g.tensor_tensor(SC[:, 4:5], SC[:, 1:2], skc[:, 0:1], ALU_.mult),
                       waits=[(VV, v_bc), (QQ, q_st)])

        v_u_done = v_bc
        t_delta_prev = 0
        l_end = l_pro
        l_ab_prev = l_pro
        l_end_prev = l_pro
        q_pro_all = q.cnt[QQ]
        for t in range(S):
            rb = t % 3
            gfold = (t + 1) % FOLD == 0
            w0 = [(VV, v_u_done), (QQ, q_pro_all)] if t == 0 else [(VV, v_u_done)]
            for j in range(8):
                t_kU = q.emit(TT, lambda tn, j=j, t=t: tn.matmul(
                    ps_kU[:, 2 * j:2 * j + 2],
                    Us[0:64, j * 128:(j + 1) * 128],
                    kek[0:64, 2 * t:2 * t + 2], start=True, stop=True),
                    waits=w0 if j == 0 else None)
            vstk = vfm[:, t:8 * S:S]
            v_ = q.emit(VV, lambda v, vstk=vstk: v.scalar_tensor_tensor(w8[:, :], ps_kU[:, 0:16:2], SC[:, 0:1], vstk, ALU_.mult, ALU_.add),
                        waits=[(TT, t_kU), (LL, l_ab_prev)])
            v_ = q.emit(VV, lambda v: v.tensor_scalar(w8[:, :], w8[:, :], SC[:, 4:5], None, ALU_.subtract))
            v_ = q.emit(VV, lambda v: v.scalar_tensor_tensor(d8[:, :], w8[:, :], 1.0, w8[:, :], ALU_.mult, ALU_.mult, accum_out=Pc[:, 0:1]))
            v_p2 = q.emit(VV, lambda v, vstk=vstk: v.scalar_tensor_tensor(d8[:, :], ps_kU[:, 1:16:2], 1.0, vstk, ALU_.mult, ALU_.mult, accum_out=Pc[:, 1:2]))
            t_qb = q.emit(TT, lambda tn: tn.matmul(ps_qb[:, 0:2], onesf[:, :], Pc[:, :], start=True, stop=True),
                          waits=[(VV, v_p2)])
            a_ = q.emit(AA, lambda s: s.activation(SC[:, 5:6], ps_qb[:, 0:1], AF_.Ln),
                        waits=[(TT, t_qb)])
            a_y0 = q.emit(AA, lambda s: s.activation(SC[:, 6:7], SC[:, 5:6], AF_.Exp, scale=-0.5))
            v_ = q.emit(VV, lambda v: v.tensor_tensor(SC[:, 8:9], ps_qb[:, 0:1], SC[:, 6:7], ALU_.mult),
                        waits=[(AA, a_y0)])
            v_ = q.emit(VV, lambda v: v.tensor_copy(SC[:, 9:10], ps_qb[:, 1:2]))
            v_g = q.emit(VV, lambda v: v.tensor_tensor(SC[:, 10:11], SC[:, 8:9], SC[:, 2:3], ALU_.mult),
                         waits=[(LL, l_end_prev)])
            v_eks = q.emit(VV, lambda v, rb=rb: v.tensor_scalar(eksr[0:1, :], vek3[0:1, rb * 1088 + 1024:rb * 1088 + 1088], SC[0:1, 10:11], None, ALU_.mult),
                           waits=[(QQ, vek_ready[t])])
            t_d = q.emit(TT, lambda tn, rb=rb: tn.matmul(ps_del[0:64, 0:512], eksr[0:1, :], vek3[0:1, rb * 1088:rb * 1088 + 512], start=True, stop=True),
                         waits=[(VV, v_eks)])
            t_delta = q.emit(TT, lambda tn, rb=rb: tn.matmul(ps_del[0:64, 512:1024], eksr[0:1, :], vek3[0:1, rb * 1088 + 512:rb * 1088 + 1024], start=True, stop=True))
            if t + 2 < S:
                rb2 = (t + 2) % 3
                vek_ready[t + 2] = q.dma(QQ, vek3[0:1, rb2 * 1088:(rb2 + 1) * 1088],
                                         vekd[t + 2:t + 3, :].bitcast(F32R),
                                         waits=[(TT, t_delta_prev)])
            t_delta_prev = t_delta
            l_ = q.emit(LL, lambda g, t=t: g.tensor_tensor(SC[:, 11:12], SC[:, 8:9], c1c[:, t:t + 1], ALU_.mult),
                        waits=[(VV, v_g)])
            l_ = q.emit(LL, lambda g: g.tensor_tensor(SC[:, 13:14], SC[:, 8:9], SC[:, 9:10], ALU_.mult))
            l_ = q.emit(LL, lambda g: g.tensor_tensor(SC[:, 13:14], SC[:, 13:14], SC[:, 7:8], ALU_.mult))
            l_ = q.emit(LL, lambda g: g.tensor_tensor(SC[:, 12:13], SC[:, 1:2], SC[:, 11:12], ALU_.mult))
            l_ = q.emit(LL, lambda g: g.tensor_tensor(SC[:, 12:13], SC[:, 12:13], SC[:, 12:13], ALU_.add))
            l_ = q.emit(LL, lambda g: g.tensor_tensor(SC[:, 12:13], SC[:, 12:13], SC[:, 13:14], ALU_.add))
            l_ = q.emit(LL, lambda g: g.tensor_tensor(SC[:, 13:14], SC[:, 8:9], SC[:, 8:9], ALU_.mult))
            l_ = q.emit(LL, lambda g, t=t: g.tensor_tensor(SC[:, 13:14], SC[:, 13:14], ddc[:, t:t + 1], ALU_.mult))
            l_ = q.emit(LL, lambda g: g.tensor_tensor(SC[:, 12:13], SC[:, 12:13], SC[:, 13:14], ALU_.add))
            l_ = q.emit(LL, lambda g: g.tensor_tensor(SC[:, 12:13], SC[:, 12:13], SC[:, 3:4], ALU_.add))
            l_vp = q.emit(LL, lambda g: g.tensor_tensor(SC[:, 14:15], SC[:, 12:13], CC[:, 4:5], ALU_.add))
            a_ = q.emit(AA, lambda s: s.activation(SC[:, 5:6], SC[:, 14:15], AF_.Ln),
                        waits=[(LL, l_vp)])
            a_ = q.emit(AA, lambda s: s.activation(SC[:, 15:16], SC[:, 5:6], AF_.Exp, scale=-0.5))
            a_ri = q.emit(AA, lambda s: s.activation(SC[:, 5:6], SC[:, 5:6], AF_.Exp, scale=0.5))
            l_ = q.emit(LL, lambda g: g.tensor_tensor(SC[:, 0:1], SC[:, 15:16], SC[:, 0:1], ALU_.mult),
                        waits=[(AA, a_ri)])
            l_ = q.emit(LL, lambda g: g.tensor_tensor(SC[:, 13:14], SC[:, 1:2], SC[:, 11:12], ALU_.subtract))
            l_ab = q.emit(LL, lambda g: g.tensor_tensor(SC[:, 1:2], SC[:, 13:14], SC[:, 15:16], ALU_.mult))
            if not gfold and t + 1 < S:
                l_ab = q.emit(LL, lambda g, t=t: g.tensor_tensor(SC[:, 4:5], SC[:, 1:2], skc[:, t + 1:t + 2], ALU_.mult))
            if not gfold:
                l_ = q.emit(LL, lambda g: g.tensor_tensor(SC[:, 7:8], SC[:, 15:16], SC[:, 7:8], ALU_.mult))
            l_ = q.emit(LL, lambda g: g.tensor_tensor(SC[:, 13:14], SC[:, 15:16], SC[:, 15:16], ALU_.mult))
            l_ = q.emit(LL, lambda g: g.tensor_tensor(SC[:, 3:4], SC[:, 13:14], SC[:, 12:13], ALU_.mult))
            if not gfold:
                l_ = q.emit(LL, lambda g: g.tensor_tensor(SC[:, 13:14], SC[:, 2:3], SC[:, 5:6], ALU_.mult))
                l_ = q.emit(LL, lambda g: g.tensor_tensor(SC[:, 11:12], SC[:, 0:1], SC[:, 13:14], ALU_.mult))
                l_ = q.emit(LL, lambda g: g.tensor_tensor(SC[:, 11:12], SC[:, 11:12], CC[:, 5:6], ALU_.mult))
                l_ = q.emit(LL, lambda g: g.tensor_tensor(SC[:, 11:12], SC[:, 11:12], CC[:, 1:2], ALU_.add))
                l_ = q.emit(LL, lambda g: g.tensor_tensor(SC[:, 2:3], SC[:, 13:14], SC[:, 11:12], ALU_.mult))
            v_u = q.emit(VV, lambda v: v.tensor_tensor(Us[:, :], Us[:, :], ps_del[0:64, :], ALU_.add),
                         waits=[(TT, t_delta)])
            if t == 0:
                v_u = q.emit(VV, lambda v: v.tensor_copy(dbg[:, 16:32], SC[:, 0:16]),
                             waits=[(LL, l_ab)])
            if gfold:
                v_u = q.emit(VV, lambda v: v.tensor_scalar(Us[:, :], Us[:, :], SC[0:64, 0:1], SC[0:64, 1:2], ALU_.mult, ALU_.add),
                             waits=[(LL, l_ab)])
                v_u = q.emit(VV, lambda v: v.memset(SC[:, 0:1], 1.0))
                v_u = q.emit(VV, lambda v: v.memset(SC[:, 1:2], 0.0))
                v_u = q.emit(VV, lambda v: v.memset(SC[:, 2:3], 1.0))
                v_u = q.emit(VV, lambda v: v.memset(SC[:, 4:5], 0.0))
                v_u = q.emit(VV, lambda v, a2v=2.0 / FD: v.memset(SC[:, 7:8], a2v))
                l_ab_prev = l_ab
                l_end = l_
                l_end_prev = l_
            else:
                l_end = l_
                l_ab_prev = l_ab
                l_end_prev = l_
            v_u_done = v_u
        v_m = q.emit(VV, lambda v: v.tensor_scalar(Mt[:, :], Us[:, :], SC[0:64, 0:1], SC[0:64, 1:2], ALU_.mult, ALU_.add),
                     waits=[(LL, l_end)])
        q.dma(QQ, Uout_o, Us[:, :].bitcast(F32), waits=[(VV, v_m)])
        q.dma(QQ, SCout_o, SC[:, 0:8], waits=[(LL, l_end), (VV, v_m)])
        q.dma(QQ, Mout_o, Mt[:, :], waits=[(VV, v_m)])
        q.dma(QQ, dbg_o, dbg[:, :], waits=[(VV, v_m)])
        q.build(nc, sems)
    return nc


def build_stage_c():
    nc = bass.Bass("TRN2", target_bir_lowering=False, debug=False)
    qfm = nc.dram_tensor("qfm", [64, TSH], F32, kind="ExternalInput").ap()
    Md = nc.dram_tensor("M", [64, D], F32, kind="ExternalInput").ap()
    woutT = nc.dram_tensor("woutT", [D, D], F32, kind="ExternalInput").ap()
    xn = nc.dram_tensor("xn", [TSH, D], F32, kind="ExternalInput").ap()
    x = nc.dram_tensor("x", [TSH, D], F32, kind="ExternalInput").ap()
    x1_o = nc.dram_tensor("x1", [TSH, D], F32, kind="ExternalOutput").ap()
    from contextlib import ExitStack
    with ExitStack() as ctx:
        def sb(name, shape, dt=F32):
            return ctx.enter_context(nc.sbuf_tensor(name, shape, dt))
        def psum(name, shape):
            return ctx.enter_context(nc.psum_tensor(name, shape, F32))
        qf = sb("qf", [64, TSH], F32R)
        Mt = sb("Mt", [64, D], F32R)
        wo = sb("wo", [128, 8 * D], F32R)
        mofm = sb("mofm", [128, 8 * TSH], F32R)
        xnt = sb("xnt", [128, D]); xtt = sb("xtt", [128, D])
        inner = sb("inner", [128, D]); t1 = sb("t1", [128, D])
        dump = sb("dumpC", [128, D]); col4 = sb("col4C", [128, 4]); nr = sb("nrC", [128, 3])
        cget = sb("constsC", [128, 8])
        ps = psum("psC", [128, 512]); ps2 = psum("psC2", [128, 512])
        S = ctx.enter_context(nc.semaphore("S"))
        sq = Seq()
        cc = Consts(nc, sq, cget)
        scr = {"dump": dump, "col4": col4, "nr": nr}
        sq.dma("sync", qf[:, :], qfm.bitcast(F32R))
        sq.dma("sync", Mt[:, :], Md.bitcast(F32R))
        for c in range(8):
            sq.dma("sync", wo[:, c * D:(c + 1) * D], woutT[c * 128:(c + 1) * 128, :].bitcast(F32R))
        for c in range(8):
            for tsl in range(TSH // 512):
                Msl = Mt[:, c * 128:(c + 1) * 128]
                qsl = qf[:, tsl * 512:(tsl + 1) * 512]
                sq.emit("tensor", lambda t, Msl=Msl, qsl=qsl: t.matmul(ps[:, :], Msl, qsl, start=True, stop=True))
                dst = mofm[:, c * TSH + tsl * 512: c * TSH + (tsl + 1) * 512]
                sq.emit("vector", lambda v, dst=dst: v.tensor_copy(dst, ps[:, :]))
        for it in range(TSH // 128):
            tok0 = it * 128
            sq.dma("sync", xnt[:, :], xn[tok0:tok0 + 128, :])
            sq.dma("sync", xtt[:, :], x[tok0:tok0 + 128, :])
            for oh in range(2):
                for c in range(8):
                    lh = mofm[:, c * TSH + tok0: c * TSH + tok0 + 128]
                    rh = wo[:, c * D + oh * 512: c * D + (oh + 1) * 512]
                    sq.emit("tensor", lambda t, lh=lh, rh=rh, c=c: t.matmul(ps2[:, :], lh, rh, start=(c == 0), stop=(c == 7)))
                sl = slice(oh * 512, (oh + 1) * 512)
                sq.emit("vector", lambda v, sl=sl: v.tensor_tensor(inner[:, sl], xnt[:, sl], ps2[:, :], ALU.add))
            emit_ln_token_tile(sq, nc, inner[:, :], t1[:, :], scr, cc)
            sq.emit("vector", lambda v: v.tensor_tensor(t1[:, :], t1[:, :], xtt[:, :], ALU.add))
            sq.dma("gpsimd", x1_o[tok0:tok0 + 128, :], t1[:, :])
        sq.build(nc, S)
    return nc


def build_stage_d():
    H = 4096
    nc = bass.Bass("TRN2", target_bir_lowering=False, debug=False)
    x1 = nc.dram_tensor("x1", [TSH, D], F32, kind="ExternalInput").ap()
    w1T = nc.dram_tensor("w1R", [32, 128, 8 * 128], F32, kind="ExternalInput").ap()
    w2T = nc.dram_tensor("w2R", [32, 128, 8 * 128], F32, kind="ExternalInput").ap()
    h1T = nc.dram_tensor("h1R", [32, 128, 32 * 128], F32, kind="ExternalInput").ap()
    h2T = nc.dram_tensor("h2R", [32, 128, 32 * 128], F32, kind="ExternalInput").ap()
    w3T = nc.dram_tensor("w3T", [H, D], F32, kind="ExternalInput").ap()
    s1d = nc.dram_tensor("s1", [H, 1], F32, kind="ExternalInput").ap()
    s2d = nc.dram_tensor("s2", [H, 1], F32, kind="ExternalInput").ap()
    sh1d = nc.dram_tensor("sh1", [H, 1], F32, kind="ExternalInput").ap()
    sh2d = nc.dram_tensor("sh2", [H, 1], F32, kind="ExternalInput").ap()
    s3d = nc.dram_tensor("s3row", [1, D], F32, kind="ExternalInput").ap()
    ident = nc.dram_tensor("ident", [128, 128], F32, kind="ExternalInput").ap()
    x2_o = nc.dram_tensor("x2", [TSH, D], F32, kind="ExternalOutput").ap()
    TS = 512
    from contextlib import ExitStack
    with ExitStack() as ctx:
        def sb(name, shape, dt=F32):
            return ctx.enter_context(nc.sbuf_tensor(name, shape, dt))
        def psum(name, shape):
            return ctx.enter_context(nc.psum_tensor(name, shape, F32))
        xt = sb("xt", [128, D]); xnt = sb("xnt", [128, D])
        dump = sb("dumpD", [128, D]); col4 = sb("col4D", [128, 4]); nr = sb("nrD", [128, 3])
        identt = sb("identt", [128, 128])
        xT = sb("xT", [128, 8 * TS], F32R)
        hbuf = sb("hbuf", [128, 32 * TS], F32R)
        gbuf = sb("gbuf", [128, 32 * TS], F32R)
        wblk = sb("wblk", [128, 32 * 128], F32R)
        wblk2 = sb("wblk2", [128, 8 * 128], F32R)
        w3b = sb("w3b", [128, D], F32R)
        scol = sb("scol", [128, 4])
        s3bc = sb("s3bc", [128, D])
        s3row = sb("s3rowS", [1, D])
        u1t = sb("u1t", [128, TS])
        fout = sb("fout", [128, D]); t1 = sb("t1D", [128, D])
        cget = sb("constsD", [128, 8])
        ps = psum("psD", [128, 512]); ps2 = psum("psD2", [128, 512]); ps_tr = psum("psD3", [128, 128])
        S = ctx.enter_context(nc.semaphore("S"))
        sq = Seq()
        cc = Consts(nc, sq, cget)
        scr = {"dump": dump, "col4": col4, "nr": nr}
        sq.dma("sync", identt[:, :], ident)
        sq.dma("sync", s3row[0:1, :], s3d)
        sq.emit("vector", lambda v: v.memset(dump[0:1, 0:128], 1.0))
        for j in range(2):
            sq.emit("tensor", lambda t, j=j: t.matmul(ps[:, :], dump[0:1, 0:128], s3row[0:1, j * 512:(j + 1) * 512], start=True, stop=True))
            sq.emit("vector", lambda v, j=j: v.tensor_copy(s3bc[:, j * 512:(j + 1) * 512], ps[:, :]))
        NSL = TSH // TS
        for isl in range(NSL):
            st0 = isl * TS
            for jt in range(TS // 128):
                tok0 = st0 + jt * 128
                sq.dma("sync", xt[:, :], x1[tok0:tok0 + 128, :])
                emit_ln_token_tile(sq, nc, xt[:, :], xnt[:, :], scr, cc)
                for c in range(8):
                    src_ = xnt[:, c * 128:(c + 1) * 128]
                    dst_ = xT[:, c * TS + jt * 128: c * TS + (jt + 1) * 128]
                    sq.emit("tensor", lambda t, src_=src_: t.transpose(ps_tr[:, :], src_, identt[:, :]))
                    sq.emit("vector", lambda v, dst_=dst_: v.tensor_copy(dst_, ps_tr[:, :]))
            for oc in range(32):
                sq.dma("sync", wblk2[:, 0:8 * 128], w1T[oc].bitcast(F32R))
                sq.dma("sync", scol[:, 0:1], s1d[oc * 128:(oc + 1) * 128, :])
                sq.dma("sync", scol[:, 1:2], s2d[oc * 128:(oc + 1) * 128, :])
                for c in range(8):
                    lh = wblk2[:, c * 128:(c + 1) * 128]
                    rh = xT[:, c * TS:(c + 1) * TS]
                    sq.emit("tensor", lambda t, lh=lh, rh=rh, c=c: t.matmul(ps[:, :], lh, rh, start=(c == 0), stop=(c == 7)))
                sq.emit("scalar", lambda s: s.activation(u1t[:, :], ps[:, :], AF.Silu, scale=scol[:, 0:1]))
                sq.dma("sync", wblk2[:, 0:8 * 128], w2T[oc].bitcast(F32R))
                for c in range(8):
                    lh = wblk2[:, c * 128:(c + 1) * 128]
                    rh = xT[:, c * TS:(c + 1) * TS]
                    sq.emit("tensor", lambda t, lh=lh, rh=rh, c=c: t.matmul(ps2[:, :], lh, rh, start=(c == 0), stop=(c == 7)))
                dst_ = hbuf[:, oc * TS:(oc + 1) * TS]
                sq.emit("vector", lambda v, dst_=dst_: v.scalar_tensor_tensor(dst_, ps2[:, :], scol[:, 1:2], u1t[:, :], ALU.mult, ALU.mult))
            for oc in range(32):
                sq.dma("sync", wblk[:, :], h1T[oc].bitcast(F32R))
                sq.dma("sync", scol[:, 2:3], sh1d[oc * 128:(oc + 1) * 128, :])
                for c in range(32):
                    lh = wblk[:, c * 128:(c + 1) * 128]
                    rh = hbuf[:, c * TS:(c + 1) * TS]
                    sq.emit("tensor", lambda t, lh=lh, rh=rh, c=c: t.matmul(ps[:, :], lh, rh, start=(c == 0), stop=(c == 31)))
                dst_ = gbuf[:, oc * TS:(oc + 1) * TS]
                sq.emit("scalar", lambda s, dst_=dst_: s.activation(dst_, ps[:, :], AF.Gelu, scale=scol[:, 2:3]))
            for oc in range(32):
                sq.dma("sync", wblk[:, :], h2T[oc].bitcast(F32R))
                sq.dma("sync", scol[:, 3:4], sh2d[oc * 128:(oc + 1) * 128, :])
                for c in range(32):
                    lh = wblk[:, c * 128:(c + 1) * 128]
                    rh = gbuf[:, c * TS:(c + 1) * TS]
                    sq.emit("tensor", lambda t, lh=lh, rh=rh, c=c: t.matmul(ps[:, :], lh, rh, start=(c == 0), stop=(c == 31)))
                dst_ = hbuf[:, oc * TS:(oc + 1) * TS]
                sq.emit("scalar", lambda s, dst_=dst_: s.activation(dst_, ps[:, :], AF.Gelu, scale=scol[:, 3:4]))
            for jt in range(TS // 128):
                tok0 = st0 + jt * 128
                for oh in range(2):
                    for c in range(32):
                        sq.dma("sync", w3b[:, oh * 512:(oh + 1) * 512],
                               w3T[c * 128:(c + 1) * 128, oh * 512:(oh + 1) * 512].bitcast(F32R))
                        lh = hbuf[:, c * TS + jt * 128: c * TS + (jt + 1) * 128]
                        sq.emit("tensor", lambda t, lh=lh, oh=oh, c=c: t.matmul(ps2[:, :], lh, w3b[:, oh * 512:(oh + 1) * 512], start=(c == 0), stop=(c == 31)))
                    sl = slice(oh * 512, (oh + 1) * 512)
                    sq.emit("vector", lambda v, sl=sl: v.tensor_tensor(fout[:, sl], ps2[:, :], s3bc[:, sl], ALU.mult))
                sq.dma("sync", xt[:, :], x1[tok0:tok0 + 128, :])
                emit_ln_token_tile(sq, nc, fout[:, :], t1[:, :], scr, cc)
                sq.emit("vector", lambda v: v.tensor_tensor(t1[:, :], t1[:, :], xt[:, :], ALU.add))
                sq.dma("gpsimd", x2_o[tok0:tok0 + 128, :], t1[:, :])
        sq.build(nc, S)
    return nc


def build_stage_e():
    nc = bass.Bass("TRN2", target_bir_lowering=False, debug=False)
    x2 = nc.dram_tensor("x2", [TSH, D], F32, kind="ExternalInput").ap()
    wqT = nc.dram_tensor("wqT", [D, D], F32, kind="ExternalInput").ap()
    wkT = nc.dram_tensor("wkT", [D, D], F32, kind="ExternalInput").ap()
    wvT = nc.dram_tensor("wvT", [D, D], F32, kind="ExternalInput").ap()
    ident = nc.dram_tensor("ident", [128, 128], F32, kind="ExternalInput").ap()
    qfm_o = nc.dram_tensor("qfm", [D, TSH], F32, kind="ExternalOutput").ap()
    kfm_o = nc.dram_tensor("kfm", [D, TSH], F32, kind="ExternalOutput").ap()
    vtok_o = nc.dram_tensor("vtok", [TSH, D], F32, kind="ExternalOutput").ap()
    from contextlib import ExitStack
    with ExitStack() as ctx:
        def sb(name, shape, dt=F32):
            return ctx.enter_context(nc.sbuf_tensor(name, shape, dt))
        def psum(name, shape):
            return ctx.enter_context(nc.psum_tensor(name, shape, F32))
        xt = sb("xt", [128, D]); xnt = sb("xnt", [128, D])
        dump = sb("dumpE", [128, D]); col4 = sb("col4E", [128, 4]); nr = sb("nrE", [128, 3])
        identt = sb("identt", [128, 128])
        xT = sb("xTE", [128, 8 * TSH // 4], F32R)
        wq = sb("wqE", [128, 8 * D], F32R); wk = sb("wkE", [128, 8 * D], F32R)
        wv = sb("wvE", [128, 8 * D], F32R)
        stage = sb("stageE", [128, D])
        cget = sb("constsE", [128, 8])
        ps = psum("psE", [128, 512]); ps_tr = psum("psE2", [128, 128])
        S = ctx.enter_context(nc.semaphore("S"))
        sq = Seq(); cc = Consts(nc, sq, cget)
        scr = {"dump": dump, "col4": col4, "nr": nr}
        sq.dma("sync", identt[:, :], ident)
        for c in range(8):
            sq.dma("sync", wq[:, c * D:(c + 1) * D], wqT[c * 128:(c + 1) * 128, :].bitcast(F32R))
            sq.dma("sync", wk[:, c * D:(c + 1) * D], wkT[c * 128:(c + 1) * 128, :].bitcast(F32R))
            sq.dma("sync", wv[:, c * D:(c + 1) * D], wvT[c * 128:(c + 1) * 128, :].bitcast(F32R))
        TS = 512
        for isl in range(TSH // TS):
            st0 = isl * TS
            for jt in range(TS // 128):
                tok0 = st0 + jt * 128
                sq.dma("sync", xt[:, :], x2[tok0:tok0 + 128, :])
                emit_ln_token_tile(sq, nc, xt[:, :], xnt[:, :], scr, cc)
                for c in range(8):
                    src_ = xnt[:, c * 128:(c + 1) * 128]
                    dst_ = xT[:, c * TS + jt * 128: c * TS + (jt + 1) * 128]
                    sq.emit("tensor", lambda t, src_=src_: t.transpose(ps_tr[:, :], src_, identt[:, :]))
                    sq.emit("vector", lambda v, dst_=dst_: v.tensor_copy(dst_, ps_tr[:, :]))
            for wmat, outd in ((wq, qfm_o), (wk, kfm_o)):
                for ot in range(8):
                    for c in range(8):
                        lh = wmat[:, c * D + ot * 128: c * D + (ot + 1) * 128]
                        rh = xT[:, c * TS:(c + 1) * TS]
                        sq.emit("tensor", lambda t, lh=lh, rh=rh, c=c: t.matmul(ps[:, :], lh, rh, start=(c == 0), stop=(c == 7)))
                    sq.emit("vector", lambda v: v.tensor_copy(stage[:, 0:512], ps[:, :]))
                    sq.dma("gpsimd", outd[ot * 128:(ot + 1) * 128, st0:st0 + TS], stage[:, 0:512])
            for jt in range(TS // 128):
                tok0 = st0 + jt * 128
                for oh in range(2):
                    for c in range(8):
                        lh = xT[:, c * TS + jt * 128: c * TS + (jt + 1) * 128]
                        rh = wv[:, c * D + oh * 512: c * D + (oh + 1) * 512]
                        sq.emit("tensor", lambda t, lh=lh, rh=rh, c=c: t.matmul(ps[:, :], lh, rh, start=(c == 0), stop=(c == 7)))
                    sl = slice(oh * 512, (oh + 1) * 512)
                    sq.emit("vector", lambda v, sl=sl: v.tensor_copy(stage[:, sl], ps[:, :]))
                sq.dma("gpsimd", vtok_o[tok0:tok0 + 128, :], stage[:, :])
        sq.build(nc, S)
    return nc


F_NBLK = [20, 24, 28, 32]
F_MOFF = [0, 20, 44, 72]
F_MTOT = 104


def build_stage_f():
    NH, HD = 16, 64
    nc = bass.Bass("TRN2", target_bir_lowering=False, debug=False)
    qfm = nc.dram_tensor("qfm", [D, TSH], F32, kind="ExternalInput").ap()
    kfm = nc.dram_tensor("kfm", [D, T], F32, kind="ExternalInput").ap()
    vtokR = nc.dram_tensor("vtokR", [128, T // 128, D], F32, kind="ExternalInput").ap()
    masks = nc.dram_tensor("masks", [F_MTOT, 128, 512], F32, kind="ExternalInput").ap()
    onesd = nc.dram_tensor("onesd", [128, 128], F32, kind="ExternalInput").ap()
    onesr = nc.dram_tensor("onesr", [1, T], F32, kind="ExternalInput").ap()
    attn_o = nc.dram_tensor("attn", [TSH, D], F32, kind="ExternalOutput").ap()
    from contextlib import ExitStack
    with ExitStack() as ctx:
        def sb(name, shape, dt=F32):
            return ctx.enter_context(nc.sbuf_tensor(name, shape, dt))
        def psum(name, shape):
            return ctx.enter_context(nc.psum_tensor(name, shape, F32))
        kh = sb("kh", [65, T], F32R)
        qh = sb("qh", [65, TSH], F32R)
        vh = sb("vh", [128, (T // 128) * (HD + 2)], F32R)
        sqk = sb("sqk", [64, T], F32R)
        krow = sb("krow", [1, T]); qrow = sb("qrow", [1, TSH])
        rows = sb("rowsF", [1, 8])
        mrow = sb("mrow", [1, TSH])
        pt = sb("pt", [128, 512], F32R)
        pte = sb("pte", [128, 512])
        mask_t = sb("mask_t", [128, 512])
        rec = sb("recF", [128, 1])
        att = sb("attF", [128, 64])
        onec = sb("onecF", [128, 1], F32R)
        cget = sb("constsF", [128, 8])
        ps_sc = psum("ps_sc", [128, 512])
        ps_nd = psum("ps_nd", [128, 4 * 128])
        ps_r = psum("ps_rF", [1, 512])
        S = ctx.enter_context(nc.semaphore("S"))
        ctx.enter_context(nc.allow_low_precision(reason="f32r attn tiles"))
        sq = Seq(); cc = Consts(nc, sq, cget)
        sq.dma("sync", onec[:, :], onesd[:, 0:1].bitcast(F32R), allow_slow_non_contiguous=True)
        for tkb in range(T // 128):
            sq.dma("sync", vh[:, tkb * 66 + 64: tkb * 66 + 66], onesd[:, 0:2].bitcast(F32R), allow_slow_non_contiguous=True)
        for h in range(NH):
            sq.dma("sync", kh[0:64, :], kfm[h * 64:(h + 1) * 64, :].bitcast(F32R))
            sq.dma("sync", kh[64:65, :], onesr.bitcast(F32R))
            sq.dma("sync", qh[0:64, :], qfm[h * 64:(h + 1) * 64, :].bitcast(F32R))
            sq.emit("vector", lambda v: v.tensor_scalar(qh[0:64, :], qh[0:64, :], 0.125, None, ALU.mult))
            for tkb in range(T // 128):
                sq.dma("sync", vh[:, tkb * 66: tkb * 66 + 64], vtokR[:, tkb, h * 64:(h + 1) * 64].bitcast(F32R))
            sq.emit("vector", lambda v: v.scalar_tensor_tensor(sqk[:, :], kh[0:64, :], 1.0, kh[0:64, :], ALU.mult, ALU.mult))
            for j in range(T // 512):
                sl = slice(j * 512, (j + 1) * 512)
                sq.emit("tensor", lambda t, sl=sl: t.matmul(ps_r[0:1, :], onec[0:64, :], sqk[:, sl], start=True, stop=True))
                sq.emit("vector", lambda v, sl=sl: v.tensor_copy(krow[0:1, sl], ps_r[0:1, :]))
            sq.emit("vector", lambda v: v.tensor_reduce(rows[0:1, 0:1], krow[0:1, :], mybir.AxisListType.X, ALU.max))
            sq.emit("vector", lambda v: v.scalar_tensor_tensor(sqk[:, 0:TSH], qh[0:64, :], 1.0, qh[0:64, :], ALU.mult, ALU.mult))
            for j in range(TSH // 512):
                sl = slice(j * 512, (j + 1) * 512)
                sq.emit("tensor", lambda t, sl=sl: t.matmul(ps_r[0:1, :], onec[0:64, :], sqk[0:64, sl], start=True, stop=True))
                sq.emit("vector", lambda v, sl=sl: v.tensor_copy(qrow[0:1, sl], ps_r[0:1, :]))
            sq.emit("vector", lambda v: v.tensor_scalar(mrow[0:1, :], qrow[0:1, :], rows[0:1, 0:1], None, ALU.mult))
            sq.emit("scalar", lambda s: s.activation(mrow[0:1, :], mrow[0:1, :], AF.Sqrt))
            sq.emit("vector", lambda v: v.tensor_scalar(qh[64:65, :], mrow[0:1, :], -1.0, None, ALU.mult))
            for jt in range(TSH // 512):
                nblk = F_NBLK[jt]
                for tkb in range(nblk):
                    lh = kh[:, tkb * 128:(tkb + 1) * 128]
                    rh = qh[:, jt * 512:(jt + 1) * 512]
                    sq.dma("sync", mask_t[:, :], masks[F_MOFF[jt] + tkb])
                    sq.emit("tensor", lambda t, lh=lh, rh=rh: t.matmul(ps_sc[:, :], lh, rh, start=True, stop=True))
                    sq.emit("scalar", lambda s: s.activation(pte[:, :], ps_sc[:, :], AF.Exp))
                    sq.emit("vector", lambda v: v.tensor_tensor(pt[:, :], pte[:, :], mask_t[:, :], ALU.mult))
                    first = tkb == 0
                    last = tkb == nblk - 1
                    for ss in range(4):
                        lh2 = pt[:, ss * 128:(ss + 1) * 128]
                        rh2 = vh[:, tkb * 66:(tkb + 1) * 66]
                        d0 = ps_nd[:, ss * 128: ss * 128 + 66]
                        st_ = first and ss == 0
                        sq.emit("tensor", lambda t, lh2=lh2, rh2=rh2, d0=d0, st_=st_, last=last: t.matmul(d0, lh2, rh2, start=st_, stop=last))
                for ss in range(4):
                    den = ps_nd[:, ss * 128 + 64: ss * 128 + 65]
                    num = ps_nd[:, ss * 128: ss * 128 + 64]
                    sq.emit("vector", lambda v, den=den: v.tensor_scalar(rec[:, 0:1], den, 1e-8, None, ALU.add))
                    sq.emit("vector", lambda v: v.reciprocal(rec[:, 0:1], rec[:, 0:1]))
                    sq.emit("vector", lambda v, num=num: v.tensor_scalar(att[:, :], num, rec[:, 0:1], None, ALU.mult))
                    tok0 = jt * 512 + ss * 128
                    sq.dma("gpsimd", attn_o[tok0:tok0 + 128, h * 64:(h + 1) * 64], att[:, :])
        sq.build(nc, S)
    return nc


def build_stage_g():
    nc = bass.Bass("TRN2", target_bir_lowering=False, debug=False)
    attn = nc.dram_tensor("attn", [TSH, D], F32, kind="ExternalInput").ap()
    woT = nc.dram_tensor("woT", [D, D], F32, kind="ExternalInput").ap()
    x2 = nc.dram_tensor("x2", [TSH, D], F32, kind="ExternalInput").ap()
    ident = nc.dram_tensor("ident", [128, 128], F32, kind="ExternalInput").ap()
    x3_o = nc.dram_tensor("x3", [TSH, D], F32, kind="ExternalOutput").ap()
    xi0_o = nc.dram_tensor("xi0", [TSH, D], F32, kind="ExternalOutput").ap()
    from contextlib import ExitStack
    with ExitStack() as ctx:
        def sb(name, shape, dt=F32):
            return ctx.enter_context(nc.sbuf_tensor(name, shape, dt))
        def psum(name, shape):
            return ctx.enter_context(nc.psum_tensor(name, shape, F32))
        at = sb("atG", [128, D]); aT = sb("aTG", [128, 8 * 128], F32R)
        wo = sb("woG", [128, 8 * D], F32R)
        identt = sb("identtG", [128, 128])
        xt = sb("xtG", [128, D]); outt = sb("outtG", [128, D]); t1 = sb("t1G", [128, D])
        dump = sb("dumpG", [128, D]); col4 = sb("col4G", [128, 4]); nr = sb("nrG", [128, 3])
        cget = sb("constsG", [128, 8])
        ps = psum("psG", [128, 512]); ps_tr = psum("psG2", [128, 128])
        S = ctx.enter_context(nc.semaphore("S"))
        sq = Seq(); cc = Consts(nc, sq, cget)
        scr = {"dump": dump, "col4": col4, "nr": nr}
        sq.dma("sync", identt[:, :], ident)
        for c in range(8):
            sq.dma("sync", wo[:, c * D:(c + 1) * D], woT[c * 128:(c + 1) * 128, :].bitcast(F32R))
        for it in range(TSH // 128):
            tok0 = it * 128
            sq.dma("sync", at[:, :], attn[tok0:tok0 + 128, :])
            sq.dma("sync", xt[:, :], x2[tok0:tok0 + 128, :])
            for c in range(8):
                src_ = at[:, c * 128:(c + 1) * 128]
                dst_ = aT[:, c * 128:(c + 1) * 128]
                sq.emit("tensor", lambda t, src_=src_: t.transpose(ps_tr[:, :], src_, identt[:, :]))
                sq.emit("vector", lambda v, dst_=dst_: v.tensor_copy(dst_, ps_tr[:, :]))
            for oh in range(2):
                for c in range(8):
                    lh = aT[:, c * 128:(c + 1) * 128]
                    rh = wo[:, c * D + oh * 512: c * D + (oh + 1) * 512]
                    sq.emit("tensor", lambda t, lh=lh, rh=rh, c=c: t.matmul(ps[:, :], lh, rh, start=(c == 0), stop=(c == 7)))
                sl = slice(oh * 512, (oh + 1) * 512)
                sq.emit("vector", lambda v, sl=sl: v.tensor_copy(outt[:, sl], ps[:, :]))
            emit_ln_token_tile(sq, nc, outt[:, :], t1[:, :], scr, cc)
            sq.emit("vector", lambda v: v.tensor_tensor(t1[:, :], t1[:, :], xt[:, :], ALU.add))
            sq.dma("gpsimd", x3_o[tok0:tok0 + 128, :], t1[:, :])
            emit_ln_token_tile(sq, nc, t1[:, :], outt[:, :], scr, cc)
            sq.dma("gpsimd", xi0_o[tok0:tok0 + 128, :], outt[:, :])
        sq.build(nc, S)
    return nc


TOH = TSH + 512


def build_stage_h1():
    nc = bass.Bass("TRN2", target_bir_lowering=False, debug=False)
    xi = nc.dram_tensor("xi", [TSH, D], F32, kind="ExternalInput").ap()
    xihalo = nc.dram_tensor("xihalo", [513, D], F32, kind="ExternalInput").ap()
    wqkvR = nc.dram_tensor("wqkvR", [12, 128, 8 * 128], F32, kind="ExternalInput").ap()
    qs2 = nc.dram_tensor("qs2", [128, 1], F32, kind="ExternalInput").ap()
    ks2 = nc.dram_tensor("ks2", [128, 1], F32, kind="ExternalInput").ap()
    r64T = nc.dram_tensor("r64T", [64, 64], F32, kind="ExternalInput").ap()
    tabs = nc.dram_tensor("tabs", [4, 128, TOH], F32, kind="ExternalInput").ap()
    ident = nc.dram_tensor("ident", [128, 128], F32, kind="ExternalInput").ap()
    onesd = nc.dram_tensor("onesd", [128, 128], F32, kind="ExternalInput").ap()
    qrot_o = nc.dram_tensor("qrot", [512, TOH], F32, kind="ExternalOutput").ap()
    krot_o = nc.dram_tensor("krot", [512, TOH], F32, kind="ExternalOutput").ap()
    vtok_o = nc.dram_tensor("vtok", [TOH, 512], F32, kind="ExternalOutput").ap()
    qkn2_o = nc.dram_tensor("qkn2", [16, TOH], F32, kind="ExternalOutput").ap()
    from contextlib import ExitStack
    with ExitStack() as ctx:
        def sb(name, shape, dt=F32):
            return ctx.enter_context(nc.sbuf_tensor(name, shape, dt))
        def psum(name, shape):
            return ctx.enter_context(nc.psum_tensor(name, shape, F32))
        ht = sb("htH", [128, D])
        hT = sb("hTH", [128, 8 * 128], F32R)
        wq = sb("wqH", [128, 12 * 1024], F32R)
        identt = sb("identtH", [128, 128])
        r64 = sb("r64H", [64, 64], F32R)
        qs = sb("qsH", [128, 1]); ks = sb("ksH", [128, 1])
        tab = sb("tabH", [128, 4 * 128], F32R)
        qkt = sb("qktH", [64, 16 * 128])
        sqt = sb("sqtH", [128, 128], F32R)
        nrow = sb("nrowH", [1, 16 * 128]); rrow = sb("rrowH", [1, 16 * 128], F32R)
        nsc = sb("nscH", [1, 16 * 128])
        qhat = sb("qhatH", [128, 128], F32R)
        t1t = sb("t1tH", [128, 128]); outt = sb("outtH", [128, 128])
        vst = sb("vstH", [128, 512])
        onec = sb("onecH", [128, 1], F32R)
        onerow = sb("onerowH", [1, 64], F32R)
        cget = sb("constsH", [128, 8])
        ps = psum("psH", [128, 128])
        ps_n = psum("psHn", [1, 128])
        ps_bc = psum("psHbc", [128, 128])
        ps_rot = psum("psHrot", [128, 128])
        ps_tr = psum("psHtr", [128, 128])
        S = ctx.enter_context(nc.semaphore("S"))
        ctx.enter_context(nc.allow_low_precision(reason="f32r staging"))
        sq = Seq(); cc = Consts(nc, sq, cget)
        sq.dma("sync", identt[:, :], ident)
        sq.dma("sync", r64[:, :], r64T.bitcast(F32R))
        sq.dma("sync", qs[:, :], qs2); sq.dma("sync", ks[:, :], ks2)
        sq.dma("sync", onec[:, :], onesd[:, 0:1].bitcast(F32R), allow_slow_non_contiguous=True)
        sq.dma("sync", onerow[0:1, :], onesd[0:1, 0:64].bitcast(F32R))
        for ob in range(12):
            sq.dma("sync", wq[:, ob * 1024:(ob + 1) * 1024], wqkvR[ob].bitcast(F32R))
        NTT = TOH // 128
        for it in range(NTT):
            u0 = it * 128
            if u0 + 128 <= 512:
                sq.dma("sync", ht[:, 0:512], xihalo[u0 + 1:u0 + 129, 0:512])
                sq.dma("sync", ht[:, 512:D], xihalo[u0:u0 + 128, 512:D])
            elif u0 == 512:
                sq.dma("sync", ht[:, 0:512], xi[0:128, 0:512])
                sq.dma("sync", ht[0:1, 512:D], xihalo[512:513, 512:D])
                sq.dma("sync", ht[1:128, 512:D], xi[0:127, 512:D])
            else:
                o0 = u0 - 512
                sq.dma("sync", ht[:, 0:512], xi[o0:o0 + 128, 0:512])
                sq.dma("sync", ht[:, 512:D], xi[o0 - 1:o0 + 127, 512:D])
            for c in range(8):
                src_ = ht[:, c * 128:(c + 1) * 128]
                dst_ = hT[:, c * 128:(c + 1) * 128]
                sq.emit("tensor", lambda t, src_=src_: t.transpose(ps_tr[:, :], src_, identt[:, :]))
                sq.emit("vector", lambda v, dst_=dst_: v.tensor_copy(dst_, ps_tr[:, :]))
            for vt_ in range(4):
                ob = 8 + vt_
                for c in range(8):
                    lh = hT[:, c * 128:(c + 1) * 128]
                    rh = wq[:, ob * 1024 + c * 128:ob * 1024 + (c + 1) * 128]
                    sq.emit("tensor", lambda t, lh=lh, rh=rh, c=c: t.matmul(ps[:, :], lh, rh, start=(c == 0), stop=(c == 7)))
                sl = slice(vt_ * 128, (vt_ + 1) * 128)
                sq.emit("vector", lambda v, sl=sl: v.tensor_copy(vst[:, sl], ps[:, :]))
            sq.dma("gpsimd", vtok_o[u0:u0 + 128, :], vst[:, :])
            for hh in range(16):
                ob2 = hh // 2
                c0 = (hh % 2) * 64
                for c in range(8):
                    lh = wq[:, ob2 * 1024 + c * 128 + c0: ob2 * 1024 + c * 128 + c0 + 64]
                    rh = hT[:, c * 128:(c + 1) * 128]
                    sq.emit("tensor", lambda t, lh=lh, rh=rh, c=c: t.matmul(ps[0:64, 0:128], lh, rh, start=(c == 0), stop=(c == 7)))
                dst_ = qkt[0:64, hh * 128:(hh + 1) * 128]
                sq.emit("vector", lambda v, dst_=dst_: v.tensor_copy(dst_, ps[0:64, 0:128]))
            for hh in range(16):
                src_ = qkt[0:64, hh * 128:(hh + 1) * 128]
                sq.emit("vector", lambda v, src_=src_: v.scalar_tensor_tensor(sqt[0:64, :], src_, 1.0, src_, ALU.mult, ALU.mult))
                sq.emit("tensor", lambda t: t.matmul(ps_n[0:1, :], onec[0:64, :], sqt[0:64, :], start=True, stop=True))
                nsl = nrow[0:1, hh * 128:(hh + 1) * 128]
                sq.emit("vector", lambda v, nsl=nsl: v.tensor_copy(nsl, ps_n[0:1, :]))
            sq.emit("vector", lambda v: v.tensor_scalar(nrow[0:1, :], nrow[0:1, :], 1e-24, None, ALU.max))
            sq.emit("scalar", lambda s: s.activation(rrow[0:1, :], nrow[0:1, :], AF.Ln))
            sq.emit("scalar", lambda s: s.activation(rrow[0:1, :], rrow[0:1, :], AF.Exp, scale=-0.5))
            sq.emit("vector", lambda v: v.tensor_tensor(nsc[0:1, :], rrow[0:1, :], rrow[0:1, :], ALU.mult))
            sq.emit("vector", lambda v: v.tensor_tensor(nsc[0:1, :], nsc[0:1, :], nrow[0:1, :], ALU.mult))
            sq.emit("vector", lambda v: v.tensor_scalar(nsc[0:1, :], nsc[0:1, :], -0.5, 1.5, ALU.mult, ALU.add))
            sq.emit("vector", lambda v: v.tensor_tensor(rrow[0:1, :], rrow[0:1, :], nsc[0:1, :], ALU.mult))
            for tb in range(4):
                sq.dma("sync", tab[:, tb * 128:(tb + 1) * 128], tabs[tb, :, u0:u0 + 128].bitcast(F32R))
            for hh in range(16):
                is_q = hh < 8
                scol = qs if is_q else ks
                Asl = tab[0:64, 0:128] if is_q else tab[0:64, 256:384]
                Bsl = tab[0:64, 128:256] if is_q else tab[0:64, 384:512]
                src_ = qkt[0:64, hh * 128:(hh + 1) * 128]
                rsl = rrow[0:1, hh * 128:(hh + 1) * 128]
                sq.emit("tensor", lambda t, rsl=rsl: t.matmul(ps_bc[0:64, :], onerow[0:1, :], rsl, start=True, stop=True))
                sq.emit("vector", lambda v, src_=src_, scol=scol: v.scalar_tensor_tensor(qhat[0:64, :], src_, scol[0:64, 0:1], ps_bc[0:64, 0:128], ALU.mult, ALU.mult))
                sq.emit("vector", lambda v: v.scalar_tensor_tensor(sqt[0:64, :], qhat[0:64, :], 1.0, qhat[0:64, :], ALU.mult, ALU.mult))
                sq.emit("tensor", lambda t: t.matmul(ps_n[0:1, :], onec[0:64, :], sqt[0:64, :], start=True, stop=True))
                sq.emit("vector", lambda v: v.tensor_copy(nsc[0:1, 0:128], ps_n[0:1, :]))
                sq.dma("gpsimd", qkn2_o[hh:hh + 1, u0:u0 + 128], nsc[0:1, 0:128])
                sq.emit("tensor", lambda t: t.matmul(ps_rot[0:64, :], r64[:, :], qhat[0:64, :], start=True, stop=True))
                sq.emit("vector", lambda v, Asl=Asl: v.tensor_tensor(t1t[0:64, :], qhat[0:64, :], Asl, ALU.mult))
                sq.emit("vector", lambda v, Bsl=Bsl: v.tensor_tensor(outt[0:64, :], ps_rot[0:64, 0:128], Bsl, ALU.mult))
                sq.emit("vector", lambda v: v.tensor_tensor(outt[0:64, :], outt[0:64, :], t1t[0:64, :], ALU.add))
                if is_q:
                    sq.dma("gpsimd", qrot_o[hh * 64:(hh + 1) * 64, u0:u0 + 128], outt[0:64, :])
                else:
                    sq.dma("gpsimd", krot_o[(hh - 8) * 64:(hh - 7) * 64, u0:u0 + 128], outt[0:64, :])
        sq.build(nc, S)
    return nc


    return nc


H_NBLK = 8


def build_stage_h2():
    nc = bass.Bass("TRN2", target_bir_lowering=False, debug=False)
    qrot = nc.dram_tensor("qrot", [512, TOH], F32, kind="ExternalInput").ap()
    krot = nc.dram_tensor("krot", [512, TOH], F32, kind="ExternalInput").ap()
    vtokR = nc.dram_tensor("vtokR", [128, TOH // 128, 512], F32, kind="ExternalInput").ap()
    masksH = nc.dram_tensor("masksH", [2, H_NBLK, 128, 512], BF16, kind="ExternalInput").ap()
    wod = nc.dram_tensor("wo", [512, D], F32, kind="ExternalInput").ap()
    qkn2 = nc.dram_tensor("qkn2", [16, TOH], F32, kind="ExternalInput").ap()
    xi = nc.dram_tensor("xi", [TSH, D], F32, kind="ExternalInput").ap()
    onesd = nc.dram_tensor("onesd", [128, 128], F32, kind="ExternalInput").ap()
    onesr = nc.dram_tensor("onesr", [1, TOH], F32, kind="ExternalInput").ap()
    xiout_o = nc.dram_tensor("xiout", [TSH, D], F32, kind="ExternalOutput").ap()
    from contextlib import ExitStack
    with ExitStack() as ctx:
        def sb(name, shape, dt=F32):
            return ctx.enter_context(nc.sbuf_tensor(name, shape, dt))
        def psum(name, shape):
            return ctx.enter_context(nc.psum_tensor(name, shape, F32))
        kh = sb("khH2", [65, TOH], F32R)
        qh = sb("qhH2", [65, TSH], F32R)
        vtk = sb("vtkH2", [128, (TOH // 128) * 512], BF16)
        mask_t = sb("maskH2", [128, 2 * H_NBLK * 512], BF16)
        sqk = sb("sqkH2", [64, TOH], F32R)
        krow = sb("krowH2", [1, TOH]); qrow = sb("qrowH2", [1, TSH])
        rows = sb("rowsH2", [1, 8]); mrow = sb("mrowH2", [1, TSH])
        pte = sb("pteH2", [128, 512]); pt = sb("ptH2", [128, 512], BF16)
        ofm = sb("ofmH2", [64, 8 * TSH], F32R)
        oev = sb("oevH2", [64, 512])
        rrow = sb("rrowH2", [1, 512], F32R)
        wo = sb("woH2", [64, 8 * D], F32R)
        xt = sb("xtH2", [128, D]); outt = sb("outtH2", [128, D])
        onec = sb("onecH2", [128, 1], F32R)
        onecb = sb("onecbH2", [128, 1], BF16)
        onerow = sb("onerowH2", [1, 64], F32R)
        cget = sb("constsH2", [128, 8])
        ps_sc = psum("ps_scH2", [128, 512])
        ps_o = psum("ps_oH2", [64, 512])
        ps_d = psum("ps_dH2", [1, 512])
        ps_bc = psum("ps_bcH2", [64, 512])
        ps_w = psum("ps_wH2", [128, 512])
        S = ctx.enter_context(nc.semaphore("S"))
        ctx.enter_context(nc.allow_low_precision(reason="bf16 P tiles"))
        sq = Seq(); cc = Consts(nc, sq, cget)
        sq.dma("sync", onec[:, :], onesd[:, 0:1].bitcast(F32R), allow_slow_non_contiguous=True)
        sq.dma("gpsimd", onecb[:, :], onesd[:, 0:1], allow_slow_non_contiguous=True)
        sq.dma("sync", onerow[0:1, :], onesd[0:1, 0:64].bitcast(F32R))
        for s_ in range(2):
            for b_ in range(H_NBLK):
                dst_ = mask_t[:, (s_ * H_NBLK + b_) * 512:(s_ * H_NBLK + b_ + 1) * 512]
                sq.dma("sync", dst_, masksH[s_, b_])
        for c in range(8):
            sq.dma("sync", wo[:, c * D:(c + 1) * D], wod[c * 64:(c + 1) * 64, :].bitcast(F32R))
        for tt_ in range(TOH // 128):
            sq.dma("gpsimd", vtk[:, tt_ * 512:(tt_ + 1) * 512], vtokR[:, tt_, :])
        for h in range(8):
            p_, hb = h // 2, h % 2
            r0 = hb * 64
            sq.dma("sync", kh[0:64, :], krot[p_ * 128 + r0:p_ * 128 + r0 + 64, :].bitcast(F32R))
            sq.dma("sync", kh[64:65, :], onesr.bitcast(F32R))
            sq.dma("sync", qh[0:64, :], qrot[p_ * 128 + r0:p_ * 128 + r0 + 64, 512:].bitcast(F32R))
            sq.dma("sync", krow[0:1, :], qkn2[8 + h:9 + h, :])
            sq.dma("sync", qrow[0:1, :], qkn2[h:h + 1, 512:])
            for w in range(4):
                ksl = krow[0:1, w * 512:w * 512 + 1024]
                rsl_ = rows[0:1, w:w + 1]
                sq.emit("vector", lambda v, ksl=ksl, rsl_=rsl_: v.tensor_reduce(rsl_, ksl, mybir.AxisListType.X, ALU.max))
                qsl_ = qrow[0:1, w * 512:(w + 1) * 512]
                msl_ = mrow[0:1, w * 512:(w + 1) * 512]
                sq.emit("vector", lambda v, qsl_=qsl_, msl_=msl_, rsl_=rsl_: v.tensor_scalar(msl_, qsl_, rsl_, 64.0, ALU.mult, ALU.mult))
            sq.emit("scalar", lambda s: s.activation(mrow[0:1, :], mrow[0:1, :], AF.Sqrt))
            sq.emit("vector", lambda v: v.tensor_scalar(qh[64:65, :], mrow[0:1, :], -1.0, None, ALU.mult))
            for w in range(4):
                mset = 0 if w == 0 else 1
                for b_ in range(H_NBLK):
                    tk0 = w * 512 + b_ * 128
                    lh = kh[:, tk0:tk0 + 128]
                    rh = qh[:, w * 512:(w + 1) * 512]
                    msl = mask_t[:, (mset * H_NBLK + b_) * 512:(mset * H_NBLK + b_ + 1) * 512]
                    sq.emit("tensor", lambda t, lh=lh, rh=rh: t.matmul(ps_sc[:, :], lh, rh, start=True, stop=True))
                    sq.emit("scalar", lambda s: s.activation(pte[:, :], ps_sc[:, :], AF.Exp))
                    sq.emit("vector", lambda v, msl=msl: v.tensor_tensor(pt[:, :], pte[:, :], msl, ALU.mult))
                    first, last = b_ == 0, b_ == H_NBLK - 1
                    vsl = vtk[:, (w * 4 + b_) * 512 + h * 64:(w * 4 + b_) * 512 + (h + 1) * 64]
                    sq.emit("tensor", lambda t, vsl=vsl, first=first, last=last: t.matmul(ps_o[:, :], vsl, pt[:, :], start=first, stop=last))
                    sq.emit("tensor", lambda t, first=first, last=last: t.matmul(ps_d[0:1, :], onecb[:, :], pt[:, :], start=first, stop=last))
                sq.emit("vector", lambda v: v.tensor_copy(oev[:, :], ps_o[:, :]))
                sq.emit("vector", lambda v: v.reciprocal(rrow[0:1, :], ps_d[0:1, :]))
                sq.emit("tensor", lambda t: t.matmul(ps_bc[0:64, :], onerow[0:1, :], rrow[0:1, :], start=True, stop=True))
                dst_ = ofm[0:64, h * TSH + w * 512:h * TSH + (w + 1) * 512]
                sq.emit("vector", lambda v, dst_=dst_: v.tensor_tensor(dst_, oev[:, :], ps_bc[0:64, :], ALU.mult))
        for it in range(TSH // 128):
            tok0 = it * 128
            sq.dma("sync", xt[:, :], xi[tok0:tok0 + 128, :])
            for oh in range(2):
                for c in range(8):
                    lh = ofm[0:64, c * TSH + tok0:c * TSH + tok0 + 128]
                    rh = wo[0:64, c * D + oh * 512:c * D + (oh + 1) * 512]
                    sq.emit("tensor", lambda t, lh=lh, rh=rh, c=c: t.matmul(ps_w[:, :], lh, rh, start=(c == 0), stop=(c == 7)))
                sl = slice(oh * 512, (oh + 1) * 512)
                sq.emit("vector", lambda v, sl=sl: v.tensor_tensor(outt[:, sl], xt[:, sl], ps_w[:, :], ALU.add))
            sq.dma("gpsimd", xiout_o[tok0:tok0 + 128, :], outt[:, :])
        sq.build(nc, S)
    return nc


HID2 = 5460
HID = 2730


def build_stage_i(final):
    nc = bass.Bass("TRN2", target_bir_lowering=False, debug=False)
    xi = nc.dram_tensor("xi", [TSH, D], F32, kind="ExternalInput").ap()
    xiprev = nc.dram_tensor("xiprev", [1, D], F32, kind="ExternalInput").ap()
    w1d = nc.dram_tensor("w1", [D, HID2], F32, kind="ExternalInput").ap()
    b1d = nc.dram_tensor("b1row", [1, HID2], F32, kind="ExternalInput").ap()
    w2d = nc.dram_tensor("w2", [HID, D], F32, kind="ExternalInput").ap()
    b2d = nc.dram_tensor("b2row", [1, D], F32, kind="ExternalInput").ap()
    lngd = nc.dram_tensor("lng", [1, D], F32, kind="ExternalInput").ap()
    lnbd = nc.dram_tensor("lnb", [1, D], F32, kind="ExternalInput").ap()
    ident = nc.dram_tensor("ident", [128, 128], F32, kind="ExternalInput").ap()
    if final:
        x3d = nc.dram_tensor("x3", [TSH, D], F32, kind="ExternalInput").ap()
    xiout_o = nc.dram_tensor("xiout", [TSH, D], F32, kind="ExternalOutput").ap()
    TS = 512
    NW1 = ceil_div(HID2, 512)
    NC2 = ceil_div(HID, 128)
    from contextlib import ExitStack
    with ExitStack() as ctx:
        def sb(name, shape, dt=F32):
            return ctx.enter_context(nc.sbuf_tensor(name, shape, dt))
        def psum(name, shape):
            return ctx.enter_context(nc.psum_tensor(name, shape, F32))
        ht = sb("htI", [128, D]); xnt = sb("xntI", [128, D])
        dump = sb("dumpI", [128, D]); col4 = sb("col4I", [128, 4]); nr = sb("nrI", [128, 3])
        identt = sb("identtI", [128, 128])
        gbc = sb("gbcI", [128, D]); bbc = sb("bbcI", [128, D])
        rowd = sb("rowdI", [1, D], F32R)
        mrow1 = sb("mrow1I", [1, 128])
        xT = sb("xTI", [128, 8 * TS], F32R)
        ggT = sb("ggTI", [128, 4 * NC2 * 128], F32R)
        w1b = sb("w1bI", [128, 8 * 512], F32R)
        b1b = sb("b1bI", [1, 512], F32R)
        w2b = sb("w2bI", [128, 512], F32R)
        gel = sb("gelI", [128, 4 * 512])
        onerow = sb("onerowI", [1, 128], F32R)
        osl = sb("oslI", [128, 4 * D])
        cget = sb("constsI", [128, 8])
        ps = psum("psI", [128, 512]); ps_tr = psum("psI2", [128, 128])
        ps4 = psum("ps4I", [128, 4 * 512])
        S = ctx.enter_context(nc.semaphore("S"))
        ctx.enter_context(nc.allow_low_precision(reason="f32r staging"))
        sq = Seq(); cc = Consts(nc, sq, cget)
        scr = {"dump": dump, "col4": col4, "nr": nr}
        sq.dma("sync", identt[:, :], ident)
        sq.emit("vector", lambda v: v.memset(mrow1[0:1, :], 1.0))
        sq.emit("vector", lambda v: v.tensor_copy(onerow[0:1, :], mrow1[0:1, :]))
        sq.dma("sync", rowd[0:1, :], lngd.bitcast(F32R))
        for j in range(2):
            sq.emit("tensor", lambda t, j=j: t.matmul(ps[:, :], onerow[0:1, :].bitcast(F32R), rowd[0:1, j * 512:(j + 1) * 512], start=True, stop=True))
            sq.emit("vector", lambda v, j=j: v.tensor_copy(gbc[:, j * 512:(j + 1) * 512], ps[:, :]))
        sq.dma("sync", rowd[0:1, :], lnbd.bitcast(F32R))
        for j in range(2):
            sq.emit("tensor", lambda t, j=j: t.matmul(ps[:, :], onerow[0:1, :].bitcast(F32R), rowd[0:1, j * 512:(j + 1) * 512], start=True, stop=True))
            sq.emit("vector", lambda v, j=j: v.tensor_copy(bbc[:, j * 512:(j + 1) * 512], ps[:, :]))
        for isl in range(TSH // TS):
            st0 = isl * TS
            for jt in range(TS // 128):
                tok0 = st0 + jt * 128
                sq.dma("sync", ht[:, 0:512], xi[tok0:tok0 + 128, 0:512])
                if tok0 == 0:
                    sq.dma("sync", ht[0:1, 512:D], xiprev[0:1, 512:D])
                    sq.dma("sync", ht[1:128, 512:D], xi[0:127, 512:D])
                else:
                    sq.dma("sync", ht[:, 512:D], xi[tok0 - 1:tok0 + 127, 512:D])
                emit_ln_token_tile(sq, nc, ht[:, :], xnt[:, :], scr, cc)
                sq.emit("vector", lambda v: v.tensor_tensor(xnt[:, :], xnt[:, :], gbc[:, :], ALU.mult))
                sq.emit("vector", lambda v: v.tensor_tensor(xnt[:, :], xnt[:, :], bbc[:, :], ALU.add))
                for c in range(8):
                    src_ = xnt[:, c * 128:(c + 1) * 128]
                    dst_ = xT[:, c * TS + jt * 128: c * TS + (jt + 1) * 128]
                    sq.emit("tensor", lambda t, src_=src_: t.transpose(ps_tr[:, :], src_, identt[:, :]))
                    sq.emit("vector", lambda v, dst_=dst_: v.tensor_copy(dst_, ps_tr[:, :]))
            NOSP = ceil_div(HID, 512)
            for osp in range(NOSP):
                o0 = osp * 512
                ow = min(512, HID - o0)
                for c in range(8):
                    sq.dma("sync", w1b[:, c * 512:c * 512 + ow], w1d[c * 128:(c + 1) * 128, HID + o0:HID + o0 + ow].bitcast(F32R))
                sq.dma("sync", b1b[0:1, 0:ow], b1d[0:1, HID + o0:HID + o0 + ow].bitcast(F32R))
                for jt in range(TS // 128):
                    for c in range(8):
                        lh = xT[:, c * TS + jt * 128: c * TS + (jt + 1) * 128]
                        rh = w1b[:, c * 512:c * 512 + ow]
                        sq.emit("tensor", lambda t, lh=lh, rh=rh, c=c, ow=ow: t.matmul(ps[:, 0:ow], lh, rh, start=(c == 0), stop=False))
                    sq.emit("tensor", lambda t, ow=ow: t.matmul(ps[:, 0:ow], onerow[0:1, :], b1b[0:1, 0:ow], start=False, stop=True))
                    dst_ = gel[:, jt * 512: jt * 512 + ow]
                    sq.emit("scalar", lambda s, dst_=dst_, ow=ow: s.activation(dst_, ps[:, 0:ow], AF.Gelu))
                for c in range(8):
                    sq.dma("sync", w1b[:, c * 512:c * 512 + ow], w1d[c * 128:(c + 1) * 128, o0:o0 + ow].bitcast(F32R))
                sq.dma("sync", b1b[0:1, 0:ow], b1d[0:1, o0:o0 + ow].bitcast(F32R))
                for jt in range(TS // 128):
                    for c in range(8):
                        lh = xT[:, c * TS + jt * 128: c * TS + (jt + 1) * 128]
                        rh = w1b[:, c * 512:c * 512 + ow]
                        sq.emit("tensor", lambda t, lh=lh, rh=rh, c=c, ow=ow: t.matmul(ps[:, 0:ow], lh, rh, start=(c == 0), stop=False))
                    sq.emit("tensor", lambda t, ow=ow: t.matmul(ps[:, 0:ow], onerow[0:1, :], b1b[0:1, 0:ow], start=False, stop=True))
                    gsl = gel[:, jt * 512: jt * 512 + ow]
                    sq.emit("vector", lambda v, gsl=gsl, ow=ow: v.tensor_tensor(gsl, gsl, ps[:, 0:ow], ALU.mult))
                    for cc2 in range(ceil_div(ow, 128)):
                        i0g = o0 + cc2 * 128
                        iw = min(128, HID - i0g)
                        src_ = gel[:, jt * 512 + cc2 * 128: jt * 512 + cc2 * 128 + iw]
                        cch = i0g // 128
                        dst_ = ggT[0:iw, (jt * NC2 + cch) * 128:(jt * NC2 + cch) * 128 + 128]
                        sq.emit("tensor", lambda t, src_=src_, iw=iw: t.transpose(ps_tr[0:iw, :], src_, identt[:, :]))
                        sq.emit("vector", lambda v, dst_=dst_, iw=iw: v.tensor_copy(dst_, ps_tr[0:iw, :]))
            for oh in range(2):
                o0 = oh * 512
                for c in range(NC2):
                    i0 = c * 128
                    iw = min(128, HID - i0)
                    sq.dma("sync", w2b[0:iw, :], w2d[i0:i0 + iw, o0:o0 + 512].bitcast(F32R))
                    for jt in range(TS // 128):
                        lh = ggT[0:iw, (jt * NC2 + c) * 128:(jt * NC2 + c) * 128 + 128]
                        dst_ = ps4[:, jt * 512:(jt + 1) * 512]
                        first = c == 0
                        sq.emit("tensor", lambda t, lh=lh, iw=iw, first=first, dst_=dst_: t.matmul(dst_, lh, w2b[0:iw, :], start=first, stop=False))
                sq.dma("sync", b1b[0:1, 0:512], b2d[0:1, o0:o0 + 512].bitcast(F32R))
                for jt in range(TS // 128):
                    dst_ = ps4[:, jt * 512:(jt + 1) * 512]
                    sq.emit("tensor", lambda t, dst_=dst_: t.matmul(dst_, onerow[0:1, :], b1b[0:1, 0:512], start=False, stop=True))
                    osl_ = osl[:, jt * D + o0: jt * D + o0 + 512]
                    sq.emit("vector", lambda v, dst_=dst_, osl_=osl_: v.tensor_copy(osl_, dst_))
            for jt in range(TS // 128):
                tok0 = st0 + jt * 128
                sq.dma("sync", ht[:, :], xi[tok0:tok0 + 128, :])
                src_ = osl[:, jt * D:(jt + 1) * D]
                sq.emit("vector", lambda v, src_=src_: v.tensor_tensor(xnt[:, :], ht[:, :], src_, ALU.add))
                if final:
                    sq.dma("sync", ht[:, :], x3d[tok0:tok0 + 128, :])
                    src2_ = osl[:, jt * D:(jt + 1) * D]
                    sq.emit("vector", lambda v, src2_=src2_: v.tensor_copy(src2_, xnt[:, :]))
                    emit_ln_token_tile(sq, nc, src2_, xnt[:, :], scr, cc)
                    sq.emit("vector", lambda v: v.tensor_tensor(xnt[:, :], xnt[:, :], ht[:, :], ALU.add))
                sq.dma("gpsimd", xiout_o[tok0:tok0 + 128, :], xnt[:, :])
        sq.build(nc, S)
    return nc


_IDENT = np.eye(128, dtype=np.float32)
_IDENT64 = np.eye(64, dtype=np.float32)
_ONESD = np.ones((128, 128), dtype=np.float32)


def host_rotary_tables():
    n = T
    DH, WIN = 64, 512
    half = np.arange(0, DH, 2, dtype=np.float64)
    inv_freq = 1.0 / 10000.0 ** (half / DH)
    t = np.arange(n, dtype=np.float64)
    freqs = t[:, None] * inv_freq[None, :]
    freqs = np.concatenate([freqs, freqs], axis=-1)
    base = (half + 0.4 * DH) / (1.4 * DH)
    power = (t - n // 2) / float(WIN // 2)
    scale = base[None, :] ** power[:, None]
    scale = np.concatenate([scale, scale], axis=-1)
    cos, sin = np.cos(freqs), np.sin(freqs)
    Aq = (cos * scale * 8.0).T
    Bq = (sin * scale * 8.0).T
    Ak = (cos / scale).T
    Bk = (sin / scale).T
    out = np.zeros((4, 128, n + 512), dtype=np.float32)
    for i, tab in enumerate((Aq, Bq, Ak, Bk)):
        dup = np.concatenate([tab, tab], axis=0)
        out[i, :, 512:] = dup
        out[i, :, :512] = dup[:, 0:1]
    return out


def host_r64T():
    R = np.zeros((64, 64), np.float32)
    for d in range(32):
        R[d, d + 32] = -1.0
        R[d + 32, d] = 1.0
    return np.ascontiguousarray(R.T)


def host_masks_f(half):
    m = np.zeros((F_MTOT, 128, 512), np.float32)
    i = np.arange(128)[:, None]
    j = np.arange(512)[None, :]
    for jt in range(4):
        tqbase = half * TSH + jt * 512
        for tkb in range(F_NBLK[jt]):
            m[F_MOFF[jt] + tkb] = ((tqbase + j) >= (tkb * 128 + i)).astype(np.float32)
    return m


def host_masks_h(half):
    import ml_dtypes
    m = np.zeros((2, H_NBLK, 128, 512), np.float32)
    i = np.arange(128)[:, None]
    j = np.arange(512)[None, :]
    for b_ in range(H_NBLK):
        diff = j - (b_ * 128 + i - 512)
        base = ((diff >= 0) & (diff <= 512)).astype(np.float32)
        m[1, b_] = base
        if half == 0 and b_ < 4:
            m[0, b_] = 0.0
        else:
            m[0, b_] = base
    return m.astype(ml_dtypes.bfloat16)


def run_stage_a(inputs):
    x = inputs["L0_attn_x"]
    nw = np.ascontiguousarray(inputs["L0_attn_norm_weight"].reshape(1, D))
    wkT = np.ascontiguousarray(inputs["L0_attn_Wk"].T)
    wqT = np.ascontiguousarray(inputs["L0_attn_Wq"].T)
    wvT = np.ascontiguousarray(inputs["L0_attn_Wv"].T)
    eta = np.ascontiguousarray(inputs["L0_attn_eta"].reshape(64, 1))
    nc = build_stage_a()
    in_maps = []
    for c in range(NCORES):
        b, h = c // 2, c % 2
        xs = np.ascontiguousarray(x[b, h * TSH:(h + 1) * TSH, :])
        in_maps.append({"x": xs, "nw": nw, "wkT": wkT, "wqT": wqT, "wvT": wvT,
                        "eta": eta, "ident": _IDENT})
    res = run_bass_kernel_spmd(nc, in_maps, core_ids=list(range(NCORES)))
    return res.results


def _blkR(wt, nc_):
    i_, o_ = wt.shape
    return np.ascontiguousarray(
        wt.reshape(nc_, 128, o_ // 128, 128).transpose(2, 1, 0, 3).reshape(o_ // 128, 128, nc_ * 128))


def _cc(a):
    return np.ascontiguousarray(a, dtype=np.float32)


def kernel(**inputs):
    ncpu = np
    cores = list(range(NCORES))
    x = inputs["L0_attn_x"]

    nwr = _cc(inputs["L0_attn_norm_weight"].reshape(1, D))
    wkT = _cc(inputs["L0_attn_Wk"].T); wqT = _cc(inputs["L0_attn_Wq"].T)
    wvT = _cc(inputs["L0_attn_Wv"].T)
    eta = _cc(inputs["L0_attn_eta"].reshape(64, 1))
    ncA = build_stage_a()
    imA = []
    for c in cores:
        b, h = c // 2, c % 2
        imA.append({"x": _cc(x[b, h * TSH:(h + 1) * TSH]), "nw": nwr, "wkT": wkT,
                    "wqT": wqT, "wvT": wvT, "eta": eta, "ident": _IDENT})
    rA = run_bass_kernel_spmd(ncA, imA, core_ids=cores).results

    SSTEPS = 1024
    kek2 = {}; vfmb = {}; vekb = {}
    for b in range(B):
        kneg = np.concatenate([rA[2 * b]["knegfm"], rA[2 * b + 1]["knegfm"]], axis=1)
        ekf = np.concatenate([rA[2 * b]["ekfm"], rA[2 * b + 1]["ekfm"]], axis=1)
        inter = np.empty((64, 2 * T), np.float32)
        inter[:, 0::2] = kneg
        inter[:, 1::2] = ekf
        kek2[b] = _cc(np.concatenate([inter, inter], axis=0))
        vfm = np.concatenate([rA[2 * b]["vfm"], rA[2 * b + 1]["vfm"]], axis=1)
        vfmb[b] = _cc(vfm.reshape(8, 128, T).transpose(1, 0, 2))
        vtok = np.concatenate([rA[2 * b]["vtok"], rA[2 * b + 1]["vtok"]], axis=0)
        vekb[b] = _cc(np.concatenate([vtok, ekf.T], axis=1))
    U0 = np.zeros((64, D), np.float32)
    SC0 = np.zeros((128, 8), np.float32)
    SC0[:, 0] = 1.0
    SC0[:, 2] = 1.0
    SC0[:, 7] = 2.0 / (64 * D)
    Ust = {b: U0 for b in range(B)}
    SCst = {b: SC0 for b in range(B)}
    Mst = {}
    ncS = build_stage_scan2(SSTEPS)
    for j in range(T // SSTEPS):
        t0 = j * SSTEPS
        imS = []
        for c in cores:
            b = c % B
            imS.append({"kek": _cc(kek2[b][:, 2 * t0:2 * t0 + 2 * SSTEPS]),
                        "vfm": _cc(vfmb[b][:, :, t0:t0 + SSTEPS]),
                        "vek": _cc(vekb[b][t0:t0 + SSTEPS]),
                        "Uin": Ust[b], "SCin": SCst[b], "onesd": _ONESD})
        rS = run_bass_kernel_spmd(ncS, imS, core_ids=cores).results
        for b in range(B):
            Ust[b] = rS[b]["Uout"]
            SCst[b] = rS[b]["SCout"]
            Mst[b] = rS[b]["Mout"]

    woutT = _cc(inputs["L0_attn_Wout"].T)
    ncC = build_stage_c()
    imC = []
    for c in cores:
        b, h = c // 2, c % 2
        imC.append({"qfm": rA[c]["qfm"], "M": Mst[b], "woutT": woutT,
                    "xn": rA[c]["xn"], "x": imA[c]["x"]})
    rC = run_bass_kernel_spmd(ncC, imC, core_ids=cores).results

    imD = []
    w1R = _blkR(inputs["L0_ffn_w1_w"].T.astype(np.float32), 8)
    w2R = _blkR(inputs["L0_ffn_w2_w"].T.astype(np.float32), 8)
    h1R = _blkR(inputs["L0_ffn_h1_w"].T.astype(np.float32), 32)
    h2R = _blkR(inputs["L0_ffn_h2_w"].T.astype(np.float32), 32)
    w3T = _cc(inputs["L0_ffn_w3_w"].T)
    ncD = build_stage_d()
    for c in cores:
        imD.append({"x1": rC[c]["x1"], "w1R": w1R, "w2R": w2R, "h1R": h1R, "h2R": h2R,
                    "w3T": w3T, "s1": _cc(inputs["L0_ffn_w1_s"].reshape(-1, 1)),
                    "s2": _cc(inputs["L0_ffn_w2_s"].reshape(-1, 1)),
                    "sh1": _cc(inputs["L0_ffn_h1_s"].reshape(-1, 1)),
                    "sh2": _cc(inputs["L0_ffn_h2_s"].reshape(-1, 1)),
                    "s3row": _cc(inputs["L0_ffn_w3_s"].reshape(1, -1)), "ident": _IDENT})
    rD = run_bass_kernel_spmd(ncD, imD, core_ids=cores).results

    ncE = build_stage_e()
    imE = []
    for c in cores:
        imE.append({"x2": rD[c]["x2"], "wqT": _cc(inputs["L1_attn_wq"].T),
                    "wkT": _cc(inputs["L1_attn_wk"].T), "wvT": _cc(inputs["L1_attn_wv"].T),
                    "ident": _IDENT})
    rE = run_bass_kernel_spmd(ncE, imE, core_ids=cores).results

    ONESR_T = np.ones((1, T), np.float32)
    kfmb = {}; vtokRb = {}
    for b in range(B):
        kfmb[b] = _cc(np.concatenate([rE[2 * b]["kfm"], rE[2 * b + 1]["kfm"]], axis=1))
        vv = np.concatenate([rE[2 * b]["vtok"], rE[2 * b + 1]["vtok"]], axis=0)
        vtokRb[b] = _cc(vv.reshape(T // 128, 128, D).transpose(1, 0, 2))
    ncF = build_stage_f()
    imF = []
    masksF = [host_masks_f(0), host_masks_f(1)]
    for c in cores:
        b, h = c // 2, c % 2
        imF.append({"qfm": rE[c]["qfm"], "kfm": kfmb[b], "vtokR": vtokRb[b],
                    "masks": masksF[h], "onesd": _ONESD, "onesr": ONESR_T})
    rF = run_bass_kernel_spmd(ncF, imF, core_ids=cores).results

    ncG = build_stage_g()
    imG = []
    for c in cores:
        imG.append({"attn": rF[c]["attn"], "woT": _cc(inputs["L1_attn_wo"].T),
                    "x2": rD[c]["x2"], "ident": _IDENT})
    rG = run_bass_kernel_spmd(ncG, imG, core_ids=cores).results

    TABS = host_rotary_tables()
    R64T = host_r64T()
    ONESR_TOH = np.ones((1, TOH), np.float32)
    xi = {c: rG[c]["xi0"] for c in cores}
    x3 = {c: rG[c]["x3"] for c in cores}
    for l in range(2):
        wqkvR = _blkR(_cc(inputs["L1_ffn_wqkv"][l]), 8)
        qs2 = _cc(np.tile(inputs["L1_ffn_q_scale"][l], 2).reshape(128, 1))
        ks2 = _cc(np.tile(inputs["L1_ffn_k_scale"][l], 2).reshape(128, 1))
        wol = _cc(inputs["L1_ffn_wo"][l])
        lngl = _cc(inputs["L1_ffn_ln_g"][l].reshape(1, -1))
        lnbl = _cc(inputs["L1_ffn_ln_b"][l].reshape(1, -1))
        w1l = _cc(inputs["L1_ffn_w1"][l]); b1l = _cc(inputs["L1_ffn_b1"][l].reshape(1, -1))
        w2l = _cc(inputs["L1_ffn_w2"][l]); b2l = _cc(inputs["L1_ffn_b2"][l].reshape(1, -1))
        ncH1 = build_stage_h1()
        im1 = []
        for c in cores:
            b, h = c // 2, c % 2
            if h == 0:
                halo = np.zeros((513, D), np.float32)
            else:
                halo = _cc(xi[c - 1][TSH - 513:TSH])
            tabsl = _cc(TABS[:, :, h * TSH:h * TSH + TOH])
            im1.append({"xi": xi[c], "xihalo": halo, "wqkvR": wqkvR, "qs2": qs2,
                        "ks2": ks2, "r64T": R64T, "tabs": tabsl, "ident": _IDENT,
                        "onesd": _ONESD})
        r1 = run_bass_kernel_spmd(ncH1, im1, core_ids=cores).results
        ncH2 = build_stage_h2()
        im2 = []
        for c in cores:
            b, h = c // 2, c % 2
            vtokR = _cc(r1[c]["vtok"].reshape(TOH // 128, 128, 512).transpose(1, 0, 2))
            im2.append({"qrot": r1[c]["qrot"], "krot": r1[c]["krot"], "vtokR": vtokR,
                        "qkn2": r1[c]["qkn2"], "masksH": host_masks_h(h), "wo": wol,
                        "xi": xi[c], "onesd": _ONESD, "onesr": ONESR_TOH})
        r2 = run_bass_kernel_spmd(ncH2, im2, core_ids=cores).results
        ncI = build_stage_i(l == 1)
        im3 = []
        for c in cores:
            b, h = c // 2, c % 2
            if h == 0:
                prev = np.zeros((1, D), np.float32)
            else:
                prev = _cc(r2[c - 1]["xiout"][-1:])
            m = {"xi": r2[c]["xiout"], "xiprev": prev, "w1": w1l, "b1row": b1l,
                 "w2": w2l, "b2row": b2l, "lng": lngl, "lnb": lnbl, "ident": _IDENT}
            if l == 1:
                m["x3"] = x3[c]
            im3.append(m)
        r3 = run_bass_kernel_spmd(ncI, im3, core_ids=cores).results
        xi = {c: r3[c]["xiout"] for c in cores}

    out = np.empty((B, T, D), np.float32)
    for c in cores:
        b, h = c // 2, c % 2
        out[b, h * TSH:(h + 1) * TSH] = xi[c]
    return out
